# version 03
# baseline (speedup 1.0000x reference)
import numpy as np
import ml_dtypes
from contextlib import ExitStack

import concourse.bass as bass
import concourse.mybir as mybir
from concourse.bass_utils import run_bass_kernel_spmd

F32 = mybir.dt.float32
BF16 = mybir.dt.bfloat16
AF = mybir.ActivationFunctionType
ALU = mybir.AluOpType
AX = mybir.AxisListType.X

P = 128
D = 1024
S = 4096
TOWN = 2048
BF16NP = ml_dtypes.bfloat16



def bitonic_substages(n):
    k = 2
    while k <= n:
        j = k // 2
        while j >= 1:
            yield k, j
            j //= 2
        k *= 2


def emit_bitonic(v, b0, b1, n):
    bufs = [b0, b1]
    cur = 0
    for k, j in bitonic_substages(n):
        src, dst = bufs[cur], bufs[1 - cur]
        a_cnt = 2 if k < n else 1
        s_cnt = n // (2 * k) if k < n else 1
        r_cnt = k // (2 * j)
        v6s = src[:, 0:n].rearrange("p (s a r b j) -> p s a r b j",
                                    s=s_cnt, a=a_cnt, r=r_cnt, b=2, j=j)
        v6d = dst[:, 0:n].rearrange("p (s a r b j) -> p s a r b j",
                                    s=s_cnt, a=a_cnt, r=r_cnt, b=2, j=j)
        for parity in range(a_cnt):
            direction = +1 if parity == 0 else -1
            A, B = v6s[:, :, parity, :, 0, :], v6s[:, :, parity, :, 1, :]
            Ao, Bo = v6d[:, :, parity, :, 0, :], v6d[:, :, parity, :, 1, :]
            opa = ALU.max if direction > 0 else ALU.min
            opb = ALU.min if direction > 0 else ALU.max
            v.tensor_tensor(out=Ao, in0=A, in1=B, op=opa)
            v.tensor_tensor(out=Bo, in0=A, in1=B, op=opb)
        cur = 1 - cur
    assert cur == 0



def _mk_rsync(s_rt, rt_cnt):
    def rsync(eng, inst):
        inst.then_inc(s_rt, 1)
        rt_cnt[0] += 1
        eng.wait_ge(s_rt, rt_cnt[0])
    return rsync



def build_l0attn(debug=False):
    nc = bass.Bass("TRN2", target_bir_lowering=False)
    xf = nc.dram_tensor("xf", [S, D], F32, kind="ExternalInput").ap()
    xo = nc.dram_tensor("xo", [TOWN, D], F32, kind="ExternalInput").ap()
    wqt = nc.dram_tensor("wqt", [D, D], F32, kind="ExternalInput").ap()
    wkt = nc.dram_tensor("wkt", [D, D], F32, kind="ExternalInput").ap()
    wvt = nc.dram_tensor("wvt", [D, D], F32, kind="ExternalInput").ap()
    idn_d = nc.dram_tensor("idn", [P, P], BF16, kind="ExternalInput").ap()
    x1o = nc.dram_tensor("x1o", [TOWN, D], F32, kind="ExternalOutput").ap()
    vscr = nc.dram_tensor("vscr", [S, D], BF16, kind="Internal").ap()
    dbg = {}
    if debug:
        for nm, w, dt in (("d_wk", 8 * D, BF16), ("d_abf", 2 * D, BF16),
                          ("d_aT", 8 * 512 * 2, BF16), ("d_kT", 8 * S, BF16),
                          ("d_qT", 8 * TOWN, BF16), ("d_scores", S, F32),
                          ("d_bs0", S, F32), ("d_bs1", S, F32),
                          ("d_abf2", S, BF16), ("d_cols", 32, F32),
                          ("d_outb", 2 * D, F32), ("d_aTg", 2 * S, BF16)):
            dbg[nm] = nc.dram_tensor(nm, [P, w], dt, kind="ExternalOutput").ap()

    es = ExitStack()
    sbt = lambda nm, shp, dt=F32: es.enter_context(nc.sbuf_tensor(nm, shp, dt))
    sem = lambda nm: es.enter_context(nc.semaphore(nm))

    kT = sbt("kT", [P, 8 * S], BF16)
    qT = sbt("qT", [P, 8 * TOWN], BF16)
    cols = sbt("cols", [P, 32])
    idn = sbt("idnb", [P, P], BF16)
    co = lambda i: cols[:, i:i + 1]
    (C_S1, C_S2, C_MEAN, C_VAR, C_STD, C_RSTD, C_NMAX, C_SSUM, C_SUMZ, C_STEP,
     C_KC, C_RK, C_NRK, C_ALPHA, C_CT, C_OS1, C_OS2, C_OMEAN, C_OVAR, C_OSTD,
     C_ORSTD, C_MR, C_STEPM, C_ALPHA2) = range(24)

    esA = ExitStack()
    sbA = lambda nm, shp, dt=F32: esA.enter_context(nc.sbuf_tensor(nm, shp, dt))
    psA = lambda nm, shp, dt=F32: esA.enter_context(nc.psum_tensor(nm, shp, dt))
    wq_b = sbA("wq_b", [P, 8 * D], BF16)
    wk_b = sbA("wk_b", [P, 8 * D], BF16)
    wv_b = sbA("wv_b", [P, 8 * D], BF16)
    xbuf = sbA("xbuf", [P, 2 * D])
    sqb = sbA("sqb", [P, D])
    a_bf = sbA("a_bf", [P, 2 * D], BF16)
    aT = sbA("aT", [P, 2 * 8 * 512], BF16)
    vbuf = sbA("vbuf", [P, 2 * D], BF16)
    ptp = psA("ptp", [P, 2 * D], BF16)
    pk = psA("pk", [P, 2 * 512])
    pv = psA("pv", [P, D])

    s_wld = sem("s_wld")
    s_wk = sem("s_wk")
    s_wv = sem("s_wv")
    s_wq = sem("s_wq")
    s_xld = [sem("s_xld0"), sem("s_xld1")]
    s_var = sem("s_var")
    s_std = sem("s_std")
    s_rt = sem("s_rt")
    s_app = sem("s_app")
    s_tp = sem("s_tp")
    s_tpc = sem("s_tpc")
    s_mm = sem("s_mm")
    s_kev = sem("s_kev")
    s_vev = sem("s_vev")
    s_vst = [sem("s_vst0"), sem("s_vst1")]

    NG = 48
    rt_cnt = [0]
    rsync = _mk_rsync(s_rt, rt_cnt)

    def src_rows(g):
        return xf[g * P:(g + 1) * P, :] if g < 32 else xo[(g - 32) * P:(g - 31) * P, :]

    def aT_view(ch):
        return aT[:, (ch % 2) * S:(ch % 2) * S + S].rearrange("p (dc t) -> p dc t", dc=8)

    def mm_base(ch):
        return ch * 12 if ch <= 8 else 96 + (ch - 8) * 8


    with nc.Block() as block:
        @block.gpsimd
        def _(g):
            g.dma_start(out=idn[:], in_=idn_d).then_inc(s_wld, 16)
            for wsrc, wdst, ws in ((wkt, wk_b, s_wk), (wvt, wv_b, s_wv),
                                   (wqt, wq_b, s_wq)):
                g.dma_start(out=wdst[:, :].rearrange("p (dc d) -> p dc d", dc=8),
                            in_=wsrc.rearrange("(dc p) d -> p dc d", p=P)
                            ).then_inc(ws, 16)

        @block.sync
        def _(sy):
            for g in range(NG):
                if g >= 2:
                    sy.wait_ge(s_app, g - 1)
                sy.dma_start(out=xbuf[:, (g % 2) * D:(g % 2 + 1) * D],
                             in_=src_rows(g)).then_inc(s_xld[g % 2], 16)
                if g % 4 == 3 and g // 4 < 8:
                    ch = g // 4
                    for tt in range(4):
                        vi = ch * 4 + tt
                        sy.wait_ge(s_vev, vi + 1)
                        r0 = ch * 512 + tt * P
                        sy.dma_start(out=vscr[r0:r0 + P, :],
                                     in_=vbuf[:, (vi % 2) * D:(vi % 2 + 1) * D]
                                     ).then_inc(s_vst[vi % 2], 16)

        @block.vector
        def _(v):
            for g in range(NG):
                b = (g % 2) * D
                xt = xbuf[:, b:b + D]
                if g >= 1:
                    gp = g - 1
                    v.wait_ge(s_tp, gp + 1)
                    ch, tt = gp // 4, gp % 4
                    v.tensor_copy(
                        out=aT_view(ch)[:, :, tt * P:(tt + 1) * P],
                        in_=ptp[:, (gp % 2) * D:(gp % 2 + 1) * D].rearrange(
                            "p (dc t) -> p dc t", dc=8)).then_inc(s_tpc, 1)
                v.wait_ge(s_xld[g % 2], 16 * (g // 2 + 1))
                v.tensor_mul(out=sqb[:], in0=xt, in1=xt)
                v.reduce_sum(out=co(C_S1), in_=xt, axis=AX)
                v.reduce_sum(out=co(C_S2), in_=sqb[:], axis=AX)
                rsync(v, v.tensor_scalar(out=co(C_MEAN), in0=co(C_S1),
                                         scalar1=1.0 / D, scalar2=None,
                                         op0=ALU.mult))
                rsync(v, v.tensor_mul(out=co(C_VAR), in0=co(C_MEAN),
                                      in1=co(C_MEAN)))
                rsync(v, v.scalar_tensor_tensor(out=co(C_VAR), in0=co(C_S2),
                                                scalar=1.0 / D, in1=co(C_VAR),
                                                op0=ALU.mult, op1=ALU.subtract))
                v.tensor_scalar(out=co(C_VAR), in0=co(C_VAR), scalar1=1e-5,
                                scalar2=None, op0=ALU.add).then_inc(s_var, 1)
                v.wait_ge(s_std, g + 1)
                rsync(v, v.reciprocal(out=co(C_RSTD), in_=co(C_STD)))
                v.tensor_scalar(out=a_bf[:, b:b + D], in0=xt,
                                scalar1=co(C_MEAN), scalar2=co(C_RSTD),
                                op0=ALU.subtract, op1=ALU.mult).then_inc(s_app, 1)
            gp = NG - 1
            v.wait_ge(s_tp, gp + 1)
            ch, tt = gp // 4, gp % 4
            v.tensor_copy(out=aT_view(ch)[:, :, tt * P:(tt + 1) * P],
                          in_=ptp[:, (gp % 2) * D:(gp % 2 + 1) * D].rearrange(
                              "p (dc t) -> p dc t", dc=8)).then_inc(s_tpc, 1)

        def _kq_evac(s, ch):
            for mh in range(8):
                s.wait_ge(s_mm, mm_base(ch) + mh + 1)
                src = pk[:, (mh % 2) * 512:(mh % 2) * 512 + 512]
                if ch < 8:
                    dst = kT[:, mh * S + ch * 512: mh * S + ch * 512 + 512]
                else:
                    chq = ch - 8
                    dst = qT[:, mh * TOWN + chq * 512: mh * TOWN + chq * 512 + 512]
                s.activation(out=dst, in_=src, func=AF.Copy).then_inc(s_kev, 1)

        @block.scalar
        def _(s):
            for g in range(NG):
                s.wait_ge(s_var, g + 1)
                s.activation(out=co(C_STD), in_=co(C_VAR),
                             func=AF.Sqrt).then_inc(s_std, 1)
                if g % 4 == 3:
                    ch = g // 4
                    _kq_evac(s, ch)
                    if ch < 8:
                        for tt in range(4):
                            s.wait_ge(s_mm, mm_base(ch) + 8 + tt + 1)
                            vi = ch * 4 + tt
                            if vi >= 2:
                                s.wait_ge(s_vst[vi % 2], 16 * ((vi - 2) // 2 + 1))
                            s.activation(out=vbuf[:, (vi % 2) * D:(vi % 2 + 1) * D],
                                         in_=pv[:], func=AF.Copy).then_inc(s_vev, 1)

        @block.tensor
        def _(t):
            t.wait_ge(s_wld, 16)
            for g in range(NG):
                t.wait_ge(s_app, g + 1)
                if g >= 2:
                    t.wait_ge(s_tpc, g - 1)
                b = (g % 2) * D
                pb = (g % 2) * D
                for dc in range(8):
                    ins = t.transpose(out=ptp[:, pb + dc * P: pb + (dc + 1) * P],
                                      in_=a_bf[:, b + dc * P: b + (dc + 1) * P],
                                      identity=idn[:])
                ins.then_inc(s_tp, 1)
                if g % 4 == 3:
                    ch = g // 4
                    t.wait_ge(s_tpc, 4 * (ch + 1))
                    aTc = aT_view(ch)
                    if ch < 8:
                        t.wait_ge(s_wk, 16)
                        t.wait_ge(s_wv, 16)
                        for mh in range(8):
                            ei = ch * 8 + mh
                            if ei >= 2:
                                t.wait_ge(s_kev, ei - 1)
                            for dc in range(8):
                                ins = t.matmul(
                                    out=pk[:, (mh % 2) * 512:(mh % 2) * 512 + 512],
                                    lhsT=wk_b[:, dc * D + mh * P: dc * D + (mh + 1) * P],
                                    rhs=aTc[:, dc, :],
                                    start=(dc == 0), stop=(dc == 7))
                            ins.then_inc(s_mm, 1)
                        for tt in range(4):
                            vi = ch * 4 + tt
                            if vi >= 1:
                                t.wait_ge(s_vev, vi)
                            for dc in range(8):
                                for nh in range(2):
                                    ins = t.matmul(
                                        out=pv[:, nh * 512:(nh + 1) * 512],
                                        lhsT=aTc[:, dc, tt * P:(tt + 1) * P],
                                        rhs=wv_b[:, dc * D + nh * 512: dc * D + (nh + 1) * 512],
                                        start=(dc == 0), stop=(dc == 7))
                            ins.then_inc(s_mm, 1)
                    else:
                        t.wait_ge(s_wq, 16)
                        for mh in range(8):
                            ei = 64 + (ch - 8) * 8 + mh
                            if ei >= 2:
                                t.wait_ge(s_kev, ei - 1)
                            for dc in range(8):
                                ins = t.matmul(
                                    out=pk[:, (mh % 2) * 512:(mh % 2) * 512 + 512],
                                    lhsT=wq_b[:, dc * D + mh * P: dc * D + (mh + 1) * P],
                                    rhs=aTc[:, dc, :],
                                    start=(dc == 0), stop=(dc == 7))
                            ins.then_inc(s_mm, 1)

        esA.close()
        esB = ExitStack()
        sbB = lambda nm, shp, dt=F32: esB.enter_context(nc.sbuf_tensor(nm, shp, dt))
        psB = lambda nm, shp, dt=F32: esB.enter_context(nc.psum_tensor(nm, shp, dt))
        scores = sbB("scores", [P, S])
        bs0 = sbB("bs0", [P, S], BF16)
        bs1 = sbB("bs1", [P, S], BF16)
        t1s = sbB("t1s", [P, D])
        onesb = sbB("onesb", [P, D])
        abf = sbB("abf", [P, S], BF16)
        aTg = sbB("aTg", [P, 2 * S], BF16)
        vch = sbB("vch", [P, 2 * D], BF16)
        outb = sbB("outb", [P, 2 * D])
        sqB2 = sbB("sqB2", [P, D])
        xro = sbB("xro", [P, 2 * D])
        pS = psB("pS", [P, 2 * 512])
        ptB = psB("ptB", [P, 2 * D], BF16)
        pav = psB("pav", [P, 2 * D])

        s_S = sem("s_S")
        s_sev = sem("s_sev")
        s_stat = sem("s_stat")
        s_zif = sem("s_zif")
        s_abf = sem("s_abf")
        s_at = sem("s_at")
        s_atc = sem("s_atc")
        s_vld = [sem("s_vld0"), sem("s_vld1")]
        s_av = sem("s_av")
        s_ovar = sem("s_ovar")
        s_ostd = sem("s_ostd")
        s_oev = sem("s_oev")
        s_xro = [sem("s_xro0"), sem("s_xro1")]
        s_st = [sem("s_st0"), sem("s_st1")]

        @block.tensor
        def _(t):
            t.wait_ge(s_mm, 128)
            t.wait_ge(s_kev, 96)
            t.wait_ge(s_tpc, NG)
            t.wait_ge(s_vev, 32)
            t.wait_ge(s_vst[0], 16 * 16)
            t.wait_ge(s_vst[1], 16 * 16)
            t.wait_ge(s_xld[0], 16 * 24)
            t.wait_ge(s_xld[1], 16 * 24)
            t.wait_ge(s_app, NG)
            for qt in range(16):
                qi = qt % 2
                g2 = qt // 2
                for nch in range(8):
                    gn = qt * 8 + nch
                    if gn >= 2:
                        t.wait_ge(s_sev, gn - 1)
                    for hc in range(8):
                        ins = t.matmul(
                            out=pS[:, (gn % 2) * 512:(gn % 2) * 512 + 512],
                            lhsT=qT[:, hc * TOWN + qt * P: hc * TOWN + (qt + 1) * P],
                            rhs=kT[:, hc * S + nch * 512: hc * S + nch * 512 + 512],
                            start=(hc == 0), stop=(hc == 7))
                    ins.then_inc(s_S, 1)
                t.wait_ge(s_abf, qt + 1)
                for hb in range(4):
                    bi = qt * 4 + hb
                    if bi >= 2:
                        t.wait_ge(s_atc, bi - 1)
                    for jj in range(8):
                        jc = hb * 8 + jj
                        ins = t.transpose(
                            out=ptB[:, (hb % 2) * D + jj * P:(hb % 2) * D + (jj + 1) * P],
                            in_=abf[:, jc * P:(jc + 1) * P],
                            identity=idn[:])
                    ins.then_inc(s_at, 1)
                if qi == 1:
                    t.wait_ge(s_atc, (qt + 1) * 4)
                    if g2 >= 1:
                        t.wait_ge(s_oev, 2 * g2)
                    for jc in range(32):
                        li = g2 * 32 + jc
                        t.wait_ge(s_vld[jc % 2], 16 * (li // 2 + 1))
                        for q2 in range(2):
                            for nh in range(2):
                                ins = t.matmul(
                                    out=pav[:, q2 * D + nh * 512: q2 * D + (nh + 1) * 512],
                                    lhsT=aTg[:, q2 * S + jc * P: q2 * S + (jc + 1) * P],
                                    rhs=vch[:, (jc % 2) * D + nh * 512:(jc % 2) * D + (nh + 1) * 512],
                                    start=(jc == 0), stop=(jc == 31))
                        ins.then_inc(s_av, 1)

        @block.scalar
        def _(s):
            s.wait_ge(s_mm, 128)
            s.wait_ge(s_kev, 96)
            s.wait_ge(s_tpc, NG)
            s.wait_ge(s_vev, 32)
            s.wait_ge(s_vst[0], 16 * 16)
            s.wait_ge(s_vst[1], 16 * 16)
            s.wait_ge(s_xld[0], 16 * 24)
            s.wait_ge(s_xld[1], 16 * 24)
            s.wait_ge(s_app, NG)
            for qt in range(16):
                for nch in range(8):
                    gn = qt * 8 + nch
                    s.wait_ge(s_S, gn + 1)
                    if nch == 0:
                        s.wait_ge(s_abf, qt)
                    s.activation(out=scores[:, nch * 512:(nch + 1) * 512],
                                 in_=pS[:, (gn % 2) * 512:(gn % 2) * 512 + 512],
                                 func=AF.Copy, scale=1.0 / 32.0).then_inc(s_sev, 1)
                if qt % 2 == 1:
                    for q2 in range(2):
                        o = qt - 1 + q2
                        s.wait_ge(s_ovar, o + 1)
                        s.activation(out=co(C_OSTD), in_=co(C_OVAR),
                                     func=AF.Sqrt).then_inc(s_ostd, 1)

        @block.gpsimd
        def _(g):
            g.wait_ge(s_mm, 128)
            g.wait_ge(s_kev, 96)
            g.wait_ge(s_tpc, NG)
            g.wait_ge(s_vev, 32)
            g.wait_ge(s_vst[0], 16 * 16)
            g.wait_ge(s_vst[1], 16 * 16)
            g.wait_ge(s_xld[0], 16 * 24)
            g.wait_ge(s_xld[1], 16 * 24)
            g.wait_ge(s_app, NG)
            for qt in range(16):
                g.wait_ge(s_stat, qt + 1)
                with nc.allow_low_precision(reason="bf16 sort keys"):
                    g.tensor_scalar(out=bs0[:], in0=scores[:],
                                    scalar1=co(C_NMAX), scalar2=None,
                                    op0=ALU.add).then_inc(s_zif, 1)

        @block.vector
        def _(v):
            v.wait_ge(s_mm, 128)
            v.wait_ge(s_kev, 96)
            v.wait_ge(s_tpc, NG)
            v.wait_ge(s_vev, 32)
            v.wait_ge(s_vst[0], 16 * 16)
            v.wait_ge(s_vst[1], 16 * 16)
            v.wait_ge(s_xld[0], 16 * 24)
            v.wait_ge(s_xld[1], 16 * 24)
            v.wait_ge(s_app, NG)
            v.memset(onesb[:], 1.0)
            for qt in range(16):
                qi = qt % 2
                g2 = qt // 2
                v.wait_ge(s_sev, (qt + 1) * 8)
                v.reduce_max(out=co(C_NMAX), in_=scores[:], axis=AX, negate=True)
                rsync(v, v.reduce_sum(out=co(C_SSUM), in_=scores[:], axis=AX))
                rsync(v, v.scalar_tensor_tensor(out=co(C_SUMZ), in0=co(C_NMAX),
                                                scalar=float(S), in1=co(C_SSUM),
                                                op0=ALU.mult, op1=ALU.add))
                rsync(v, v.tensor_scalar(out=co(C_STEP), in0=co(C_SUMZ),
                                         scalar1=-1.0, scalar2=1.0 / S,
                                         op0=ALU.add, op1=ALU.mult))
                v.tensor_sub(out=co(C_STEPM), in0=co(C_STEP),
                             in1=co(C_NMAX)).then_inc(s_stat, 1)
                v.wait_ge(s_zif, qt + 1)
                v.tensor_scalar(out=bs1[:], in0=scores[:], scalar1=co(C_STEPM),
                                scalar2=None, op0=ALU.is_gt)
                rsync(v, v.reduce_sum(out=co(C_KC), in_=bs1[:], axis=AX))
                rsync(v, v.reciprocal(out=co(C_RK), in_=co(C_KC)))
                rsync(v, v.tensor_scalar(out=co(C_NRK), in0=co(C_RK),
                                         scalar1=-1.0, scalar2=None,
                                         op0=ALU.mult))
                rsync(v, v.tensor_add(out=co(C_ALPHA), in0=co(C_STEP),
                                      in1=co(C_RK)))
                v.tensor_add(out=co(C_ALPHA2), in0=co(C_ALPHA), in1=co(C_NMAX))
                emit_bitonic(v, bs0, bs1, S)
                for seg in range(4):
                    ins = v.tensor_tensor_scan(
                        out=t1s[:],
                        data0=onesb[:], data1=bs0[:, seg * D:(seg + 1) * D],
                        initial=0.0, op0=ALU.mult, op1=ALU.add)
                    if seg > 0:
                        ins = v.tensor_scalar(out=t1s[:], in0=t1s[:],
                                              scalar1=co(C_CT), scalar2=None,
                                              op0=ALU.add)
                    if seg < 3:
                        rsync(v, ins)
                        rsync(v, v.tensor_copy(out=co(C_CT),
                                               in_=t1s[:, D - 1:D]))
                    v.tensor_scalar(out=t1s[:], in0=t1s[:],
                                    scalar1=co(C_NRK), scalar2=None,
                                    op0=ALU.mult)
                    v.tensor_add(out=t1s[:], in0=t1s[:],
                                 in1=scores[:, seg * D:(seg + 1) * D])
                    ins = v.tensor_scalar(out=abf[:, seg * D:(seg + 1) * D],
                                          in0=t1s[:], scalar1=co(C_ALPHA2),
                                          scalar2=0.0, op0=ALU.add,
                                          op1=ALU.max)
                ins.then_inc(s_abf, 1)
                for hb in range(4):
                    v.wait_ge(s_at, qt * 4 + hb + 1)
                    v.tensor_copy(out=aTg[:, qi * S + hb * D: qi * S + (hb + 1) * D],
                                  in_=ptB[:, (hb % 2) * D:(hb % 2 + 1) * D]
                                  ).then_inc(s_atc, 1)
                if qi == 1:
                    for q2 in range(2):
                        o = qt - 1 + q2
                        v.wait_ge(s_av, 32 * (g2 + 1))
                        if o >= 2:
                            v.wait_ge(s_st[o % 2], 16 * (o // 2))
                        ob = outb[:, (o % 2) * D:(o % 2 + 1) * D]
                        v.tensor_copy(out=ob, in_=pav[:, q2 * D:(q2 + 1) * D])
                        v.tensor_mul(out=sqB2[:], in0=ob, in1=ob)
                        v.reduce_sum(out=co(C_OS1), in_=ob, axis=AX)
                        rsync(v, v.reduce_sum(out=co(C_OS2), in_=sqB2[:], axis=AX))
                        rsync(v, v.tensor_scalar(out=co(C_OMEAN), in0=co(C_OS1),
                                                 scalar1=1.0 / D, scalar2=None,
                                                 op0=ALU.mult))
                        rsync(v, v.tensor_mul(out=co(C_OVAR), in0=co(C_OMEAN),
                                              in1=co(C_OMEAN)))
                        rsync(v, v.scalar_tensor_tensor(out=co(C_OVAR), in0=co(C_OS2),
                                                        scalar=1.0 / D, in1=co(C_OVAR),
                                                        op0=ALU.mult, op1=ALU.subtract))
                        v.tensor_scalar(out=co(C_OVAR), in0=co(C_OVAR), scalar1=1e-5,
                                        scalar2=None, op0=ALU.add).then_inc(s_ovar, 1)
                        v.wait_ge(s_ostd, o + 1)
                        rsync(v, v.reciprocal(out=co(C_ORSTD), in_=co(C_OSTD)))
                        rsync(v, v.tensor_mul(out=co(C_MR), in0=co(C_OMEAN),
                                              in1=co(C_ORSTD)))
                        v.wait_ge(s_xro[o % 2], 16 * (o // 2 + 1))
                        v.tensor_scalar(out=ob, in0=ob, scalar1=co(C_ORSTD),
                                        scalar2=co(C_MR), op0=ALU.mult,
                                        op1=ALU.subtract)
                        v.tensor_add(out=ob, in0=ob,
                                     in1=xro[:, (o % 2) * D:(o % 2 + 1) * D]
                                     ).then_inc(s_oev, 1)

        @block.sync
        def _(sy):
            sy.wait_ge(s_mm, 128)
            sy.wait_ge(s_kev, 96)
            sy.wait_ge(s_tpc, NG)
            sy.wait_ge(s_vev, 32)
            sy.wait_ge(s_vst[0], 16 * 16)
            sy.wait_ge(s_vst[1], 16 * 16)
            sy.wait_ge(s_xld[0], 16 * 24)
            sy.wait_ge(s_xld[1], 16 * 24)
            sy.wait_ge(s_app, NG)
            for g2 in range(8):
                for jc in range(32):
                    li = g2 * 32 + jc
                    if li >= 2:
                        sy.wait_ge(s_av, li - 1)
                    sy.dma_start(out=vch[:, (jc % 2) * D:(jc % 2 + 1) * D],
                                 in_=vscr[jc * P:(jc + 1) * P, :]
                                 ).then_inc(s_vld[jc % 2], 16)
                for qi2 in range(2):
                    o = g2 * 2 + qi2
                    if o >= 2:
                        sy.wait_ge(s_oev, o - 1)
                    sy.dma_start(out=xro[:, (o % 2) * D:(o % 2 + 1) * D],
                                 in_=xo[o * P:(o + 1) * P, :]).then_inc(s_xro[o % 2], 16)
                    sy.wait_ge(s_oev, o + 1)
                    sy.dma_start(out=x1o[o * P:(o + 1) * P, :],
                                 in_=outb[:, (o % 2) * D:(o % 2 + 1) * D]
                                 ).then_inc(s_st[o % 2], 16)
            sy.wait_ge(s_st[0], 16 * 8)
            sy.wait_ge(s_st[1], 16 * 8)
            if debug:
                for nm, buf in (("d_kT", kT), ("d_qT", qT), ("d_scores", scores),
                                ("d_bs0", bs0), ("d_bs1", bs1), ("d_abf2", abf),
                                ("d_cols", cols), ("d_outb", outb), ("d_aTg", aTg)):
                    sy.dma_start(out=dbg[nm], in_=buf[:]).then_inc(s_st[0], 16)
                sy.wait_ge(s_st[0], 16 * 16)

        esB.close()
    es.close()
    return nc




def _prep_l0attn_inputs(inputs):
    x0 = np.ascontiguousarray(inputs["L0_attn_x"])
    wqt = np.ascontiguousarray(inputs["L0_attn_Wq"].T)
    wkt = np.ascontiguousarray(inputs["L0_attn_Wk"].T)
    wvt = np.ascontiguousarray(inputs["L0_attn_Wv"].T)
    idn = np.eye(P, dtype=BF16NP)
    maps = []
    for c in range(8):
        b, h = c // 2, c % 2
        maps.append({
            "xf": x0[b],
            "xo": np.ascontiguousarray(x0[b, h * TOWN:(h + 1) * TOWN]),
            "wqt": wqt, "wkt": wkt, "wvt": wvt, "idn": idn,
        })
    return maps


def run_l0attn(inputs):
    nc = build_l0attn()
    maps = _prep_l0attn_inputs(inputs)
    res = run_bass_kernel_spmd(nc, maps, core_ids=list(range(8)))
    x1 = np.empty((4, S, D), np.float32)
    for c in range(8):
        b, h = c // 2, c % 2
        x1[b, h * TOWN:(h + 1) * TOWN] = res.results[c]["x1o"]
    return x1


def kernel(**inputs):
    inputs = {k: np.asarray(v) for k, v in inputs.items()}
    x1 = run_l0attn(inputs)
    mid = run_l0ffn_layer(0, x1, inputs)
    ff = run_l0ffn_layer(1, mid, inputs)
    x3 = run_l1attn(x1, ff, inputs)
    xmid = run_l1ffn_attn(x3, inputs)
    x4 = run_l1moe(xmid, x3, inputs)
    return x4



NT_B = 21
NQ_B = 17
NF_B = 16
HID2 = 2816
W1W = 5632
FCW = [512, 512, 512, 512, 512, 256]


P1LVL = [5]


def build_l0ffn(apply_ln, upto=5, debug=False):
    p1lvl = P1LVL[0]
    nc = bass.Bass("TRN2", target_bir_lowering=False)
    yh = nc.dram_tensor("yh", [2688, D], F32, kind="ExternalInput").ap()
    wqkv = nc.dram_tensor("wqkv", [D, 1536], F32, kind="ExternalInput").ap()
    wo_ = nc.dram_tensor("wo_", [512, D], F32, kind="ExternalInput").ap()
    lngc = nc.dram_tensor("lngc", [P, 8], F32, kind="ExternalInput").ap()
    lnbc = nc.dram_tensor("lnbc", [P, 8], F32, kind="ExternalInput").ap()
    qs_d = nc.dram_tensor("qs", [1, 64], F32, kind="ExternalInput").ap()
    ks_d = nc.dram_tensor("ks", [1, 64], F32, kind="ExternalInput").ap()
    w1p = nc.dram_tensor("w1p", [D, W1W], F32, kind="ExternalInput").ap()
    b1p = nc.dram_tensor("b1p", [1, W1W], F32, kind="ExternalInput").ap()
    w2p = nc.dram_tensor("w2p", [HID2, D], F32, kind="ExternalInput").ap()
    b2r = nc.dram_tensor("b2r", [1, D], F32, kind="ExternalInput").ap()
    tabs_d = {nm: nc.dram_tensor(nm, [2688, 64], F32, kind="ExternalInput").ap()
              for nm in ("csq", "ssq", "csk", "ssk")}
    maskT = nc.dram_tensor("maskT", [NQ_B, 5 * P, P], F32, kind="ExternalInput").ap()
    idn_d = nc.dram_tensor("idn", [P, P], BF16, kind="ExternalInput").ap()
    msh_d = nc.dram_tensor("mshift", [P, P], F32, kind="ExternalInput").ap()
    mla_d = nc.dram_tensor("mlast", [P, P], F32, kind="ExternalInput").ap()
    onesr_d = nc.dram_tensor("onesr", [1, P], BF16, kind="ExternalInput").ap()
    yout = nc.dram_tensor("yout", [TOWN, D], F32, kind="ExternalOutput").ap()
    xf1scr = nc.dram_tensor("xf1scr", [NQ_B * P, D], F32, kind="Internal").ap()
    dbg = {}
    if debug:
        for nm, w, dt in (("dB_kTr", NT_B * 512, BF16), ("dB_qTr", NQ_B * 512, BF16),
                          ("dB_vtok", NT_B * 512, BF16), ("dB_xf1", NQ_B * 8 * P, BF16),
                          ("dB_acc", NF_B * D, F32), ("dB_r2", NF_B, F32),
                          ("dB_obf", NQ_B * 512, BF16), ("dB_den", NQ_B * 8, F32),
                          ("dB_slf", 5 * P, F32),
                          ("dB_xft", NQ_B * D, BF16),
                          ("dB_rows", NF_B * P, F32), ("dB_statc", NQ_B * 4, F32)):
            dbg[nm] = nc.dram_tensor(nm, [P, w], dt, kind="ExternalOutput").ap()
    s_dbg = nc.semaphore("sB_dbg").__enter__()

    es = ExitStack()
    sbt = lambda nm, shp, dt=F32: es.enter_context(nc.sbuf_tensor(nm, shp, dt))
    sem = lambda nm: es.enter_context(nc.semaphore(nm))

    idn = sbt("idnb", [P, P], BF16)
    msh = sbt("mshb", [P, P], F32)
    mla = sbt("mlab", [P, P], F32)
    cols = sbt("colsB", [P, 64])
    co = lambda i: cols[:, i:i + 1]
    (B_S1, B_S2, B_MEAN, B_VAR, B_STD, B_RSTD,
     B_MU2, B_EX2, B_VAR2, B_STD2, B_R2, B_NMU2, B_DEN, B_T1, B_T2) = range(15)
    B_NRM = 16
    B_SNRM = 32
    statml = sbt("statml", [P, NT_B * 2])
    statc = sbt("statc", [P, NQ_B * 4])
    r2col = sbt("r2col", [P, NF_B])
    lngb = sbt("lngb", [P, 8])
    lnbb = sbt("lnbb", [P, 8])
    gcolb = sbt("gcolb", [P, 8], BF16)
    lcolb = sbt("lcolb", [P, 8], BF16)
    onecb = sbt("onecb", [P, 1], BF16)
    onesr = sbt("onesrB", [1, P], BF16)
    xf1T = sbt("xf1TB", [P, 8 * NQ_B * P], BF16)
    rows2a = sbt("rows2aB", [1, NF_B * P], BF16)
    rows2b = sbt("rows2bB", [1, NF_B * P], BF16)
    rw2bf = sbt("rw2bfB", [P, 2], BF16)
    esR = ExitStack()
    qTr = esR.enter_context(nc.sbuf_tensor("qTrB", [P, NQ_B * 512], BF16))
    kTr = esR.enter_context(nc.sbuf_tensor("kTrB", [P, NT_B * 512], BF16))
    vtok = esR.enter_context(nc.sbuf_tensor("vtokB", [P, NT_B * 512], BF16))
    pbig = es.enter_context(nc.psum_tensor("pbigB", [P, 2048], F32))
    pex = es.enter_context(nc.psum_tensor("pexB", [P, 512], F32))
    ptp1 = es.enter_context(nc.psum_tensor("ptp1B", [P, D], BF16))
    prot = es.enter_context(nc.psum_tensor("protB", [P, D], BF16))

    s_w = sem("sB_w")
    s_ld = [sem("sB_ld0"), sem("sB_ld1")]
    s_var = sem("sB_var")
    s_std = sem("sB_std")
    s_rt = sem("sB_rt")
    s_app = sem("sB_app")
    s_tp = sem("sB_tp")
    s_tpc = sem("sB_tpc")
    s_qkv = sem("sB_qkv")
    s_qke = sem("sB_qke")
    s_nrm = sem("sB_nrm")
    s_nsq = sem("sB_nsq")
    s_rot = sem("sB_rot")
    s_rtp = sem("sB_rtp")
    s_rtc = sem("sB_rtc")
    rt_cnt = [0]
    rsync = _mk_rsync(s_rt, rt_cnt)

    es1 = ExitStack()
    sb1 = lambda nm, shp, dt=F32: es1.enter_context(nc.sbuf_tensor(nm, shp, dt))
    wqkv_b = sb1("wqkv_b", [P, 8 * 1536], BF16)
    ybuf = sb1("ybufB", [P, 2 * D])
    sqb = sb1("sqbB", [P, 512])
    a_bf = sb1("a_bfB", [P, 2 * D], BF16)
    yT = sb1("yTB", [P, 8 * 2688], BF16)
    qkvb = sb1("qkvbB", [P, 1536])
    qrot = sb1("qrotB", [P, 2 * 512], BF16)
    krot = sb1("krotB", [P, 2 * 512], BF16)
    tq1 = sb1("tq1B", [P, NT_B * 64], BF16)
    tq2 = sb1("tq2B", [P, NT_B * 64], BF16)
    tk1 = sb1("tk1B", [P, NT_B * 64], BF16)
    tk2 = sb1("tk2B", [P, NT_B * 64], BF16)
    qhat = sb1("qhatB", [P, 512], BF16)[:, :]
    rotb = sb1("rotbB", [P, 512], BF16)[:, :]
    qsx = sb1("qsxB", [P, 64], BF16)
    ksx = sb1("ksxB", [P, 64], BF16)

    pqkv = pbig[:, 0:1536]

    def yT_v(dc):
        return yT[:, dc * 2688:(dc + 1) * 2688]

    def qkv_lhsT(g, dc):
        if dc < 4:
            return yT_v(dc)[:, g * P:(g + 1) * P]
        lo = max(g * P - 1, 0)
        return yT_v(dc)[:, lo:lo + P]

    def _qkv_mm(t, g):
        t.wait_ge(s_tpc, g + 1)
        if g >= 1:
            t.wait_ge(s_qke, g)
        for nch in range(3):
            for dc in range(8):
                ins = t.matmul(out=pqkv[:, nch * 512:(nch + 1) * 512],
                               lhsT=qkv_lhsT(g, dc),
                               rhs=wqkv_b[:, dc * 1536 + nch * 512:
                                          dc * 1536 + (nch + 1) * 512],
                               start=(dc == 0), stop=(dc == 7))
        ins.then_inc(s_qkv, 1)

    def _rot_tp(t, g):
        t.wait_ge(s_rot, g + 1)
        if g >= 1:
            t.wait_ge(s_rtc, g)
        qb = (g % 2) * 512
        for bc in range(4):
            ins = t.transpose(out=prot[:, bc * P:(bc + 1) * P],
                              in_=krot[:, qb + bc * P:qb + (bc + 1) * P],
                              identity=idn[:])
        if g >= 4 and P1LVL[0] >= 4:
            for bc in range(4):
                ins = t.transpose(out=prot[:, 512 + bc * P:512 + (bc + 1) * P],
                                  in_=qrot[:, qb + bc * P:qb + (bc + 1) * P],
                                  identity=idn[:])
        ins.then_inc(s_rtp, 1)

    def _rot_dve(v, g):
        v.wait_ge(s_qkv, g + 1)
        qb = 0
        v.tensor_copy(out=qkvb[:, qb:qb + 1536], in_=pqkv[:]).then_inc(s_qke, 1)
        v.tensor_copy(out=vtok[:, g * 512:(g + 1) * 512],
                      in_=qkvb[:, qb + 1024:qb + 1536])
        v.tensor_mul(out=sqb[:, 0:512], in0=qkvb[:, qb + 512:qb + 1024],
                     in1=qkvb[:, qb + 512:qb + 1024])
        v.reduce_sum(out=cols[:, B_NRM + 8:B_NRM + 16],
                     in_=sqb[:, 0:512].rearrange("p (h c) -> p h c", h=8), axis=AX)
        v.tensor_mul(out=sqb[:, 0:512], in0=qkvb[:, qb:qb + 512],
                     in1=qkvb[:, qb:qb + 512])
        v.reduce_sum(out=cols[:, B_NRM:B_NRM + 8],
                     in_=sqb[:, 0:512].rearrange("p (h c) -> p h c", h=8),
                     axis=AX).then_inc(s_nrm, 1)
        v.wait_ge(s_nsq, g + 1)
        rsync(v, v.tensor_scalar(out=cols[:, B_SNRM:B_SNRM + 16],
                                 in0=cols[:, B_SNRM:B_SNRM + 16],
                                 scalar1=1e-12, scalar2=None, op0=ALU.max))
        rsync(v, v.reciprocal(out=cols[:, B_SNRM:B_SNRM + 16],
                              in_=cols[:, B_SNRM:B_SNRM + 16]))
        if P1LVL[0] < 2:
            v.nop().then_inc(s_rot, 1)
            return
        if g >= 2 and P1LVL[0] >= 3:
            v.wait_ge(s_rtp, g - 1)
        srcs = [(qkvb[:, qb + 512:qb + 1024], tk1, tk2, krot, 8)]
        if g >= 4:
            srcs.append((qkvb[:, qb:qb + 512], tq1, tq2, qrot, 0))
        for src, t1, t2, dst, nbo in srcs:
            nb = B_SNRM + nbo
            for h in range(8):
                v.tensor_scalar(out=qhat[:, h * 64:(h + 1) * 64] if False else qhat[:, h * 64:(h + 1) * 64],
                                in0=src[:, h * 64:(h + 1) * 64],
                                scalar1=cols[:, nb + h:nb + h + 1],
                                scalar2=None, op0=ALU.mult)
            qh3 = qhat.rearrange("p (h c) -> p h c", h=8)
            rb3 = rotb.rearrange("p (h c) -> p h c", h=8)
            v.tensor_scalar(out=rb3[:, :, 0:32], in0=qh3[:, :, 32:64],
                            scalar1=-1.0, scalar2=None, op0=ALU.mult)
            v.tensor_copy(out=rb3[:, :, 32:64], in_=qh3[:, :, 0:32])
            tt1 = t1[:, g * 64:(g + 1) * 64]
            tt2 = t2[:, g * 64:(g + 1) * 64]
            for h in range(8):
                v.tensor_mul(out=qh3[:, h, :], in0=qh3[:, h, :], in1=tt1)
                v.tensor_mul(out=rb3[:, h, :], in0=rb3[:, h, :], in1=tt2)
            ins = v.tensor_add(out=dst[:, (g % 2) * 512:(g % 2 + 1) * 512],
                               in0=qhat, in1=rotb)
        ins.then_inc(s_rot, 1)

    def _rtc_dve(v, g):
        v.wait_ge(s_rtp, g + 1)
        ins = v.tensor_copy(out=kTr[:, g * 512:(g + 1) * 512], in_=prot[:, 0:512])
        if g >= 4 and P1LVL[0] >= 5:
            for bc in range(4):
                ins = v.tensor_copy(
                    out=qTr[:, (g - 4) * 512 + bc * P:(g - 4) * 512 + (bc + 1) * P],
                    in_=prot[:, 512 + bc * P:512 + (bc + 1) * P])
        ins.then_inc(s_rtc, 1)

    def _ln_tile(v, g, yt, dest_bf):
        if apply_ln:
            v.tensor_mul(out=sqb[:], in0=yt[:, 0:512], in1=yt[:, 0:512])
            v.reduce_sum(out=co(B_S2), in_=sqb[:], axis=AX)
            v.tensor_mul(out=sqb[:], in0=yt[:, 512:1024], in1=yt[:, 512:1024])
            v.reduce_sum(out=co(B_EX2), in_=sqb[:], axis=AX)
            v.reduce_sum(out=co(B_S1), in_=yt, axis=AX)
            rsync(v, v.tensor_add(out=co(B_S2), in0=co(B_S2), in1=co(B_EX2)))
            rsync(v, v.tensor_scalar(out=co(B_MEAN), in0=co(B_S1),
                                     scalar1=1.0 / D, scalar2=None,
                                     op0=ALU.mult))
            rsync(v, v.tensor_mul(out=co(B_VAR), in0=co(B_MEAN), in1=co(B_MEAN)))
            rsync(v, v.scalar_tensor_tensor(out=co(B_VAR), in0=co(B_S2),
                                            scalar=1.0 / D, in1=co(B_VAR),
                                            op0=ALU.mult, op1=ALU.subtract))
            v.tensor_scalar(out=co(B_VAR), in0=co(B_VAR), scalar1=1e-5,
                            scalar2=None, op0=ALU.add).then_inc(s_var, 1)
            v.wait_ge(s_std, g + 1)
            rsync(v, v.reciprocal(out=co(B_RSTD), in_=co(B_STD)))
            v.tensor_copy(out=statml[:, g * 2:g * 2 + 1], in_=co(B_MEAN))
            v.tensor_copy(out=statml[:, g * 2 + 1:g * 2 + 2], in_=co(B_RSTD))
            v.tensor_scalar(out=dest_bf, in0=yt, scalar1=co(B_MEAN),
                            scalar2=co(B_RSTD), op0=ALU.subtract,
                            op1=ALU.mult).then_inc(s_app, 1)
        else:
            v.tensor_copy(out=dest_bf, in_=yt).then_inc(s_app, 1)

    with nc.Block() as block:
        @block.gpsimd
        def _(g):
            g.dma_start(out=idn[:], in_=idn_d).then_inc(s_w, 16)
            g.dma_start(out=msh[:], in_=msh_d).then_inc(s_w, 16)
            g.dma_start(out=mla[:], in_=mla_d).then_inc(s_w, 16)
            g.dma_start(out=wqkv_b[:, :].rearrange("p (dc c) -> p dc c", dc=8),
                        in_=wqkv.rearrange("(dc p) c -> p dc c", p=P)
                        ).then_inc(s_w, 16)
            for i, nm in enumerate(("csq", "ssq", "csk", "ssk")):
                dst = (tq1, tq2, tk1, tk2)[i]
                g.dma_start(out=dst[:, :].rearrange("p (t c) -> p t c", c=64),
                            in_=tabs_d[nm].rearrange("(t p) c -> p t c", p=P)
                            ).then_inc(s_w, 16)
            g.dma_start(out=qsx[0:1, :], in_=qs_d).then_inc(s_w, 16)
            g.dma_start(out=ksx[0:1, :], in_=ks_d).then_inc(s_w, 16)
            g.dma_start(out=lngb[:], in_=lngc).then_inc(s_w, 16)
            g.dma_start(out=lnbb[:], in_=lnbc).then_inc(s_w, 16)
            g.dma_start(out=onesr[:], in_=onesr_d).then_inc(s_w, 16)

        @block.sync
        def _(sy):
            for g in range(NT_B):
                if g >= 2:
                    if p1lvl >= 1:
                        sy.wait_ge(s_qke, g - 1)
                    else:
                        sy.wait_ge(s_app, g - 1)
                sy.dma_start(out=ybuf[:, (g % 2) * D:(g % 2 + 1) * D],
                             in_=yh[g * P:(g + 1) * P, :]).then_inc(s_ld[g % 2], 16)

        @block.vector
        def _(v):
            v.memset(onecb[:], 1.0)
            v.wait_ge(s_tp, 1)
            v.tensor_copy(out=qsx[:], in_=pbig[:, 1536:1600])
            v.tensor_copy(out=ksx[:], in_=pbig[:, 1600:1664])
            v.tensor_copy(out=qhat[:, 0:32], in_=qsx[:, 32:64])
            v.tensor_copy(out=qhat[:, 32:64], in_=qsx[:, 0:32])
            v.tensor_copy(out=rotb[:, 0:32], in_=ksx[:, 32:64])
            v.tensor_copy(out=rotb[:, 32:64], in_=ksx[:, 0:32])
            for tt in range(NT_B):
                sl = slice(tt * 64, (tt + 1) * 64)
                v.tensor_mul(out=tq1[:, sl], in0=tq1[:, sl], in1=qsx[:])
                v.tensor_mul(out=tq2[:, sl], in0=tq2[:, sl], in1=qhat[:, 0:64])
                v.tensor_mul(out=tk1[:, sl], in0=tk1[:, sl], in1=ksx[:])
                v.tensor_mul(out=tk2[:, sl], in0=tk2[:, sl], in1=rotb[:, 0:64])
            v.tensor_copy(out=gcolb[:], in_=lngb[:])
            v.tensor_copy(out=lcolb[:], in_=lnbb[:]).then_inc(s_tpc, 0) if False \
                else v.tensor_copy(out=lcolb[:], in_=lnbb[:])
            for g in range(NT_B):
                if g >= 1:
                    gp = g - 1
                    v.wait_ge(s_tp, gp + 2)
                    v.tensor_copy(
                        out=yT[:, :].rearrange("p (dc t) -> p dc t", dc=8)[
                            :, :, gp * P:(gp + 1) * P],
                        in_=ptp1[:, :].rearrange("p (dc t) -> p dc t", dc=8)
                    ).then_inc(s_tpc, 1)
                    if p1lvl >= 1:
                        _rot_dve(v, gp)
                b = (g % 2) * D
                yt = ybuf[:, b:b + D]
                v.wait_ge(s_ld[g % 2], 16 * (g // 2 + 1))
                _ln_tile(v, g, yt, a_bf[:, b:b + D])
                if g >= 2 and p1lvl >= 3:
                    _rtc_dve(v, g - 2)
            gp = NT_B - 1
            v.wait_ge(s_tp, gp + 2)
            v.tensor_copy(
                out=yT[:, :].rearrange("p (dc t) -> p dc t", dc=8)[
                    :, :, gp * P:(gp + 1) * P],
                in_=ptp1[:, :].rearrange("p (dc t) -> p dc t", dc=8)
            ).then_inc(s_tpc, 1)
            if p1lvl >= 1:
                _rot_dve(v, gp)
            if p1lvl >= 3:
                _rtc_dve(v, NT_B - 2)
                _rtc_dve(v, NT_B - 1)

        @block.scalar
        def _(s):
            for g in range(NT_B):
                if apply_ln:
                    s.wait_ge(s_var, g + 1)
                    s.activation(out=co(B_STD), in_=co(B_VAR),
                                 func=AF.Sqrt).then_inc(s_std, 1)
                if p1lvl >= 1:
                    s.wait_ge(s_nrm, g + 1)
                    s.activation(out=cols[:, B_SNRM:B_SNRM + 16],
                                 in_=cols[:, B_NRM:B_NRM + 16],
                                 func=AF.Sqrt).then_inc(s_nsq, 1)

        @block.tensor
        def _(t):
            t.wait_ge(s_w, 208)
            t.matmul(out=pbig[:, 1536:1600], lhsT=onesr[0:1, :],
                     rhs=qsx[0:1, :], start=True, stop=False)
            t.matmul(out=pbig[:, 1600:1664], lhsT=onesr[0:1, :],
                     rhs=ksx[0:1, :], start=False, stop=True).then_inc(s_tp, 1)
            for g in range(NT_B):
                if g >= 1 and p1lvl >= 1:
                    _qkv_mm(t, g - 1)
                t.wait_ge(s_app, g + 1)
                if g >= 1:
                    t.wait_ge(s_tpc, g)
                b = (g % 2) * D
                for dc in range(8):
                    ins = t.transpose(out=ptp1[:, dc * P:(dc + 1) * P],
                                      in_=a_bf[:, b + dc * P: b + (dc + 1) * P],
                                      identity=idn[:])
                ins.then_inc(s_tp, 1)
                if g >= 2 and p1lvl >= 3:
                    _rot_tp(t, g - 2)
            if p1lvl >= 1:
                _qkv_mm(t, NT_B - 1)
            if p1lvl >= 3:
                _rot_tp(t, NT_B - 2)
                _rot_tp(t, NT_B - 1)

        if upto <= 1:
            @block.sync
            def _(sy):
                p1x = [(s_tpc, NT_B), (s_app, NT_B)]
                if p1lvl >= 1:
                    p1x += [(s_qkv, NT_B), (s_qke, NT_B), (s_rot, NT_B),
                            (s_nsq, NT_B)]
                if p1lvl >= 3:
                    p1x += [(s_rtc, NT_B)]
                for sm, vv in p1x:
                    sy.wait_ge(sm, vv)
                n = 0
                if debug:
                    for nm, buf in (("dB_kTr", kTr), ("dB_qTr", qTr), ("dB_vtok", vtok)):
                        sy.dma_start(out=dbg[nm], in_=buf[:]).then_inc(s_dbg, 16)
                        n += 16
                sy.dma_start(out=yout[0:P, :], in_=ybuf[:, 0:D]).then_inc(s_dbg, 16)
                sy.wait_ge(s_dbg, n + 16)
            es1.close()
            esR.close()
            es.close()
            return nc
        es1.close()
        es2 = ExitStack()
        sb2 = lambda nm, shp, dt=F32: es2.enter_context(nc.sbuf_tensor(nm, shp, dt))
        wo_b = sb2("wo_bB", [P, 4 * D], BF16)
        mkb = sb2("mkbB", [P, 2 * 5 * P])
        slf = sb2("slfB", [P, 2 * P])
        slb = sb2("slbB", [P, 5 * P], BF16)
        o_bf = sb2("o_bfB", [P, 512], BF16)
        oT = sb2("oTB", [P, 512], BF16)
        ybuf2 = sb2("ybuf2B", [P, 2 * D])
        sqb2 = sb2("sqb2B", [P, D])
        xf1t = sb2("xf1tB", [P, 2 * D])
        xf1bf = sb2("xf1bfB", [P, D], BF16)
        if debug:
            dbg_slf = sb2("dbgslfB", [P, 5 * P])
            dbg_obf = sb2("dbgobfB", [P, NQ_B * 512], BF16)
            dbg_den = sb2("dbgdenB", [P, NQ_B * 8])
            dbg_xft = sb2("dbgxftB", [P, NQ_B * D], BF16)

        pS2h = [pex[:, 0:128], pbig[:, 0:128]]
        po2 = pbig[:, 512:640]
        pwo2 = pbig[:, 1024:2048]

        s_wo_ld = sem("sB_wold")
        s_mk = [sem("sB_mk0"), sem("sB_mk1")]
        s_ld2 = [sem("sB_ld2a"), sem("sB_ld2b")]
        s_S2 = sem("sB_S2")
        s_msk = sem("sB_msk")
        s_exp = sem("sB_exp")
        s_pv = sem("sB_pv")
        s_oev = sem("sB_oev")
        s_otp = sem("sB_otp")
        s_otc = sem("sB_otc")
        s_wo2 = sem("sB_wo2")
        s_xft = sem("sB_xft")
        s_xtp = sem("sB_xtp")
        s_xtc = sem("sB_xtc")
        s_xfs = [sem("sB_xfs0"), sem("sB_xfs1")]

        def p1gate(e):
            e.wait_ge(s_qkv, NT_B)
            e.wait_ge(s_qke, NT_B)
            e.wait_ge(s_tpc, NT_B)
            e.wait_ge(s_rtc, NT_B)
            e.wait_ge(s_rot, NT_B)
            e.wait_ge(s_app, NT_B)
            e.wait_ge(s_nsq, NT_B)
            e.wait_ge(s_ld[0], 16 * 11)
            e.wait_ge(s_ld[1], 16 * 10)

        @block.gpsimd
        def _(g):
            p1gate(g)
            g.dma_start(out=wo_b[:, :].rearrange("p (dc c) -> p dc c", dc=4),
                        in_=wo_.rearrange("(dc p) c -> p dc c", p=P)
                        ).then_inc(s_wo_ld, 16)

        @block.sync
        def _(sy):
            p1gate(sy)
            for qt in range(NQ_B):
                if qt >= 2:
                    sy.wait_ge(s_msk, (qt - 1) * 40)
                sy.dma_start(out=mkb[:, (qt % 2) * 640:(qt % 2 + 1) * 640]
                             .rearrange("p (c q) -> p c q", c=5),
                             in_=maskT[qt].rearrange("(c p) q -> p c q", p=P)
                             ).then_inc(s_mk[qt % 2], 16)
                if qt >= 2:
                    sy.wait_ge(s_xft, qt - 1)
                sy.dma_start(out=ybuf2[:, (qt % 2) * D:(qt % 2 + 1) * D],
                             in_=yh[(qt + 4) * P:(qt + 5) * P, :]
                             ).then_inc(s_ld2[qt % 2], 16)
                sy.wait_ge(s_xft, qt + 1)
                sy.dma_start(out=xf1scr[qt * P:(qt + 1) * P, :],
                             in_=xf1t[:, (qt % 2) * D:(qt % 2 + 1) * D]
                             ).then_inc(s_xfs[qt % 2], 16)

        @block.tensor
        def _(t):
            t.wait_ge(s_wo_ld, 16)
            for qt in range(NQ_B):
                t.wait_ge(s_mk[qt % 2], 16 * (qt // 2 + 1))
                for h in range(8):
                    hb = (h % 2) * 64
                    hc = (h // 2) * P
                    for jt in range(5):
                        idx = (qt * 8 + h) * 5 + jt
                        if idx >= 2:
                            t.wait_ge(s_msk, idx - 1)
                        ins = t.matmul(
                            out=pS2h[jt % 2],
                            lhsT=kTr[hb:hb + 64, (qt + jt) * 512 + hc:
                                     (qt + jt) * 512 + hc + P],
                            rhs=qTr[hb:hb + 64, qt * 512 + hc: qt * 512 + hc + P],
                            start=True, stop=True)
                        ins.then_inc(s_S2, 1)
                    if qt * 8 + h >= 1:
                        t.wait_ge(s_oev, qt * 8 + h)
                    for jt in range(5):
                        idx = (qt * 8 + h) * 5 + jt
                        t.wait_ge(s_exp, idx + 1)
                        t.matmul(out=po2[:, 0:64],
                                 lhsT=slb[:, jt * P:(jt + 1) * P],
                                 rhs=vtok[:, (qt + jt) * 512 + h * 64:
                                          (qt + jt) * 512 + h * 64 + 64],
                                 start=(jt == 0), stop=False)
                        ins = t.matmul(out=po2[:, 64:65],
                                       lhsT=slb[:, jt * P:(jt + 1) * P],
                                       rhs=onecb[:, 0:1],
                                       start=False, stop=(jt == 4))
                    ins.then_inc(s_pv, 1)
                t.wait_ge(s_oev, qt * 8 + 8)
                if qt >= 1:
                    t.wait_ge(s_otc, qt)
                for bc in range(4):
                    ins = t.transpose(out=ptp1[:, bc * P:(bc + 1) * P],
                                      in_=o_bf[:, bc * P:(bc + 1) * P],
                                      identity=idn[:])
                ins.then_inc(s_otp, 1)
                t.wait_ge(s_otc, qt + 1)
                if qt >= 1:
                    t.wait_ge(s_xft, qt)
                for dc in range(4):
                    for nh in range(2):
                        ins = t.matmul(
                            out=pwo2[:, nh * 512:(nh + 1) * 512],
                            lhsT=oT[:, dc * P:(dc + 1) * P],
                            rhs=wo_b[:, dc * D + nh * 512: dc * D + (nh + 1) * 512],
                            start=(dc == 0), stop=(dc == 3))
                ins.then_inc(s_wo2, 1)
                t.wait_ge(s_xft, qt + 1)
                if qt >= 1:
                    t.wait_ge(s_xtc, qt)
                for dc in range(8):
                    ins = t.transpose(out=prot[:, dc * P:(dc + 1) * P],
                                      in_=xf1bf[:, dc * P:(dc + 1) * P],
                                      identity=idn[:])
                ins.then_inc(s_xtp, 1)

        @block.vector
        def _(v):
            p1gate(v)
            for qt in range(NQ_B):
                v.wait_ge(s_mk[qt % 2], 16 * (qt // 2 + 1))
                for h in range(8):
                    for jt in range(5):
                        idx = (qt * 8 + h) * 5 + jt
                        v.wait_ge(s_S2, idx + 1)
                        if idx >= 2:
                            v.wait_ge(s_exp, idx - 1)
                        v.tensor_add(out=slf[:, (jt % 2) * P:(jt % 2 + 1) * P],
                                     in0=pS2h[jt % 2],
                                     in1=mkb[:, (qt % 2) * 640 + jt * P:
                                             (qt % 2) * 640 + (jt + 1) * P]
                                     ).then_inc(s_msk, 1)
                        if debug and qt == 3 and h == 0:
                            v.tensor_copy(out=dbg_slf[:, jt * P:(jt + 1) * P],
                                          in_=slf[:, (jt % 2) * P:(jt % 2 + 1) * P])
                    v.wait_ge(s_pv, qt * 8 + h + 1)
                    rsync(v, v.tensor_copy(out=co(B_DEN), in_=po2[:, 64:65]))
                    rsync(v, v.reciprocal(out=co(B_DEN), in_=co(B_DEN)))
                    v.tensor_scalar(out=o_bf[:, h * 64:(h + 1) * 64],
                                    in0=po2[:, 0:64], scalar1=co(B_DEN),
                                    scalar2=None, op0=ALU.mult).then_inc(s_oev, 1)
                    if debug:
                        v.tensor_copy(out=dbg_den[:, qt * 8 + h:qt * 8 + h + 1],
                                      in_=co(B_DEN))
                        v.tensor_copy(out=dbg_obf[:, qt * 512 + h * 64:
                                                  qt * 512 + (h + 1) * 64],
                                      in_=o_bf[:, h * 64:(h + 1) * 64])
                v.wait_ge(s_otp, qt + 1)
                for bc in range(4):
                    ins = v.tensor_copy(out=oT[:, bc * P:(bc + 1) * P],
                                        in_=ptp1[:, bc * P:(bc + 1) * P])
                ins.then_inc(s_otc, 1)
                xb = (qt % 2) * D
                v.wait_ge(s_ld2[qt % 2], 16 * (qt // 2 + 1))
                v.wait_ge(s_wo2, qt + 1)
                if qt >= 2:
                    v.wait_ge(s_xfs[qt % 2], 16 * ((qt - 2) // 2 + 1))
                yt2 = ybuf2[:, xb:xb + D]
                dst = xf1t[:, xb:xb + D]
                if apply_ln:
                    g_ = qt + 4
                    v.tensor_scalar(out=dst, in0=yt2,
                                    scalar1=statml[:, g_ * 2:g_ * 2 + 1],
                                    scalar2=statml[:, g_ * 2 + 1:g_ * 2 + 2],
                                    op0=ALU.subtract, op1=ALU.mult)
                    v.tensor_add(out=dst, in0=dst, in1=pwo2[:])
                else:
                    v.tensor_add(out=dst, in0=yt2, in1=pwo2[:])
                v.tensor_mul(out=sqb2[:], in0=dst, in1=dst)
                v.reduce_sum(out=statc[:, qt * 4:qt * 4 + 1], in_=dst[:, 0:512], axis=AX)
                v.reduce_sum(out=statc[:, qt * 4 + 1:qt * 4 + 2], in_=dst[:, 512:1024], axis=AX)
                v.reduce_sum(out=statc[:, qt * 4 + 2:qt * 4 + 3], in_=sqb2[:, 0:512], axis=AX)
                v.reduce_sum(out=statc[:, qt * 4 + 3:qt * 4 + 4], in_=sqb2[:, 512:1024], axis=AX)
                if debug:
                    v.tensor_copy(out=dbg_xft[:, qt * D:(qt + 1) * D], in_=dst)
                v.tensor_copy(out=xf1bf[:], in_=dst).then_inc(s_xft, 1)
                v.wait_ge(s_xtp, qt + 1)
                v.tensor_copy(
                    out=xf1T[:, :].rearrange("p (dc t) -> p dc t", dc=8)[
                        :, :, qt * P:(qt + 1) * P],
                    in_=prot[:, :].rearrange("p (dc t) -> p dc t", dc=8)
                ).then_inc(s_xtc, 1)

        @block.scalar
        def _(s):
            p1gate(s)
            for qt in range(NQ_B):
                for h in range(8):
                    hh = qt * 8 + h
                    if hh >= 1:
                        s.wait_ge(s_pv, hh)
                    for jt in range(5):
                        idx = hh * 5 + jt
                        s.wait_ge(s_msk, idx + 1)
                        s.activation(out=slb[:, jt * P:(jt + 1) * P],
                                     in_=slf[:, (jt % 2) * P:(jt % 2 + 1) * P],
                                     func=AF.Exp).then_inc(s_exp, 1)

        if upto <= 2:
            @block.sync
            def _(sy):
                sy.wait_ge(s_xtc, NQ_B)
                sy.wait_ge(s_xfs[0], 16 * 9)
                sy.wait_ge(s_xfs[1], 16 * 8)
                n = 0
                if debug:
                    for nm2, buf2 in (("dB_kTr", kTr), ("dB_qTr", qTr),
                                      ("dB_vtok", vtok)):
                        sy.dma_start(out=dbg[nm2], in_=buf2[:]).then_inc(s_dbg, 16)
                    n += 48
                    sy.dma_start(out=dbg["dB_xf1"], in_=xf1T[:]).then_inc(s_dbg, 16)
                    sy.dma_start(out=dbg["dB_obf"], in_=dbg_obf[:]).then_inc(s_dbg, 16)
                    sy.dma_start(out=dbg["dB_den"], in_=dbg_den[:]).then_inc(s_dbg, 16)
                    sy.dma_start(out=dbg["dB_xft"], in_=dbg_xft[:]).then_inc(s_dbg, 16)
                    sy.dma_start(out=dbg["dB_slf"], in_=dbg_slf[:]).then_inc(s_dbg, 16)
                    n += 64
                sy.dma_start(out=yout[0:P, :], in_=xf1t[:, 0:D]).then_inc(s_dbg, 16)
                sy.wait_ge(s_dbg, n + 16)
            es2.close()
            esR.close()
            es.close()
            return nc
        s_sh = sem("sB_sh")
        s_sv2 = sem("sB_sv2")
        s_ss2 = sem("sB_ss2")
        s_sr2 = sem("sB_sr2")
        s_sc2 = sem("sB_sc2")

        def _st_tp(t, f):
            t.wait_ge(s_sr2, f + 1)
            if f >= 1:
                t.wait_ge(s_sc2, f)
            t.transpose(out=prot[0:1, 0:P], in_=rw2bf[:, 0:1], identity=idn[:])
            ins = t.transpose(out=prot[0:1, P:2 * P], in_=rw2bf[:, 1:2],
                              identity=idn[:])
            ins.then_inc(s_otp, 1)

        @block.tensor
        def _(t):
            t.wait_ge(s_xtc, NQ_B)
            t.wait_ge(s_xfs[0], 16 * 9)
            t.wait_ge(s_xfs[1], 16 * 8)
            pst3 = pbig[:, 512:516]
            for f in range(NF_B):
                if f >= 1:
                    t.wait_ge(s_sv2, f)
                t.matmul(out=pst3, lhsT=msh[:], rhs=statc[:, (f + 1) * 4:(f + 2) * 4],
                         start=True, stop=False)
                t.matmul(out=pst3, lhsT=mla[:], rhs=statc[:, f * 4:(f + 1) * 4],
                         start=False, stop=True).then_inc(s_sh, 1)
                if f >= 1:
                    _st_tp(t, f - 1)
            _st_tp(t, NF_B - 1)

        @block.vector
        def _(v):
            for f in range(NF_B):
                if f >= 1:
                    v.wait_ge(s_otp, NQ_B + f)
                    v.tensor_copy(out=rows2a[0:1, (f - 1) * P:f * P],
                                  in_=prot[0:1, 0:P])
                    v.tensor_copy(out=rows2b[0:1, (f - 1) * P:f * P],
                                  in_=prot[0:1, P:2 * P]).then_inc(s_sc2, 1)
                v.wait_ge(s_sh, f + 1)
                pst3 = pbig[:, 512:516]
                rsync(v, v.tensor_add(out=co(B_MU2),
                                      in0=statc[:, (f + 1) * 4:(f + 1) * 4 + 1],
                                      in1=pst3[:, 1:2]))
                rsync(v, v.tensor_add(out=co(B_EX2),
                                      in0=statc[:, (f + 1) * 4 + 2:(f + 1) * 4 + 3],
                                      in1=pst3[:, 3:4]))
                rsync(v, v.tensor_scalar(out=co(B_MU2), in0=co(B_MU2),
                                         scalar1=1.0 / D, scalar2=None, op0=ALU.mult))
                rsync(v, v.tensor_mul(out=co(B_VAR2), in0=co(B_MU2), in1=co(B_MU2)))
                rsync(v, v.scalar_tensor_tensor(out=co(B_VAR2), in0=co(B_EX2),
                                                scalar=1.0 / D, in1=co(B_VAR2),
                                                op0=ALU.mult, op1=ALU.subtract))
                v.tensor_scalar(out=co(B_VAR2), in0=co(B_VAR2), scalar1=1e-5,
                                scalar2=None, op0=ALU.add).then_inc(s_sv2, 1)
                v.wait_ge(s_ss2, f + 1)
                rsync(v, v.reciprocal(out=co(B_R2), in_=co(B_STD2)))
                v.tensor_copy(out=r2col[:, f:f + 1], in_=co(B_R2))
                rsync(v, v.tensor_scalar(out=co(B_NMU2), in0=co(B_MU2),
                                         scalar1=-1.0, scalar2=None, op0=ALU.mult))
                v.tensor_copy(out=rw2bf[:, 0:1], in_=co(B_NMU2))
                v.tensor_copy(out=rw2bf[:, 1:2], in_=co(B_STD2)).then_inc(s_sr2, 1)
            v.wait_ge(s_otp, NQ_B + NF_B)
            v.tensor_copy(out=rows2a[0:1, (NF_B - 1) * P:NF_B * P],
                          in_=prot[0:1, 0:P])
            v.tensor_copy(out=rows2b[0:1, (NF_B - 1) * P:NF_B * P],
                          in_=prot[0:1, P:2 * P]).then_inc(s_sc2, 1)

        @block.scalar
        def _(s):
            for f in range(NF_B):
                s.wait_ge(s_sv2, f + 1)
                s.activation(out=co(B_STD2), in_=co(B_VAR2),
                             func=AF.Sqrt).then_inc(s_ss2, 1)

        if upto <= 3:
            @block.sync
            def _(sy):
                sy.wait_ge(s_sc2, NF_B)
                n = 0
                if debug:
                    sy.dma_start(out=dbg["dB_r2"], in_=r2col[:]).then_inc(s_dbg, 16)
                    sy.dma_start(out=dbg["dB_statc"], in_=statc[:]).then_inc(s_dbg, 16)
                    n += 32
                sy.dma_start(out=yout[0:P, :], in_=ybuf2[:, 0:D]).then_inc(s_dbg, 16)
                sy.wait_ge(s_dbg, n + 16)
            es2.close()
            esR.close()
            es.close()
            return nc
        es2.close()
        esR.close()
        es4 = ExitStack()
        sb4 = lambda nm, shp, dt=F32: es4.enter_context(nc.sbuf_tensor(nm, shp, dt))
        gw1a = sb4("gw1aB", [P, 8 * 512], BF16)
        gw1g = sb4("gw1gB", [P, 8 * 512], BF16)
        w2c = sb4("w2cB", [P, 4 * D], BF16)
        acc_sb = sb4("accB", [P, NF_B * D])
        a_sb = sb4("a_sbB", [P, 512])
        g_sb = sb4("g_sbB", [P, 512])
        gel = sb4("gelB", [P, 512])
        mid = sb4("midB", [P, 512], BF16)
        midT = sb4("midTB", [P, 512], BF16)
        browa = sb4("browaB", [1, 512], BF16)
        crowa = sb4("crowaB", [1, 512], BF16)
        browg = sb4("browgB", [1, 512], BF16)
        crowg = sb4("crowgB", [1, 512], BF16)
        crf = sb4("crfB", [1, 2 * 512])
        b1c = sb4("b1cB", [1, 2 * 512])
        b2rw = sb4("b2rwB", [1, D], BF16)
        onerw = sb4("onerwB", [1, P], BF16)
        xrr = sb4("xrrB", [P, 2 * D])
        youtb = sb4("youtbB", [P, 2 * D])

        psA = pbig[:, 0:512]
        psG = pbig[:, 512:1024]
        pw2 = pbig[:, 1024:2048]
        pbc = pbig[0:1, 1536:2048]

        s_w4 = sem("sB_w4")
        s_r2a = sem("sB_r2a")
        s_xg = sem("sB_xg")
        s_bc = sem("sB_bc")
        s_bcc = sem("sB_bcc")
        s_h1 = sem("sB_h1")
        s_ev4 = sem("sB_ev4")
        s_gel = sem("sB_gel")
        s_mid = sem("sB_mid")
        s_mtp = sem("sB_mtp")
        s_mtc = sem("sB_mtc")
        s_w2c = sem("sB_w2c")
        s_acc = sem("sB_acc")
        s_xrl = [sem("sB_xrl0"), sem("sB_xrl1")]
        s_yst = [sem("sB_yst0"), sem("sB_yst1")]
        s_fin = sem("sB_fin")

        NCI = 6
        CW = FCW

        def coff(ci):
            return sum(CW[:ci])

        def p3gate(e):
            e.wait_ge(s_sc2, NF_B)
            e.wait_ge(s_xtc, NQ_B)

        @block.gpsimd
        def _(g):
            p3gate(g)
            g.dma_start(out=b2rw[:], in_=b2r).then_inc(s_w4, 16)
            for ci in range(NCI):
                w = CW[ci]
                if ci >= 1:
                    g.wait_ge(s_h1, NF_B * ci)
                    g.wait_ge(s_w2c, NF_B * ci)
                    g.wait_ge(s_bcc, 4 * ci)
                co_ = coff(ci)
                g.dma_start(out=gw1a[:, :].rearrange("p (dc w) -> p dc w", dc=8)[:, :, 0:w],
                            in_=w1p[:, co_:co_ + w].rearrange("(dc p) w -> p dc w", p=P)
                            ).then_inc(s_w4, 16)
                g.dma_start(out=gw1g[:, :].rearrange("p (dc w) -> p dc w", dc=8)[:, :, 0:w],
                            in_=w1p[:, HID2 + co_:HID2 + co_ + w].rearrange(
                                "(dc p) w -> p dc w", p=P)).then_inc(s_w4, 16)
                g.dma_start(out=w2c[:, :].rearrange("p (fs d) -> p fs d", fs=4)[
                                :, 0:w // P, :],
                            in_=w2p[co_:co_ + w, :].rearrange("(fs p) d -> p fs d", p=P)
                            ).then_inc(s_w4, 16)
                g.dma_start(out=b1c[0:1, 0:w], in_=b1p[0:1, co_:co_ + w]
                            ).then_inc(s_w4, 16)
                g.dma_start(out=b1c[0:1, 512:512 + w],
                            in_=b1p[0:1, HID2 + co_:HID2 + co_ + w]).then_inc(s_w4, 16)

        @block.vector
        def _(v):
            p3gate(v)
            v.memset(onerw[:], 1.0)
            for dc in range(8):
                ins = v.tensor_scalar(
                    out=xf1T[:, dc * (NQ_B * P):(dc + 1) * (NQ_B * P)],
                    in0=xf1T[:, dc * (NQ_B * P):(dc + 1) * (NQ_B * P)],
                    scalar1=lngb[:, dc:dc + 1], scalar2=None, op0=ALU.mult)
            ins.then_inc(s_xg, 1)
            for ci in range(NCI):
                w = CW[ci]
                for st, (dst, addb, boff) in enumerate(
                        ((browa, False, 0), (crowa, True, 0),
                         (browg, False, 512), (crowg, True, 512))):
                    v.wait_ge(s_bc, ci * 4 + st + 1)
                    if addb:
                        v.tensor_add(out=crf[0:1, 0:w], in0=pbc[0:1, 0:w],
                                     in1=b1c[0:1, boff:boff + w])
                        v.tensor_copy(out=dst[0:1, 0:w],
                                      in_=crf[0:1, 0:w]).then_inc(s_bcc, 1)
                    else:
                        v.tensor_copy(out=dst[0:1, 0:w],
                                      in_=pbc[0:1, 0:w]).then_inc(s_bcc, 1)
                for f in range(NF_B):
                    it = ci * NF_B + f
                    v.wait_ge(s_h1, it + 1)
                    v.tensor_scalar(out=a_sb[:, 0:w], in0=psA[:, 0:w],
                                    scalar1=r2col[:, f:f + 1], scalar2=None,
                                    op0=ALU.mult)
                    v.tensor_scalar(out=g_sb[:, 0:w], in0=psG[:, 0:w],
                                    scalar1=r2col[:, f:f + 1], scalar2=None,
                                    op0=ALU.mult).then_inc(s_ev4, 1)
                    v.wait_ge(s_gel, it + 1)
                    v.tensor_mul(out=mid[:, 0:w], in0=a_sb[:, 0:w],
                                 in1=gel[:, 0:w]).then_inc(s_mid, 1)
                    v.wait_ge(s_mtp, it + 1)
                    for bc in range(w // P):
                        ins = v.tensor_copy(out=midT[:, bc * P:(bc + 1) * P],
                                            in_=ptp1[:, bc * P:(bc + 1) * P])
                    ins.then_inc(s_mtc, 1)
                    v.wait_ge(s_w2c, it + 1)
                    acc = acc_sb[:, f * D:(f + 1) * D]
                    if ci == 0:
                        v.tensor_copy(out=acc, in_=pw2[:]).then_inc(s_acc, 1)
                    else:
                        v.tensor_add(out=acc, in0=acc, in1=pw2[:]).then_inc(s_acc, 1)
            for f in range(NF_B):
                v.wait_ge(s_xrl[f % 2], 16 * (f // 2 + 1))
                if f >= 2:
                    v.wait_ge(s_yst[f % 2], 16 * (f // 2))
                v.tensor_add(out=youtb[:, (f % 2) * D:(f % 2 + 1) * D],
                             in0=acc_sb[:, f * D:(f + 1) * D],
                             in1=xrr[:, (f % 2) * D:(f % 2 + 1) * D]
                             ).then_inc(s_fin, 1)

        @block.tensor
        def _(t):
            p3gate(t)
            t.wait_ge(s_xg, 1)
            for ci in range(NCI):
                w = CW[ci]
                t.wait_ge(s_w4, 16 + (ci + 1) * 80)
                if ci >= 1:
                    t.wait_ge(s_acc, NF_B * ci)
                for st, (lcol, srcb) in enumerate(
                        ((gcolb, gw1a), (lcolb, gw1a), (gcolb, gw1g), (lcolb, gw1g))):
                    if ci * 4 + st >= 1:
                        t.wait_ge(s_bcc, ci * 4 + st)
                    for dc in range(8):
                        ins = t.matmul(out=pbc[0:1, 0:w],
                                       lhsT=lcol[:, dc:dc + 1],
                                       rhs=srcb[:, dc * 512:dc * 512 + w],
                                       start=(dc == 0), stop=(dc == 7))
                    ins.then_inc(s_bc, 1)
                t.wait_ge(s_bcc, ci * 4 + 4)
                for f in range(NF_B):
                    it = ci * NF_B + f
                    if it >= 1:
                        t.wait_ge(s_ev4, it)
                    tok = (f + 1) * P
                    for half, (ps, brow, crow, gsrc) in enumerate(
                            ((psA, browa, crowa, gw1a), (psG, browg, crowg, gw1g))):
                        for dc in range(8):
                            off = tok if dc < 4 else tok - 1
                            t.matmul(out=ps[:, 0:w],
                                     lhsT=xf1T[:, dc * (NQ_B * P) + off:
                                               dc * (NQ_B * P) + off + P],
                                     rhs=gsrc[:, dc * 512:dc * 512 + w],
                                     start=(dc == 0), stop=False)
                        t.matmul(out=ps[:, 0:w],
                                 lhsT=rows2a[0:1, f * P:(f + 1) * P],
                                 rhs=brow[0:1, 0:w], start=False, stop=False)
                        ins = t.matmul(out=ps[:, 0:w],
                                       lhsT=rows2b[0:1, f * P:(f + 1) * P],
                                       rhs=crow[0:1, 0:w], start=False, stop=True)
                    ins.then_inc(s_h1, 1)
                    t.wait_ge(s_mid, it + 1)
                    if it >= 1:
                        t.wait_ge(s_mtc, it)
                    for fs in range(w // P):
                        ins = t.transpose(out=ptp1[:, fs * P:(fs + 1) * P],
                                          in_=mid[:, fs * P:(fs + 1) * P],
                                          identity=idn[:])
                    ins.then_inc(s_mtp, 1)
                    t.wait_ge(s_mtc, it + 1)
                    if it >= 1:
                        t.wait_ge(s_acc, it)
                    nfs = w // P
                    for fs in range(nfs):
                        for nh in range(2):
                            ins = t.matmul(
                                out=pw2[:, nh * 512:(nh + 1) * 512],
                                lhsT=midT[:, fs * P:(fs + 1) * P],
                                rhs=w2c[:, fs * D + nh * 512: fs * D + (nh + 1) * 512],
                                start=(fs == 0),
                                stop=(fs == nfs - 1 and ci != NCI - 1))
                    if ci == NCI - 1:
                        for nh in range(2):
                            ins = t.matmul(out=pw2[:, nh * 512:(nh + 1) * 512],
                                           lhsT=onerw[0:1, :],
                                           rhs=b2rw[0:1, nh * 512:(nh + 1) * 512],
                                           start=False, stop=True)
                    ins.then_inc(s_w2c, 1)

        @block.scalar
        def _(s):
            p3gate(s)
            for ci in range(NCI):
                w = CW[ci]
                for f in range(NF_B):
                    it = ci * NF_B + f
                    s.wait_ge(s_ev4, it + 1)
                    s.activation(out=gel[:, 0:w], in_=g_sb[:, 0:w],
                                 func=AF.Gelu).then_inc(s_gel, 1)

        @block.sync
        def _(sy):
            p3gate(sy)
            for f in range(NF_B):
                if f >= 2:
                    sy.wait_ge(s_fin, f - 1)
                sy.dma_start(out=xrr[:, (f % 2) * D:(f % 2 + 1) * D],
                             in_=xf1scr[(f + 1) * P:(f + 2) * P, :]
                             ).then_inc(s_xrl[f % 2], 16)
                sy.wait_ge(s_fin, f + 1)
                sy.dma_start(out=yout[f * P:(f + 1) * P, :],
                             in_=youtb[:, (f % 2) * D:(f % 2 + 1) * D]
                             ).then_inc(s_yst[f % 2], 16)
            sy.wait_ge(s_yst[0], 16 * 8)
            sy.wait_ge(s_yst[1], 16 * 8)

        es4.close()
    es.close()
    return nc


def _l0ffn_host_prep(l, inputs):
    w1 = inputs["L0_ffn_w1"][l]
    b1 = inputs["L0_ffn_b1"][l]
    w2 = inputs["L0_ffn_w2"][l]
    HID = 2730
    w1p = np.zeros((D, W1W), np.float32)
    w1p[:, 0:HID] = w1[:, 0:HID]
    w1p[:, HID2:HID2 + HID] = w1[:, HID:]
    b1p = np.zeros((1, W1W), np.float32)
    b1p[0, 0:HID] = b1[0:HID]
    b1p[0, HID2:HID2 + HID] = b1[HID:]
    w2p = np.zeros((HID2, D), np.float32)
    w2p[0:HID] = w2
    lng = inputs["L0_ffn_ln_g"][l].reshape(8, P).T.copy()
    lnb = inputs["L0_ffn_ln_b"][l].reshape(8, P).T.copy()
    com = {
        "wqkv": np.ascontiguousarray(inputs["L0_ffn_wqkv"][l]),
        "wo_": np.ascontiguousarray(inputs["L0_ffn_wo"][l]),
        "lngc": np.ascontiguousarray(lng), "lnbc": np.ascontiguousarray(lnb),
        "qs": inputs["L0_ffn_q_scale"][l].reshape(1, 64).astype(np.float32),
        "ks": inputs["L0_ffn_k_scale"][l].reshape(1, 64).astype(np.float32),
        "w1p": w1p, "b1p": b1p, "w2p": w2p,
        "b2r": inputs["L0_ffn_b2"][l].reshape(1, D).astype(np.float32),
        "idn": np.eye(P, dtype=BF16NP),
        "mshift": np.eye(P, k=1, dtype=np.float32),
        "mlast": np.zeros((P, P), np.float32),
        "onesr": np.ones((1, P), BF16NP),
    }
    com["mlast"][127, 0] = 1.0
    return com


def _l0ffn_tabs(half):
    t = (half * TOWN - 640 + np.arange(2688)).astype(np.float64)
    hi = np.arange(0, 64, 2).astype(np.float64)
    inv = 1.0 / 10000.0 ** (hi / 64.0)
    fr = t[:, None] * inv[None, :]
    fr = np.concatenate([fr, fr], 1)
    base = (hi + 0.4 * 64) / (1.4 * 64)
    power = (t - 2048.0) / 256.0
    sc = base[None, :] ** power[:, None]
    sc = np.concatenate([sc, sc], 1)
    return {"csq": (np.cos(fr) * sc).astype(np.float32),
            "ssq": (np.sin(fr) * sc).astype(np.float32),
            "csk": (np.cos(fr) / sc).astype(np.float32),
            "ssk": (np.sin(fr) / sc).astype(np.float32)}


def _l0ffn_mask(half):
    m = np.full((NQ_B, 5 * P, P), -30000.0, np.float32)
    for mt in range(NQ_B):
        jj = np.arange(5 * P)[:, None]
        i = np.arange(P)[None, :]
        delta = 512 + i - jj
        ok = (delta >= 0) & (delta <= 512)
        if half == 0:
            ok &= (mt * P + jj) >= 640
        m[mt][ok] = 0.0
    if half == 0:
        m[0][:] = 0.0
    return m


def run_l0ffn(l, ystream, inputs):
    nc = build_l0ffn(apply_ln=(l == 0) or None)
    raise RuntimeError


def prep_l0ffn_maps(l, apply_ln, ystream, inputs):
    com = _l0ffn_host_prep(l, inputs)
    tabs = [_l0ffn_tabs(0), _l0ffn_tabs(1)]
    masks = [_l0ffn_mask(0), _l0ffn_mask(1)]
    maps = []
    for c in range(8):
        b, h = c // 2, c % 2
        yhv = np.zeros((2688, D), np.float32)
        lo = h * TOWN - 640
        src_lo = max(lo, 0)
        yhv[src_lo - lo:] = ystream[b, src_lo:h * TOWN + TOWN]
        mp = dict(com)
        mp["yh"] = yhv
        mp.update(tabs[h])
        mp["maskT"] = masks[h]
        maps.append(mp)
    return maps


def run_l0ffn_layer(l, ystream, inputs):
    nc = build_l0ffn(apply_ln=(l == 0))
    maps = prep_l0ffn_maps(l, l == 0, ystream, inputs)
    res = run_bass_kernel_spmd(nc, maps, core_ids=list(range(8)))
    out = np.empty((4, S, D), np.float32)
    for c in range(8):
        b, h = c // 2, c % 2
        out[b, h * TOWN:(h + 1) * TOWN] = res.results[c]["yout"]
    return out



NHC = 16


def build_l1attn():
    nc = bass.Bass("TRN2", target_bir_lowering=False)
    x1f = nc.dram_tensor("x1f", [S, D], F32, kind="ExternalInput").ap()
    fff = nc.dram_tensor("fff", [S, D], F32, kind="ExternalInput").ap()
    x1o_ = nc.dram_tensor("x1o_", [TOWN, D], F32, kind="ExternalInput").ap()
    ffo_ = nc.dram_tensor("ffo_", [TOWN, D], F32, kind="ExternalInput").ap()
    wq_d = nc.dram_tensor("wq", [D, D], F32, kind="ExternalInput").ap()
    wk_d = nc.dram_tensor("wk", [D, D], F32, kind="ExternalInput").ap()
    wv_d = nc.dram_tensor("wv", [D, D], F32, kind="ExternalInput").ap()
    wo_d = nc.dram_tensor("wo", [D, D], F32, kind="ExternalInput").ap()
    g1c_d = nc.dram_tensor("g1c", [P, 8], F32, kind="ExternalInput").ap()
    b1c_d = nc.dram_tensor("b1c", [P, 8], F32, kind="ExternalInput").ap()
    idn_d = nc.dram_tensor("idn", [P, P], BF16, kind="ExternalInput").ap()
    onesr_d = nc.dram_tensor("onesr", [1, 512], BF16, kind="ExternalInput").ap()
    x3o = nc.dram_tensor("x3o", [TOWN, D], F32, kind="ExternalOutput").ap()
    x2scr = nc.dram_tensor("x2scr", [TOWN, D], F32, kind="Internal").ap()
    vscrC = nc.dram_tensor("vscrC", [S, D], BF16, kind="Internal").ap()

    es = ExitStack()
    sbt = lambda nm, shp, dt=F32: es.enter_context(nc.sbuf_tensor(nm, shp, dt))
    sem = lambda nm: es.enter_context(nc.semaphore(nm))

    idn = sbt("idnC", [P, P], BF16)
    onesr = sbt("onesrC", [1, 512], BF16)
    onescl = sbt("onesclC", [P, 1], BF16)
    g1 = sbt("g1C", [P, 8])
    b1 = sbt("b1C", [P, 8])
    bcol = sbt("bcolC", [P, 8], BF16)
    cols = sbt("colsC", [P, 32])
    co = lambda i: cols[:, i:i + 1]
    (K_S1, K_S2, K_SB, K_MEAN, K_VAR, K_STD, K_RSTD, K_M1, K_R1, K_M2, K_R2,
     K_OS1, K_OS2, K_OM, K_OV, K_OSD, K_OR, K_MR, K_DEN) = range(19)
    brq = sbt("brqC", [1, D], BF16)
    brk = sbt("brkC", [1, D], BF16)
    brv = sbt("brvC", [1, D], BF16)
    aoT = sbt("aoTC", [P, 8 * TOWN], BF16)
    denr = sbt("denrC", [1, 512], BF16)

    esK = ExitStack()
    kT = esK.enter_context(nc.sbuf_tensor("kTC", [P, 8 * S], BF16))
    qT = esK.enter_context(nc.sbuf_tensor("qTC", [P, 8 * TOWN], BF16))

    pbig = es.enter_context(nc.psum_tensor("pbigC", [P, 2048], F32))
    pex = es.enter_context(nc.psum_tensor("pexC", [P, 512], F32))
    ptp = es.enter_context(nc.psum_tensor("ptpC", [P, D], BF16))
    pden = es.enter_context(nc.psum_tensor("pdenC", [1, 512], F32))

    s_w = sem("sC_w")
    s_ld = [sem("sC_ld0"), sem("sC_ld1")]
    s_fld = [sem("sC_f0"), sem("sC_f1")]
    s_var = sem("sC_var")
    s_std = sem("sC_std")
    s_rt = sem("sC_rt")
    s_app = sem("sC_app")
    s_tp = sem("sC_tp")
    s_tpc = sem("sC_tpc")
    s_mm = sem("sC_mm")
    s_kev = sem("sC_kev")
    s_wsc = sem("sC_wsc")
    s_bc = sem("sC_bc")
    s_bcc = sem("sC_bcc")
    s_x2s = [sem("sC_x2s0"), sem("sC_x2s1")]
    s_vstC = [sem("sC_vst0"), sem("sC_vst1")]
    rt_cnt = [0]

    def mmkev_v(ch, tt):
        return ch * 12 + 8 + tt + 1

    rsync = _mk_rsync(s_rt, rt_cnt)

    esA = ExitStack()
    sbA = lambda nm, shp, dt=F32: esA.enter_context(nc.sbuf_tensor(nm, shp, dt))
    wkb = sbA("wkbC", [P, 8 * D], BF16)
    wvb = sbA("wvbC", [P, 8 * D], BF16)
    wqb = wkb
    xbuf = sbA("xbufC", [P, 2 * D])
    fbuf = sbA("fbufC", [P, 2 * D])
    x2t = sbA("x2tC", [P, D])
    sqc = sbA("sqcC", [P, 512])
    a_bf = sbA("a_bfC", [P, 2 * D], BF16)
    aT = sbA("aTC", [P, 8 * 512], BF16)
    vbufC = sbA("vbufC", [P, 2 * D], BF16)

    pk = [pbig[:, 0:512], pbig[:, 512:1024]]
    pv = pbig[:, 1024:2048]

    NG = 48

    def aT_view(ch):
        return aT[:, :].rearrange("p (dc t) -> p dc t", dc=8)

    def mm_base(ch):
        return ch * 13 if ch < 8 else 104 + (ch - 8) * 9

    def mm_base(ch):
        return ch * 12 if ch <= 8 else 96 + (ch - 8) * 8

    with nc.Block() as block:
        @block.gpsimd
        def _(g):
            g.dma_start(out=idn[:], in_=idn_d).then_inc(s_w, 16)
            g.dma_start(out=onesr[:], in_=onesr_d).then_inc(s_w, 16)
            g.dma_start(out=g1[:], in_=g1c_d).then_inc(s_w, 16)
            g.dma_start(out=b1[:], in_=b1c_d).then_inc(s_w, 16)
            for wsrc, wdst in ((wk_d, wkb), (wv_d, wvb)):
                g.dma_start(out=wdst[:, :].rearrange("p (dc d) -> p dc d", dc=8),
                            in_=wsrc.rearrange("(dc p) d -> p dc d", p=P)
                            ).then_inc(s_w, 16)
            g.wait_ge(s_mm, 96)
            g.dma_start(out=wkb[:, :].rearrange("p (dc d) -> p dc d", dc=8),
                        in_=wq_d.rearrange("(dc p) d -> p dc d", p=P)
                        ).then_inc(s_w, 16)

        @block.vector
        def _(v):
            v.wait_ge(s_w, 96)
            v.memset(onescl[:], 1.0)
            v.tensor_copy(out=bcol[:], in_=b1[:]).then_inc(s_wsc, 1)
            for wi, (wdst, dst) in enumerate(((wkb, brk), (wvb, brv))):
                for nh in range(2):
                    v.wait_ge(s_bc, wi * 2 + nh + 1)
                    v.tensor_copy(out=dst[0:1, nh * 512:(nh + 1) * 512],
                                  in_=pden[0:1, 0:512]).then_inc(s_bcc, 1)
                for dc in range(8):
                    ins = v.tensor_scalar(
                        out=wdst[:, dc * D:(dc + 1) * D],
                        in0=wdst[:, dc * D:(dc + 1) * D],
                        scalar1=g1[:, dc:dc + 1], scalar2=1.0,
                        op0=ALU.mult, op1=ALU.mult)
                ins.then_inc(s_wsc, 1)

        @block.tensor
        def _(t):
            t.wait_ge(s_wsc, 1)
            for wi, wdst in enumerate((wkb, wvb)):
                for nh in range(2):
                    for dc in range(8):
                        ins = t.matmul(
                            out=pden[0:1, 0:512],
                            lhsT=bcol[:, dc:dc + 1],
                            rhs=wdst[:, dc * D + nh * 512:dc * D + (nh + 1) * 512],
                            start=(dc == 0), stop=(dc == 7))
                    ins.then_inc(s_bc, 1)
                    t.wait_ge(s_bcc, wi * 2 + nh + 1)

        @block.sync
        def _(sy):
            for g in range(NG):
                if g >= 2:
                    sy.wait_ge(s_app, g - 1)
                if g < 32:
                    sy.dma_start(out=xbuf[:, (g % 2) * D:(g % 2 + 1) * D],
                                 in_=x1f[g * P:(g + 1) * P, :]).then_inc(s_ld[g % 2], 16)
                    sy.dma_start(out=fbuf[:, (g % 2) * D:(g % 2 + 1) * D],
                                 in_=fff[g * P:(g + 1) * P, :]).then_inc(s_fld[g % 2], 16)
                if g % 4 == 3 and g // 4 < 8 and g >= 3:
                    ch = g // 4
                    for tt in range(4):
                        vi = ch * 4 + tt
                        sy.wait_ge(s_kev, mmkev_v(ch, tt))
                        sy.dma_start(out=vscrC[vi * P:(vi + 1) * P, :],
                                     in_=vbufC[:, (vi % 2) * D:(vi % 2 + 1) * D]
                                     ).then_inc(s_vstC[vi % 2], 16)
                if g >= 32:
                    go = g - 32
                    sy.dma_start(out=xbuf[:, (g % 2) * D:(g % 2 + 1) * D],
                                 in_=x1o_[go * P:(go + 1) * P, :]).then_inc(s_ld[g % 2], 16)
                    sy.dma_start(out=fbuf[:, (g % 2) * D:(g % 2 + 1) * D],
                                 in_=ffo_[go * P:(go + 1) * P, :]).then_inc(s_fld[g % 2], 16)
                    sy.wait_ge(s_app, g + 1)
                    sy.dma_start(out=x2scr[go * P:(go + 1) * P, :],
                                 in_=x2t[:]).then_inc(s_x2s[go % 2], 16)

        def _lnc(v, xin, c_mean, c_rstd, g, phase):
            v.tensor_mul(out=sqc[:], in0=xin[:, 0:512], in1=xin[:, 0:512])
            v.reduce_sum(out=co(K_S2), in_=sqc[:], axis=AX)
            v.tensor_mul(out=sqc[:], in0=xin[:, 512:1024], in1=xin[:, 512:1024])
            v.reduce_sum(out=co(K_SB), in_=sqc[:], axis=AX)
            v.reduce_sum(out=co(K_S1), in_=xin, axis=AX)
            rsync(v, v.tensor_add(out=co(K_S2), in0=co(K_S2), in1=co(K_SB)))
            rsync(v, v.tensor_scalar(out=co(c_mean), in0=co(K_S1),
                                     scalar1=1.0 / D, scalar2=None, op0=ALU.mult))
            rsync(v, v.tensor_mul(out=co(K_VAR), in0=co(c_mean), in1=co(c_mean)))
            rsync(v, v.scalar_tensor_tensor(out=co(K_VAR), in0=co(K_S2),
                                            scalar=1.0 / D, in1=co(K_VAR),
                                            op0=ALU.mult, op1=ALU.subtract))
            v.tensor_scalar(out=co(K_VAR), in0=co(K_VAR), scalar1=1e-5,
                            scalar2=None, op0=ALU.add).then_inc(s_var, 1)
            v.wait_ge(s_std, 3 * g + phase)
            rsync(v, v.reciprocal(out=co(c_rstd), in_=co(K_STD)))

        @block.vector
        def _(v):
            for g in range(NG):
                b = (g % 2) * D
                xt = xbuf[:, b:b + D]
                ft = fbuf[:, b:b + D]
                if g >= 1:
                    gp = g - 1
                    v.wait_ge(s_tp, gp + 1)
                    ch, tt = gp // 4, gp % 4
                    if tt == 0 and ch >= 1:
                        v.wait_ge(s_mm, mm_base(ch))
                    v.tensor_copy(
                        out=aT_view(ch)[:, :, tt * P:(tt + 1) * P],
                        in_=ptp[:, :].rearrange("p (dc t) -> p dc t", dc=8)
                    ).then_inc(s_tpc, 1)
                if g == 36:
                    for nh in range(2):
                        v.wait_ge(s_bc, 4 + nh + 1)
                        v.tensor_scalar(out=brq[0:1, nh * 512:(nh + 1) * 512],
                                        in0=pden[0:1, 0:512],
                                        scalar1=float(1.0 / 8.0), scalar2=None,
                                        op0=ALU.mult).then_inc(s_bcc, 1)
                    sc2 = float(1.0 / 8.0)
                    for dc in range(8):
                        ins = v.tensor_scalar(out=wkb[:, dc * D:(dc + 1) * D],
                                              in0=wkb[:, dc * D:(dc + 1) * D],
                                              scalar1=g1[:, dc:dc + 1],
                                              scalar2=sc2, op0=ALU.mult,
                                              op1=ALU.mult)
                    ins.then_inc(s_wsc, 1)
                v.wait_ge(s_ld[g % 2], 16 * (g // 2 + 1))
                v.wait_ge(s_fld[g % 2], 16 * (g // 2 + 1))
                if g >= 33:
                    go1 = g - 33
                    v.wait_ge(s_x2s[go1 % 2], 16 * (go1 // 2 + 1))
                x2s = x2t[:]
                _lnc(v, ft, K_MEAN, K_RSTD, g, 1)
                v.tensor_scalar(out=x2s, in0=ft, scalar1=co(K_MEAN),
                                scalar2=co(K_RSTD), op0=ALU.subtract, op1=ALU.mult)
                v.tensor_add(out=x2s, in0=x2s, in1=xt)
                _lnc(v, x2s, K_M1, K_R1, g, 2)
                v.tensor_scalar(out=xt, in0=x2s, scalar1=co(K_M1),
                                scalar2=co(K_R1), op0=ALU.subtract, op1=ALU.mult)
                _lnc(v, xt, K_M2, K_R2, g, 3)
                v.tensor_scalar(out=a_bf[:, b:b + D], in0=xt,
                                scalar1=co(K_M2), scalar2=co(K_R2),
                                op0=ALU.subtract, op1=ALU.mult).then_inc(s_app, 1)
            gp = NG - 1
            v.wait_ge(s_tp, gp + 1)
            ch, tt = gp // 4, gp % 4
            v.tensor_copy(out=aT_view(ch)[:, :, tt * P:(tt + 1) * P],
                          in_=ptp[:, :].rearrange("p (dc t) -> p dc t", dc=8)
                          ).then_inc(s_tpc, 1)

        @block.scalar
        def _(s):
            for g in range(NG):
                for ph in range(3):
                    s.wait_ge(s_var, 3 * g + ph + 1)
                    s.activation(out=co(K_STD), in_=co(K_VAR),
                                 func=AF.Sqrt).then_inc(s_std, 1)
                if g % 4 == 3:
                    ch = g // 4
                    for mh in range(8):
                        s.wait_ge(s_mm, mm_base(ch) + mh + 1)
                        src = pk[mh % 2][:, 0:512]
                        if ch < 8:
                            dst = kT[:, mh * S + ch * 512: mh * S + ch * 512 + 512]
                        else:
                            chq = ch - 8
                            dst = qT[:, mh * TOWN + chq * 512: mh * TOWN + chq * 512 + 512]
                        s.activation(out=dst, in_=src, func=AF.Copy).then_inc(s_kev, 1)
                    if ch < 8:
                        for tt in range(4):
                            s.wait_ge(s_mm, mm_base(ch) + 8 + tt + 1)
                            vi = ch * 4 + tt
                            if vi >= 2:
                                s.wait_ge(s_vstC[vi % 2], 16 * ((vi - 2) // 2 + 1))
                            s.activation(out=vbufC[:, (vi % 2) * D:(vi % 2 + 1) * D],
                                         in_=pv[:], func=AF.Copy).then_inc(s_kev, 1)

        @block.tensor
        def _(t):
            t.wait_ge(s_w, 96)
            t.wait_ge(s_bcc, 4)
            t.wait_ge(s_wsc, 3)
            for g in range(NG):
                t.wait_ge(s_app, g + 1)
                if g >= 2:
                    t.wait_ge(s_tpc, g - 1)
                b = (g % 2) * D
                for dc in range(8):
                    ins = t.transpose(out=ptp[:, dc * P:(dc + 1) * P],
                                      in_=a_bf[:, b + dc * P: b + (dc + 1) * P],
                                      identity=idn[:])
                ins.then_inc(s_tp, 1)
                if g % 4 == 3:
                    ch = g // 4
                    if ch == 8:
                        t.wait_ge(s_w, 112)
                        for nh in range(2):
                            for dc in range(8):
                                ins = t.matmul(
                                    out=pden[0:1, 0:512],
                                    lhsT=bcol[:, dc:dc + 1],
                                    rhs=wkb[:, dc * D + nh * 512:
                                            dc * D + (nh + 1) * 512],
                                    start=(dc == 0), stop=(dc == 7))
                            ins.then_inc(s_bc, 1)
                            t.wait_ge(s_bcc, 4 + nh + 1)
                        t.wait_ge(s_wsc, 4)
                    t.wait_ge(s_tpc, 4 * (ch + 1))
                    aTc = aT_view(ch)
                    if ch < 8:
                        kev_base = ch * 12
                        for mh in range(8):
                            ei = kev_base + mh
                            if ei >= 2:
                                t.wait_ge(s_kev, ei - 1)
                            for dc in range(8):
                                t.matmul(out=pk[mh % 2][:, 0:512],
                                         lhsT=wkb[:, dc * D + mh * P: dc * D + (mh + 1) * P],
                                         rhs=aTc[:, dc, :],
                                         start=(dc == 0), stop=False)
                            ins = t.matmul(out=pk[mh % 2][:, 0:512],
                                           lhsT=brk[0:1, mh * P:(mh + 1) * P],
                                           rhs=onesr[0:1, 0:512],
                                           start=False, stop=True)
                            ins.then_inc(s_mm, 1)
                        for tt in range(4):
                            vi = ch * 4 + tt
                            if vi >= 1:
                                t.wait_ge(s_kev, kev_base + 8 + tt)
                            for dc in range(8):
                                for nh in range(2):
                                    t.matmul(
                                        out=pv[:, nh * 512:(nh + 1) * 512],
                                        lhsT=aTc[:, dc, tt * P:(tt + 1) * P],
                                        rhs=wvb[:, dc * D + nh * 512: dc * D + (nh + 1) * 512],
                                        start=(dc == 0), stop=False)
                            for nh in range(2):
                                ins = t.matmul(out=pv[:, nh * 512:(nh + 1) * 512],
                                               lhsT=onesr[0:1, tt * P:(tt + 1) * P]
                                               if False else onesr[0:1, 0:P],
                                               rhs=brv[0:1, nh * 512:(nh + 1) * 512],
                                               start=False, stop=True)
                            ins.then_inc(s_mm, 1)
                    else:
                        for mh in range(8):
                            ei = 96 + (ch - 8) * 8 + mh
                            if ei >= 2:
                                t.wait_ge(s_kev, ei - 1)
                            for dc in range(8):
                                t.matmul(out=pk[mh % 2][:, 0:512],
                                         lhsT=wqb[:, dc * D + mh * P: dc * D + (mh + 1) * P],
                                         rhs=aTc[:, dc, :],
                                         start=(dc == 0), stop=False)
                            ins = t.matmul(out=pk[mh % 2][:, 0:512],
                                           lhsT=brq[0:1, mh * P:(mh + 1) * P],
                                           rhs=onesr[0:1, 0:512],
                                           start=False, stop=True)
                            ins.then_inc(s_mm, 1)

        esA.close()
        esB = ExitStack()
        sbB = lambda nm, shp, dt=F32: esB.enter_context(nc.sbuf_tensor(nm, shp, dt))
        vtk = esB.enter_context(nc.sbuf_tensor("vtkC", [P, 32 * 1040], BF16))
        slb = sbB("slbC", [P, 4 * 512], BF16)
        otb = sbB("otbC", [P, 512], BF16)
        s_vlC = sem("sC_vl")
        s_S = sem("sC_S")
        s_ex = sem("sC_ex")
        s_oc = sem("sC_oc")
        s_ot = sem("sC_ot")
        s_den = sem("sC_den")
        s_bca = sem("sC_bca")
        s_ao = sem("sC_ao")

        pS = [pex[:, 0:512], pbig[:, 0:512]]
        pOT = pbig[:, 512:1024]
        pbcast = pbig[:, 1024:1536]

        def gateB(e):
            e.wait_ge(s_kev, 128)
            e.wait_ge(s_tpc, NG)
            e.wait_ge(s_mm, 128)
            e.wait_ge(s_x2s[0], 16 * 8)
            e.wait_ge(s_x2s[1], 16 * 8)
            e.wait_ge(s_vstC[0], 16 * 16)
            e.wait_ge(s_vstC[1], 16 * 16)

        @block.gpsimd
        def _(g):
            gateB(g)
            g.memset(vtk[:], 1.0)
            for hh2 in range(16):
                g.dma_start(
                    out=vtk[:, :].rearrange("p (t h e) -> p t h e",
                                            t=32, h=16)[:, :, hh2, 0:64],
                    in_=vscrC.rearrange("(t p) (h d) -> p t h d",
                                        p=P, d=64)[:, :, hh2, :]
                    ).then_inc(s_vlC, 16)

        @block.tensor
        def _(t):
            gateB(t)
            t.wait_ge(s_vlC, 256)
            for h in range(NHC):
                hb = (h % 2) * 64
                hc = h // 2
                for qc in range(4):
                    ii = h * 4 + qc
                    for tt2 in range(34):
                        if tt2 < 32:
                            jc = tt2
                            gidx = ii * 32 + jc
                            if gidx >= 2:
                                t.wait_ge(s_ex, gidx - 1)
                            t.matmul(out=pS[jc % 2],
                                     lhsT=kT[hb:hb + 64, hc * S + jc * P: hc * S + (jc + 1) * P],
                                     rhs=qT[hb:hb + 64, hc * TOWN + qc * 512: hc * TOWN + (qc + 1) * 512],
                                     start=True, stop=True).then_inc(s_S, 1)
                        if tt2 == 2 and ii >= 1:
                            t.wait_ge(s_ao, ii)
                        if tt2 >= 2:
                            jc = tt2 - 2
                            gidx = ii * 32 + jc
                            t.wait_ge(s_ex, gidx + 1)
                            t.matmul(out=pOT[0:65, :],
                                     lhsT=vtk[:, jc * 1040 + h * 65:
                                              jc * 1040 + h * 65 + 65],
                                     rhs=slb[:, (jc % 4) * 512:(jc % 4 + 1) * 512],
                                     start=(jc == 0), stop=(jc == 31)
                                     ).then_inc(s_oc, 1)
                    t.wait_ge(s_den, ii + 1)
                    t.matmul(out=pbcast[0:64, :], lhsT=onesr[0:1, 0:64],
                             rhs=denr[0:1, :], start=True,
                             stop=True).then_inc(s_bca, 1)

        @block.scalar
        def _(s):
            gateB(s)
            for h in range(NHC):
                for qc in range(4):
                    ii = h * 4 + qc
                    for jc in range(32):
                        gidx = ii * 32 + jc
                        s.wait_ge(s_S, gidx + 1)
                        if gidx >= 4:
                            s.wait_ge(s_oc, gidx - 3)
                        s.activation(out=slb[:, (jc % 4) * 512:(jc % 4 + 1) * 512],
                                     in_=pS[jc % 2], func=AF.Exp).then_inc(s_ex, 1)
                    s.wait_ge(s_oc, (ii + 1) * 32)
                    s.activation(out=otb[0:64, :], in_=pOT[0:64, :],
                                 func=AF.Copy).then_inc(s_ot, 1)

        @block.vector
        def _(v):
            gateB(v)
            for h in range(NHC):
                hb = (h % 2) * 64
                hc = h // 2
                for qc in range(4):
                    ii = h * 4 + qc
                    v.wait_ge(s_oc, (ii + 1) * 32)
                    rsync(v, v.tensor_copy(out=denr[0:1, :],
                                           in_=pOT[64:65, :]))
                    with nc.allow_low_precision(reason="bf16 softmax denom recip feeding bf16 multiply"):
                        rsync(v, v.reciprocal(out=denr[0:1, :], in_=denr[0:1, :]))
                    v.nop().then_inc(s_den, 1)
                    v.wait_ge(s_bca, ii + 1)
                    v.wait_ge(s_ot, ii + 1)
                    v.tensor_tensor(
                        out=aoT[hb:hb + 64, hc * TOWN + qc * 512: hc * TOWN + (qc + 1) * 512],
                        in0=otb[0:64, :], in1=pbcast[0:64, :],
                        op=ALU.mult).then_inc(s_ao, 1)

        esB.close()
        esK.close()
        esC = ExitStack()
        sbC = lambda nm, shp, dt=F32: esC.enter_context(nc.sbuf_tensor(nm, shp, dt))
        wob = sbC("wobC", [P, 8 * D], BF16)
        x2r = sbC("x2rC", [P, 2 * D])
        outc = sbC("outcC", [P, 2 * D])
        sqd = sbC("sqdC", [P, D])
        s_wo = sem("sC_wo")
        s_wol = sem("sC_wol")
        s_ov = sem("sC_ov")
        s_osd = sem("sC_osd")
        s_oev = sem("sC_oev")
        s_x2l = [sem("sC_x2l0"), sem("sC_x2l1")]
        s_st = [sem("sC_st0"), sem("sC_st1")]
        pwoC = pbig[:, 0:1024]

        @block.gpsimd
        def _(g):
            g.wait_ge(s_ao, 64)
            g.dma_start(out=wob[:, :].rearrange("p (dc d) -> p dc d", dc=8),
                        in_=wo_d.rearrange("(dc p) d -> p dc d", p=P)
                        ).then_inc(s_wol, 16)

        @block.sync
        def _(sy):
            sy.wait_ge(s_ao, 64)
            for o in range(16):
                if o >= 2:
                    sy.wait_ge(s_oev, o - 1)
                sy.dma_start(out=x2r[:, (o % 2) * D:(o % 2 + 1) * D],
                             in_=x2scr[o * P:(o + 1) * P, :]).then_inc(s_x2l[o % 2], 16)
                sy.wait_ge(s_oev, o + 1)
                sy.dma_start(out=x3o[o * P:(o + 1) * P, :],
                             in_=outc[:, (o % 2) * D:(o % 2 + 1) * D]
                             ).then_inc(s_st[o % 2], 16)
            sy.wait_ge(s_st[0], 16 * 8)
            sy.wait_ge(s_st[1], 16 * 8)

        @block.tensor
        def _(t):
            t.wait_ge(s_wol, 16)
            for o in range(16):
                if o >= 1:
                    t.wait_ge(s_ov, 2 * o - 1)
                for cc in range(8):
                    for nh in range(2):
                        ins = t.matmul(
                            out=pwoC[:, nh * 512:(nh + 1) * 512],
                            lhsT=aoT[:, cc * TOWN + o * P: cc * TOWN + (o + 1) * P],
                            rhs=wob[:, cc * D + nh * 512: cc * D + (nh + 1) * 512],
                            start=(cc == 0), stop=(cc == 7))
                ins.then_inc(s_wo, 1)

        @block.vector
        def _(v):
            for o in range(16):
                ob = outc[:, (o % 2) * D:(o % 2 + 1) * D]
                v.wait_ge(s_wo, o + 1)
                if o >= 2:
                    v.wait_ge(s_st[o % 2], 16 * (o // 2))
                v.tensor_copy(out=ob, in_=pwoC[:]).then_inc(s_ov, 1)
                v.tensor_mul(out=sqd[:], in0=ob, in1=ob)
                v.reduce_sum(out=co(K_OS1), in_=ob, axis=AX)
                v.reduce_sum(out=co(K_OS2), in_=sqd[:], axis=AX)
                rsync(v, v.tensor_scalar(out=co(K_OM), in0=co(K_OS1),
                                         scalar1=1.0 / D, scalar2=None, op0=ALU.mult))
                rsync(v, v.tensor_mul(out=co(K_OV), in0=co(K_OM), in1=co(K_OM)))
                rsync(v, v.scalar_tensor_tensor(out=co(K_OV), in0=co(K_OS2),
                                                scalar=1.0 / D, in1=co(K_OV),
                                                op0=ALU.mult, op1=ALU.subtract))
                v.tensor_scalar(out=co(K_OV), in0=co(K_OV), scalar1=1e-5,
                                scalar2=None, op0=ALU.add).then_inc(s_ov, 1)
                v.wait_ge(s_osd, o + 1)
                rsync(v, v.reciprocal(out=co(K_OR), in_=co(K_OSD)))
                v.wait_ge(s_x2l[o % 2], 16 * (o // 2 + 1))
                v.tensor_scalar(out=ob, in0=ob, scalar1=co(K_OM),
                                scalar2=co(K_OR), op0=ALU.subtract, op1=ALU.mult)
                v.tensor_add(out=ob, in0=ob,
                             in1=x2r[:, (o % 2) * D:(o % 2 + 1) * D]
                             ).then_inc(s_oev, 1)

        @block.scalar
        def _(s):
            for o in range(16):
                s.wait_ge(s_ov, 2 * o + 2)
                s.activation(out=co(K_OSD), in_=co(K_OV),
                             func=AF.Sqrt).then_inc(s_osd, 1)

        esC.close()
    es.close()
    return nc


def prep_l1attn_maps(x1, ffout, inputs):
    com = {
        "wq": np.ascontiguousarray(inputs["L1_attn_Wq"]),
        "wk": np.ascontiguousarray(inputs["L1_attn_Wk"]),
        "wv": np.ascontiguousarray(inputs["L1_attn_Wv"]),
        "wo": np.ascontiguousarray(inputs["L1_attn_Wo"]),
        "g1c": inputs["L1_attn_ln_gamma"].reshape(8, P).T.copy(),
        "b1c": inputs["L1_attn_ln_beta"].reshape(8, P).T.copy(),
        "idn": np.eye(P, dtype=BF16NP),
        "onesr": np.ones((1, 512), BF16NP),
    }
    maps = []
    for c in range(8):
        b, h = c // 2, c % 2
        mp = dict(com)
        mp["x1f"] = x1[b]
        mp["fff"] = ffout[b]
        mp["x1o_"] = np.ascontiguousarray(x1[b, h * TOWN:(h + 1) * TOWN])
        mp["ffo_"] = np.ascontiguousarray(ffout[b, h * TOWN:(h + 1) * TOWN])
        maps.append(mp)
    return maps


def run_l1attn(x1, ffout, inputs):
    nc = build_l1attn()
    maps = prep_l1attn_maps(x1, ffout, inputs)
    res = run_bass_kernel_spmd(nc, maps, core_ids=list(range(8)))
    out = np.empty((4, S, D), np.float32)
    for c in range(8):
        b, h = c // 2, c % 2
        out[b, h * TOWN:(h + 1) * TOWN] = res.results[c]["x3o"]
    return out


def build_l1ffn_attn():
    nc = bass.Bass("TRN2", target_bir_lowering=False)
    x1f = nc.dram_tensor("x1f", [S, D], F32, kind="ExternalInput").ap()
    x1o_ = nc.dram_tensor("x1o_", [TOWN, D], F32, kind="ExternalInput").ap()
    bq_d = nc.dram_tensor("bq_r", [1, D], F32, kind="ExternalInput").ap()
    bk_d = nc.dram_tensor("bk_r", [1, D], F32, kind="ExternalInput").ap()
    bv_d = nc.dram_tensor("bv_r", [1, D], F32, kind="ExternalInput").ap()
    bo_d = nc.dram_tensor("bo_r", [1, D], F32, kind="ExternalInput").ap()
    wq_d = nc.dram_tensor("wq", [D, D], F32, kind="ExternalInput").ap()
    wk_d = nc.dram_tensor("wk", [D, D], F32, kind="ExternalInput").ap()
    wv_d = nc.dram_tensor("wv", [D, D], F32, kind="ExternalInput").ap()
    wo_d = nc.dram_tensor("wo", [D, D], F32, kind="ExternalInput").ap()
    g1c_d = nc.dram_tensor("g1c", [P, 8], F32, kind="ExternalInput").ap()
    b1c_d = nc.dram_tensor("b1c", [P, 8], F32, kind="ExternalInput").ap()
    idn_d = nc.dram_tensor("idn", [P, P], BF16, kind="ExternalInput").ap()
    onesr_d = nc.dram_tensor("onesr", [1, 512], BF16, kind="ExternalInput").ap()
    x3o = nc.dram_tensor("x3o", [TOWN, D], F32, kind="ExternalOutput").ap()
    x2scr = nc.dram_tensor("x2scr", [TOWN, D], F32, kind="Internal").ap()
    vscrC = nc.dram_tensor("vscrC", [S, D], BF16, kind="Internal").ap()

    es = ExitStack()
    sbt = lambda nm, shp, dt=F32: es.enter_context(nc.sbuf_tensor(nm, shp, dt))
    sem = lambda nm: es.enter_context(nc.semaphore(nm))

    idn = sbt("idnC", [P, P], BF16)
    onesr = sbt("onesrC", [1, 512], BF16)
    onescl = sbt("onesclC", [P, 1], BF16)
    g1 = sbt("g1C", [P, 8])
    b1 = sbt("b1C", [P, 8])
    bcol = sbt("bcolC", [P, 8], BF16)
    cols = sbt("colsC", [P, 32])
    co = lambda i: cols[:, i:i + 1]
    (K_S1, K_S2, K_SB, K_MEAN, K_VAR, K_STD, K_RSTD, K_M1, K_R1, K_M2, K_R2,
     K_OS1, K_OS2, K_OM, K_OV, K_OSD, K_OR, K_MR, K_DEN) = range(19)
    brq = sbt("brqC", [1, D], BF16)
    brk = sbt("brkC", [1, D], BF16)
    brv = sbt("brvC", [1, D], BF16)
    aoT = sbt("aoTC", [P, 8 * TOWN], BF16)
    denr = sbt("denrC", [1, 512], BF16)

    esK = ExitStack()
    kT = esK.enter_context(nc.sbuf_tensor("kTC", [P, 8 * S], BF16))
    qT = esK.enter_context(nc.sbuf_tensor("qTC", [P, 8 * TOWN], BF16))

    pbig = es.enter_context(nc.psum_tensor("pbigC", [P, 2048], F32))
    pex = es.enter_context(nc.psum_tensor("pexC", [P, 512], F32))
    ptp = es.enter_context(nc.psum_tensor("ptpC", [P, D], BF16))
    pden = es.enter_context(nc.psum_tensor("pdenC", [1, 512], F32))

    s_w = sem("sC_w")
    s_bb = sem("sC_bb")
    s_ld = [sem("sC_ld0"), sem("sC_ld1")]
    s_fld = [sem("sC_f0"), sem("sC_f1")]
    s_var = sem("sC_var")
    s_std = sem("sC_std")
    s_rt = sem("sC_rt")
    s_app = sem("sC_app")
    s_tp = sem("sC_tp")
    s_tpc = sem("sC_tpc")
    s_mm = sem("sC_mm")
    s_kev = sem("sC_kev")
    s_wsc = sem("sC_wsc")
    s_bc = sem("sC_bc")
    s_bcc = sem("sC_bcc")
    s_x2s = [sem("sC_x2s0"), sem("sC_x2s1")]
    s_vstC = [sem("sC_vst0"), sem("sC_vst1")]
    rt_cnt = [0]

    def mmkev_v(ch, tt):
        return ch * 12 + 8 + tt + 1

    rsync = _mk_rsync(s_rt, rt_cnt)

    esA = ExitStack()
    sbA = lambda nm, shp, dt=F32: esA.enter_context(nc.sbuf_tensor(nm, shp, dt))
    wkb = sbA("wkbC", [P, 8 * D], BF16)
    wvb = sbA("wvbC", [P, 8 * D], BF16)
    wqb = wkb
    xbuf = sbA("xbufC", [P, 2 * D])
    trow = sbA("trowD", [1, 512])
    bbq = sbA("bbqD", [1, D], BF16)
    bbk = sbA("bbkD", [1, D], BF16)
    bbv = sbA("bbvD", [1, D], BF16)
    x2t = sbA("x2tC", [P, D])
    sqc = sbA("sqcC", [P, 512])
    a_bf = sbA("a_bfC", [P, 2 * D], BF16)
    aT = sbA("aTC", [P, 8 * 512], BF16)
    vbufC = sbA("vbufC", [P, 2 * D], BF16)

    pk = [pbig[:, 0:512], pbig[:, 512:1024]]
    pv = pbig[:, 1024:2048]

    NG = 48

    def aT_view(ch):
        return aT[:, :].rearrange("p (dc t) -> p dc t", dc=8)

    def mm_base(ch):
        return ch * 13 if ch < 8 else 104 + (ch - 8) * 9

    def mm_base(ch):
        return ch * 12 if ch <= 8 else 96 + (ch - 8) * 8

    with nc.Block() as block:
        @block.gpsimd
        def _(g):
            g.dma_start(out=bbq[:], in_=bq_d).then_inc(s_bb, 16)
            g.dma_start(out=bbk[:], in_=bk_d).then_inc(s_bb, 16)
            g.dma_start(out=bbv[:], in_=bv_d).then_inc(s_bb, 16)
            g.dma_start(out=idn[:], in_=idn_d).then_inc(s_w, 16)
            g.dma_start(out=onesr[:], in_=onesr_d).then_inc(s_w, 16)
            g.dma_start(out=g1[:], in_=g1c_d).then_inc(s_w, 16)
            g.dma_start(out=b1[:], in_=b1c_d).then_inc(s_w, 16)
            for wsrc, wdst in ((wk_d, wkb), (wv_d, wvb)):
                g.dma_start(out=wdst[:, :].rearrange("p (dc d) -> p dc d", dc=8),
                            in_=wsrc.rearrange("(dc p) d -> p dc d", p=P)
                            ).then_inc(s_w, 16)
            g.wait_ge(s_mm, 96)
            g.dma_start(out=wkb[:, :].rearrange("p (dc d) -> p dc d", dc=8),
                        in_=wq_d.rearrange("(dc p) d -> p dc d", p=P)
                        ).then_inc(s_w, 16)

        @block.vector
        def _(v):
            v.wait_ge(s_w, 96)
            v.wait_ge(s_bb, 48)
            v.memset(onescl[:], 1.0)
            v.tensor_copy(out=bcol[:], in_=b1[:]).then_inc(s_wsc, 1)
            for wi, (wdst, dst, brow) in enumerate(((wkb, brk, bbk),
                                                    (wvb, brv, bbv))):
                for nh in range(2):
                    v.wait_ge(s_bc, wi * 2 + nh + 1)
                    with nc.allow_low_precision(reason="bf16 attn bias rows"):
                        v.tensor_tensor(out=dst[0:1, nh * 512:(nh + 1) * 512],
                                        in0=pden[0:1, 0:512],
                                        in1=brow[0:1, nh * 512:(nh + 1) * 512],
                                        op=ALU.add).then_inc(s_bcc, 1)
                for dc in range(8):
                    ins = v.tensor_scalar(
                        out=wdst[:, dc * D:(dc + 1) * D],
                        in0=wdst[:, dc * D:(dc + 1) * D],
                        scalar1=g1[:, dc:dc + 1], scalar2=1.0,
                        op0=ALU.mult, op1=ALU.mult)
                ins.then_inc(s_wsc, 1)

        @block.tensor
        def _(t):
            t.wait_ge(s_wsc, 1)
            for wi, wdst in enumerate((wkb, wvb)):
                for nh in range(2):
                    for dc in range(8):
                        ins = t.matmul(
                            out=pden[0:1, 0:512],
                            lhsT=bcol[:, dc:dc + 1],
                            rhs=wdst[:, dc * D + nh * 512:dc * D + (nh + 1) * 512],
                            start=(dc == 0), stop=(dc == 7))
                    ins.then_inc(s_bc, 1)
                    t.wait_ge(s_bcc, wi * 2 + nh + 1)

        @block.sync
        def _(sy):
            for g in range(NG):
                if g >= 2:
                    sy.wait_ge(s_app, g - 1)
                if g < 32:
                    sy.dma_start(out=xbuf[:, (g % 2) * D:(g % 2 + 1) * D],
                                 in_=x1f[g * P:(g + 1) * P, :]).then_inc(s_ld[g % 2], 16)
                if g % 4 == 3 and g // 4 < 8 and g >= 3:
                    ch = g // 4
                    for tt in range(4):
                        vi = ch * 4 + tt
                        sy.wait_ge(s_kev, mmkev_v(ch, tt))
                        sy.dma_start(out=vscrC[vi * P:(vi + 1) * P, :],
                                     in_=vbufC[:, (vi % 2) * D:(vi % 2 + 1) * D]
                                     ).then_inc(s_vstC[vi % 2], 16)
                if g >= 32:
                    go = g - 32
                    sy.dma_start(out=xbuf[:, (g % 2) * D:(g % 2 + 1) * D],
                                 in_=x1o_[go * P:(go + 1) * P, :]).then_inc(s_ld[g % 2], 16)
                    sy.wait_ge(s_app, g + 1)
                    sy.dma_start(out=x2scr[go * P:(go + 1) * P, :],
                                 in_=x2t[:]).then_inc(s_x2s[go % 2], 16)

        def _lnc(v, xin, c_mean, c_rstd, g, phase):
            v.tensor_mul(out=sqc[:], in0=xin[:, 0:512], in1=xin[:, 0:512])
            v.reduce_sum(out=co(K_S2), in_=sqc[:], axis=AX)
            v.tensor_mul(out=sqc[:], in0=xin[:, 512:1024], in1=xin[:, 512:1024])
            v.reduce_sum(out=co(K_SB), in_=sqc[:], axis=AX)
            v.reduce_sum(out=co(K_S1), in_=xin, axis=AX)
            rsync(v, v.tensor_add(out=co(K_S2), in0=co(K_S2), in1=co(K_SB)))
            rsync(v, v.tensor_scalar(out=co(c_mean), in0=co(K_S1),
                                     scalar1=1.0 / D, scalar2=None, op0=ALU.mult))
            rsync(v, v.tensor_mul(out=co(K_VAR), in0=co(c_mean), in1=co(c_mean)))
            rsync(v, v.scalar_tensor_tensor(out=co(K_VAR), in0=co(K_S2),
                                            scalar=1.0 / D, in1=co(K_VAR),
                                            op0=ALU.mult, op1=ALU.subtract))
            v.tensor_scalar(out=co(K_VAR), in0=co(K_VAR), scalar1=1e-5,
                            scalar2=None, op0=ALU.add).then_inc(s_var, 1)
            v.wait_ge(s_std, 2 * g + phase)
            rsync(v, v.reciprocal(out=co(c_rstd), in_=co(K_STD)))

        @block.vector
        def _(v):
            for g in range(NG):
                b = (g % 2) * D
                xt = xbuf[:, b:b + D]
                if g >= 1:
                    gp = g - 1
                    v.wait_ge(s_tp, gp + 1)
                    ch, tt = gp // 4, gp % 4
                    if tt == 0 and ch >= 1:
                        v.wait_ge(s_mm, mm_base(ch))
                    v.tensor_copy(
                        out=aT_view(ch)[:, :, tt * P:(tt + 1) * P],
                        in_=ptp[:, :].rearrange("p (dc t) -> p dc t", dc=8)
                    ).then_inc(s_tpc, 1)
                if g == 36:
                    for nh in range(2):
                        v.wait_ge(s_bc, 4 + nh + 1)
                        rsync(v, v.tensor_tensor(out=trow[0:1, :],
                                                 in0=pden[0:1, 0:512],
                                                 in1=bbq[0:1, nh * 512:(nh + 1) * 512],
                                                 op=ALU.add))
                        v.tensor_scalar(out=brq[0:1, nh * 512:(nh + 1) * 512],
                                        in0=trow[0:1, :],
                                        scalar1=float(1.0 / 8.0), scalar2=None,
                                        op0=ALU.mult).then_inc(s_bcc, 1)
                    sc2 = float(1.0 / 8.0)
                    for dc in range(8):
                        ins = v.tensor_scalar(out=wkb[:, dc * D:(dc + 1) * D],
                                              in0=wkb[:, dc * D:(dc + 1) * D],
                                              scalar1=g1[:, dc:dc + 1],
                                              scalar2=sc2, op0=ALU.mult,
                                              op1=ALU.mult)
                    ins.then_inc(s_wsc, 1)
                v.wait_ge(s_ld[g % 2], 16 * (g // 2 + 1))
                if g >= 33:
                    go1 = g - 33
                    v.wait_ge(s_x2s[go1 % 2], 16 * (go1 // 2 + 1))
                x2s = x2t[:]
                _lnc(v, xt, K_MEAN, K_RSTD, g, 1)
                v.tensor_scalar(out=x2s, in0=xt, scalar1=co(K_MEAN),
                                scalar2=co(K_RSTD), op0=ALU.subtract, op1=ALU.mult)
                _lnc(v, x2s, K_M2, K_R2, g, 2)
                v.tensor_scalar(out=a_bf[:, b:b + D], in0=x2s,
                                scalar1=co(K_M2), scalar2=co(K_R2),
                                op0=ALU.subtract, op1=ALU.mult).then_inc(s_app, 1)
            gp = NG - 1
            v.wait_ge(s_tp, gp + 1)
            ch, tt = gp // 4, gp % 4
            v.tensor_copy(out=aT_view(ch)[:, :, tt * P:(tt + 1) * P],
                          in_=ptp[:, :].rearrange("p (dc t) -> p dc t", dc=8)
                          ).then_inc(s_tpc, 1)

        @block.scalar
        def _(s):
            for g in range(NG):
                for ph in range(2):
                    s.wait_ge(s_var, 2 * g + ph + 1)
                    s.activation(out=co(K_STD), in_=co(K_VAR),
                                 func=AF.Sqrt).then_inc(s_std, 1)
                if g % 4 == 3:
                    ch = g // 4
                    for mh in range(8):
                        s.wait_ge(s_mm, mm_base(ch) + mh + 1)
                        src = pk[mh % 2][:, 0:512]
                        if ch < 8:
                            dst = kT[:, mh * S + ch * 512: mh * S + ch * 512 + 512]
                        else:
                            chq = ch - 8
                            dst = qT[:, mh * TOWN + chq * 512: mh * TOWN + chq * 512 + 512]
                        s.activation(out=dst, in_=src, func=AF.Copy).then_inc(s_kev, 1)
                    if ch < 8:
                        for tt in range(4):
                            s.wait_ge(s_mm, mm_base(ch) + 8 + tt + 1)
                            vi = ch * 4 + tt
                            if vi >= 2:
                                s.wait_ge(s_vstC[vi % 2], 16 * ((vi - 2) // 2 + 1))
                            s.activation(out=vbufC[:, (vi % 2) * D:(vi % 2 + 1) * D],
                                         in_=pv[:], func=AF.Copy).then_inc(s_kev, 1)

        @block.tensor
        def _(t):
            t.wait_ge(s_w, 96)
            t.wait_ge(s_bcc, 4)
            t.wait_ge(s_wsc, 3)
            for g in range(NG):
                t.wait_ge(s_app, g + 1)
                if g >= 2:
                    t.wait_ge(s_tpc, g - 1)
                b = (g % 2) * D
                for dc in range(8):
                    ins = t.transpose(out=ptp[:, dc * P:(dc + 1) * P],
                                      in_=a_bf[:, b + dc * P: b + (dc + 1) * P],
                                      identity=idn[:])
                ins.then_inc(s_tp, 1)
                if g % 4 == 3:
                    ch = g // 4
                    if ch == 8:
                        t.wait_ge(s_w, 112)
                        for nh in range(2):
                            for dc in range(8):
                                ins = t.matmul(
                                    out=pden[0:1, 0:512],
                                    lhsT=bcol[:, dc:dc + 1],
                                    rhs=wkb[:, dc * D + nh * 512:
                                            dc * D + (nh + 1) * 512],
                                    start=(dc == 0), stop=(dc == 7))
                            ins.then_inc(s_bc, 1)
                            t.wait_ge(s_bcc, 4 + nh + 1)
                        t.wait_ge(s_wsc, 4)
                    t.wait_ge(s_tpc, 4 * (ch + 1))
                    aTc = aT_view(ch)
                    if ch < 8:
                        kev_base = ch * 12
                        for mh in range(8):
                            ei = kev_base + mh
                            if ei >= 2:
                                t.wait_ge(s_kev, ei - 1)
                            for dc in range(8):
                                t.matmul(out=pk[mh % 2][:, 0:512],
                                         lhsT=wkb[:, dc * D + mh * P: dc * D + (mh + 1) * P],
                                         rhs=aTc[:, dc, :],
                                         start=(dc == 0), stop=False)
                            ins = t.matmul(out=pk[mh % 2][:, 0:512],
                                           lhsT=brk[0:1, mh * P:(mh + 1) * P],
                                           rhs=onesr[0:1, 0:512],
                                           start=False, stop=True)
                            ins.then_inc(s_mm, 1)
                        for tt in range(4):
                            vi = ch * 4 + tt
                            if vi >= 1:
                                t.wait_ge(s_kev, kev_base + 8 + tt)
                            for dc in range(8):
                                for nh in range(2):
                                    t.matmul(
                                        out=pv[:, nh * 512:(nh + 1) * 512],
                                        lhsT=aTc[:, dc, tt * P:(tt + 1) * P],
                                        rhs=wvb[:, dc * D + nh * 512: dc * D + (nh + 1) * 512],
                                        start=(dc == 0), stop=False)
                            for nh in range(2):
                                ins = t.matmul(out=pv[:, nh * 512:(nh + 1) * 512],
                                               lhsT=onesr[0:1, tt * P:(tt + 1) * P]
                                               if False else onesr[0:1, 0:P],
                                               rhs=brv[0:1, nh * 512:(nh + 1) * 512],
                                               start=False, stop=True)
                            ins.then_inc(s_mm, 1)
                    else:
                        for mh in range(8):
                            ei = 96 + (ch - 8) * 8 + mh
                            if ei >= 2:
                                t.wait_ge(s_kev, ei - 1)
                            for dc in range(8):
                                t.matmul(out=pk[mh % 2][:, 0:512],
                                         lhsT=wqb[:, dc * D + mh * P: dc * D + (mh + 1) * P],
                                         rhs=aTc[:, dc, :],
                                         start=(dc == 0), stop=False)
                            ins = t.matmul(out=pk[mh % 2][:, 0:512],
                                           lhsT=brq[0:1, mh * P:(mh + 1) * P],
                                           rhs=onesr[0:1, 0:512],
                                           start=False, stop=True)
                            ins.then_inc(s_mm, 1)

        esA.close()
        esB = ExitStack()
        sbB = lambda nm, shp, dt=F32: esB.enter_context(nc.sbuf_tensor(nm, shp, dt))
        vtk = esB.enter_context(nc.sbuf_tensor("vtkC", [P, 32 * 1040], BF16))
        slb = sbB("slbC", [P, 4 * 512], BF16)
        otb = sbB("otbC", [P, 512], BF16)
        s_vlC = sem("sC_vl")
        s_S = sem("sC_S")
        s_ex = sem("sC_ex")
        s_oc = sem("sC_oc")
        s_ot = sem("sC_ot")
        s_den = sem("sC_den")
        s_bca = sem("sC_bca")
        s_ao = sem("sC_ao")

        pS = [pex[:, 0:512], pbig[:, 0:512]]
        pOT = pbig[:, 512:1024]
        pbcast = pbig[:, 1024:1536]

        def gateB(e):
            e.wait_ge(s_kev, 128)
            e.wait_ge(s_tpc, NG)
            e.wait_ge(s_mm, 128)
            e.wait_ge(s_x2s[0], 16 * 8)
            e.wait_ge(s_x2s[1], 16 * 8)
            e.wait_ge(s_vstC[0], 16 * 16)
            e.wait_ge(s_vstC[1], 16 * 16)

        @block.gpsimd
        def _(g):
            gateB(g)
            g.memset(vtk[:], 1.0)
            for hh2 in range(16):
                g.dma_start(
                    out=vtk[:, :].rearrange("p (t h e) -> p t h e",
                                            t=32, h=16)[:, :, hh2, 0:64],
                    in_=vscrC.rearrange("(t p) (h d) -> p t h d",
                                        p=P, d=64)[:, :, hh2, :]
                    ).then_inc(s_vlC, 16)

        @block.tensor
        def _(t):
            gateB(t)
            t.wait_ge(s_vlC, 256)
            for h in range(NHC):
                hb = (h % 2) * 64
                hc = h // 2
                for qc in range(4):
                    ii = h * 4 + qc
                    for tt2 in range(34):
                        if tt2 < 32:
                            jc = tt2
                            gidx = ii * 32 + jc
                            if gidx >= 2:
                                t.wait_ge(s_ex, gidx - 1)
                            t.matmul(out=pS[jc % 2],
                                     lhsT=kT[hb:hb + 64, hc * S + jc * P: hc * S + (jc + 1) * P],
                                     rhs=qT[hb:hb + 64, hc * TOWN + qc * 512: hc * TOWN + (qc + 1) * 512],
                                     start=True, stop=True).then_inc(s_S, 1)
                        if tt2 == 2 and ii >= 1:
                            t.wait_ge(s_ao, ii)
                        if tt2 >= 2:
                            jc = tt2 - 2
                            gidx = ii * 32 + jc
                            t.wait_ge(s_ex, gidx + 1)
                            t.matmul(out=pOT[0:65, :],
                                     lhsT=vtk[:, jc * 1040 + h * 65:
                                              jc * 1040 + h * 65 + 65],
                                     rhs=slb[:, (jc % 4) * 512:(jc % 4 + 1) * 512],
                                     start=(jc == 0), stop=(jc == 31)
                                     ).then_inc(s_oc, 1)
                    t.wait_ge(s_den, ii + 1)
                    t.matmul(out=pbcast[0:64, :], lhsT=onesr[0:1, 0:64],
                             rhs=denr[0:1, :], start=True,
                             stop=True).then_inc(s_bca, 1)

        @block.scalar
        def _(s):
            gateB(s)
            for h in range(NHC):
                for qc in range(4):
                    ii = h * 4 + qc
                    for jc in range(32):
                        gidx = ii * 32 + jc
                        s.wait_ge(s_S, gidx + 1)
                        if gidx >= 4:
                            s.wait_ge(s_oc, gidx - 3)
                        s.activation(out=slb[:, (jc % 4) * 512:(jc % 4 + 1) * 512],
                                     in_=pS[jc % 2], func=AF.Exp).then_inc(s_ex, 1)
                    s.wait_ge(s_oc, (ii + 1) * 32)
                    s.activation(out=otb[0:64, :], in_=pOT[0:64, :],
                                 func=AF.Copy).then_inc(s_ot, 1)

        @block.vector
        def _(v):
            gateB(v)
            for h in range(NHC):
                hb = (h % 2) * 64
                hc = h // 2
                for qc in range(4):
                    ii = h * 4 + qc
                    v.wait_ge(s_oc, (ii + 1) * 32)
                    rsync(v, v.tensor_copy(out=denr[0:1, :],
                                           in_=pOT[64:65, :]))
                    with nc.allow_low_precision(reason="bf16 softmax denom recip feeding bf16 multiply"):
                        rsync(v, v.reciprocal(out=denr[0:1, :], in_=denr[0:1, :]))
                    v.nop().then_inc(s_den, 1)
                    v.wait_ge(s_bca, ii + 1)
                    v.wait_ge(s_ot, ii + 1)
                    v.tensor_tensor(
                        out=aoT[hb:hb + 64, hc * TOWN + qc * 512: hc * TOWN + (qc + 1) * 512],
                        in0=otb[0:64, :], in1=pbcast[0:64, :],
                        op=ALU.mult).then_inc(s_ao, 1)

        esB.close()
        esK.close()
        esC = ExitStack()
        sbC = lambda nm, shp, dt=F32: esC.enter_context(nc.sbuf_tensor(nm, shp, dt))
        wob = sbC("wobC", [P, 8 * D], BF16)
        bbo = sbC("bboC", [1, D], BF16)
        x2r = sbC("x2rC", [P, 2 * D])
        outc = sbC("outcC", [P, 2 * D])
        s_wo = sem("sC_wo")
        s_wol = sem("sC_wol")
        s_bol = sem("sC_bol")
        s_ov = sem("sC_ov")
        s_oev = sem("sC_oev")
        s_x2l = [sem("sC_x2l0"), sem("sC_x2l1")]
        s_st = [sem("sC_st0"), sem("sC_st1")]
        pwoC = pbig[:, 0:1024]

        @block.gpsimd
        def _(g):
            g.wait_ge(s_ao, 64)
            g.dma_start(out=bbo[:], in_=bo_d).then_inc(s_bol, 16)
            g.dma_start(out=wob[:, :].rearrange("p (dc d) -> p dc d", dc=8),
                        in_=wo_d.rearrange("(dc p) d -> p dc d", p=P)
                        ).then_inc(s_wol, 16)

        @block.sync
        def _(sy):
            sy.wait_ge(s_ao, 64)
            for o in range(16):
                if o >= 2:
                    sy.wait_ge(s_oev, o - 1)
                sy.dma_start(out=x2r[:, (o % 2) * D:(o % 2 + 1) * D],
                             in_=x2scr[o * P:(o + 1) * P, :]).then_inc(s_x2l[o % 2], 16)
                sy.wait_ge(s_oev, o + 1)
                sy.dma_start(out=x3o[o * P:(o + 1) * P, :],
                             in_=outc[:, (o % 2) * D:(o % 2 + 1) * D]
                             ).then_inc(s_st[o % 2], 16)
            sy.wait_ge(s_st[0], 16 * 8)
            sy.wait_ge(s_st[1], 16 * 8)

        @block.tensor
        def _(t):
            t.wait_ge(s_wol, 16)
            t.wait_ge(s_bol, 16)
            for o in range(16):
                if o >= 1:
                    t.wait_ge(s_ov, o)
                for nh in range(2):
                    for cc in range(8):
                        t.matmul(
                            out=pwoC[:, nh * 512:(nh + 1) * 512],
                            lhsT=aoT[:, cc * TOWN + o * P: cc * TOWN + (o + 1) * P],
                            rhs=wob[:, cc * D + nh * 512: cc * D + (nh + 1) * 512],
                            start=(cc == 0), stop=False)
                    ins = t.matmul(
                        out=pwoC[:, nh * 512:(nh + 1) * 512],
                        lhsT=onesr[0:1, 0:P],
                        rhs=bbo[0:1, nh * 512:(nh + 1) * 512],
                        start=False, stop=True)
                ins.then_inc(s_wo, 1)

        @block.vector
        def _(v):
            for o in range(16):
                ob = outc[:, (o % 2) * D:(o % 2 + 1) * D]
                v.wait_ge(s_wo, o + 1)
                if o >= 2:
                    v.wait_ge(s_st[o % 2], 16 * (o // 2))
                v.tensor_copy(out=ob, in_=pwoC[:]).then_inc(s_ov, 1)
                v.wait_ge(s_x2l[o % 2], 16 * (o // 2 + 1))
                v.tensor_add(out=ob, in0=ob,
                             in1=x2r[:, (o % 2) * D:(o % 2 + 1) * D]
                             ).then_inc(s_oev, 1)

        esC.close()
    es.close()
    return nc



def prep_l1fa_maps(x3, inputs):
    com = {
        "wq": np.ascontiguousarray(inputs["L1_ffn_wq"]),
        "wk": np.ascontiguousarray(inputs["L1_ffn_wk"]),
        "wv": np.ascontiguousarray(inputs["L1_ffn_wv"]),
        "wo": np.ascontiguousarray(inputs["L1_ffn_wo"]),
        "g1c": inputs["L1_ffn_ln1_g"].reshape(8, P).T.copy(),
        "b1c": inputs["L1_ffn_ln1_b"].reshape(8, P).T.copy(),
        "bq_r": inputs["L1_ffn_bq"].reshape(1, D),
        "bk_r": inputs["L1_ffn_bk"].reshape(1, D),
        "bv_r": inputs["L1_ffn_bv"].reshape(1, D),
        "bo_r": inputs["L1_ffn_bo"].reshape(1, D),
        "idn": np.eye(P, dtype=BF16NP),
        "onesr": np.ones((1, 512), BF16NP),
    }
    maps = []
    for c in range(8):
        b, h = c // 2, c % 2
        mp = dict(com)
        mp["x1f"] = x3[b]
        mp["x1o_"] = np.ascontiguousarray(x3[b, h * TOWN:(h + 1) * TOWN])
        maps.append(mp)
    return maps


def run_l1ffn_attn(x3, inputs):
    nc = build_l1ffn_attn()
    maps = prep_l1fa_maps(x3, inputs)
    res = run_bass_kernel_spmd(nc, maps, core_ids=list(range(8)))
    out = np.empty((4, S, D), np.float32)
    for c in range(8):
        b, h = c // 2, c % 2
        out[b, h * TOWN:(h + 1) * TOWN] = res.results[c]["x3o"]
    return out


EF = 4096


def build_l1moe():
    nc = bass.Bass("TRN2", target_bir_lowering=False)
    xmid_d = nc.dram_tensor("xmid_", [TOWN, D], F32, kind="ExternalInput").ap()
    x3_d = nc.dram_tensor("x3o_", [TOWN, D], F32, kind="ExternalInput").ap()
    g2c_d = nc.dram_tensor("g2c", [P, 8], F32, kind="ExternalInput").ap()
    b2c_d = nc.dram_tensor("b2c", [P, 8], F32, kind="ExternalInput").ap()
    gw_d = nc.dram_tensor("gw_n", [D, 8], F32, kind="ExternalInput").ap()
    gb_d = nc.dram_tensor("gb_r", [1, 8], F32, kind="ExternalInput").ap()
    w1_d = nc.dram_tensor("w1f", [8 * D, EF], F32, kind="ExternalInput").ap()
    b1_d = nc.dram_tensor("b1cs", [8 * P, 32], F32, kind="ExternalInput").ap()
    w2_d = nc.dram_tensor("w2f", [8 * EF, D], F32, kind="ExternalInput").ap()
    b2_d = nc.dram_tensor("b2r", [8, D], F32, kind="ExternalInput").ap()
    idn_d = nc.dram_tensor("idn", [P, P], BF16, kind="ExternalInput").ap()
    onesr_d = nc.dram_tensor("onesr", [1, 512], BF16, kind="ExternalInput").ap()
    x4_d = nc.dram_tensor("x4o", [TOWN, D], F32, kind="ExternalOutput").ap()
    w1b = nc.dram_tensor("w1bscr", [8 * D, EF], BF16, kind="Internal").ap()

    es = ExitStack()
    sbt = lambda nm, shp, dt=F32: es.enter_context(nc.sbuf_tensor(nm, shp, dt))
    sem = lambda nm: es.enter_context(nc.semaphore(nm))

    idn = sbt("idnM", [P, P], BF16)
    onesr = sbt("onesrM", [1, 512], BF16)
    g2 = sbt("g2M", [P, 8])
    b2 = sbt("b2M", [P, 8])
    gwb = sbt("gwbM", [P, 64], BF16)
    gbb = sbt("gbbM", [1, 8], BF16)
    cols = sbt("colsM", [P, 16])
    co = lambda i: cols[:, i:i + 1]
    (K_S1, K_S2, K_SB, K_MEAN, K_VAR, K_STD, K_RSTD, K_MX) = range(8)
    lg = sbt("lgM", [P, 8])
    maskA = sbt("maskAM", [P, 8 * 16])
    hn2T = sbt("hn2TM", [P, 8 * TOWN], BF16)
    accM = sbt("accM", [P, 16 * D], BF16)
    xb = sbt("xbM", [P, D])
    x3b = sbt("x3bM", [P, D])
    outb = sbt("outbM", [P, 2 * D])
    fot = sbt("fotM", [P, D])
    sqm = sbt("sqmM", [P, 512])
    nbf = sbt("nbfM", [P, D], BF16)
    ybf = sbt("ybfM", [P, 2 * 512], BF16)

    ptp = es.enter_context(nc.psum_tensor("ptpM", [P, D], BF16))
    pg = es.enter_context(nc.psum_tensor("pgM", [P, 8], F32))
    pM = es.enter_context(nc.psum_tensor("pMM", [P, 1024], F32))
    pY = es.enter_context(nc.psum_tensor("pYM", [P, 1024], F32))

    s_wg = sem("sM_wg")
    s_c0 = [sem("sM_c00"), sem("sM_c01")]
    s_c1 = [sem("sM_c10"), sem("sM_c11")]
    s_xl = sem("sM_xl")
    s_var2 = sem("sM_var")
    s_std2 = sem("sM_std")
    s_rt = sem("sM_rt")
    s_nb = sem("sM_nb")
    s_tp0 = sem("sM_tp0")
    s_tpc0 = sem("sM_tpc0")
    s_g = sem("sM_g")
    s_gc = sem("sM_gc")
    s_b1 = sem("sM_b1")
    s_w1 = [sem("sM_w10"), sem("sM_w11")]
    s_mu = sem("sM_mu")
    s_mev = sem("sM_mev")
    s_w2 = sem("sM_w2")
    s_b2 = sem("sM_b2")
    s_y = sem("sM_y")
    s_ye = sem("sM_ye")
    s_ytp = sem("sM_ytp")
    s_ymc = sem("sM_ymc")
    s_xl2 = sem("sM_xl2")
    s_x3l = sem("sM_x3l")
    s_fst = sem("sM_fst")
    s_fo = sem("sM_fo")
    s_st = [sem("sM_st0"), sem("sM_st1")]
    rt_cnt = [0]
    rsync = _mk_rsync(s_rt, rt_cnt)

    esP0 = ExitStack()
    cb = esP0.enter_context(nc.sbuf_tensor("cbM", [P, 2 * 2048], BF16))

    esE = ExitStack()
    w2e = esE.enter_context(nc.sbuf_tensor("w2eM", [P, 32 * D], BF16))
    w1c = esE.enter_context(nc.sbuf_tensor("w1cM", [P, 2 * 8 * P], BF16))
    midT = esE.enter_context(nc.sbuf_tensor("midTM", [P, 32 * 512], BF16))
    b1ce = esE.enter_context(nc.sbuf_tensor("b1ceM", [P, 32], F32))
    b2re = esE.enter_context(nc.sbuf_tensor("b2reM", [1, D], BF16))

    w1cv = lambda sl: w1c[:, :].rearrange("p (s dc m) -> p s dc m", s=2, dc=8)[:, sl]
    w2ev = w2e[:, :].rearrange("p (fc d) -> p fc d", fc=32)

    def _ln2(v, xin, base):
        v.tensor_mul(out=sqm[:], in0=xin[:, 0:512], in1=xin[:, 0:512])
        v.reduce_sum(out=co(K_S2), in_=sqm[:], axis=AX)
        v.tensor_mul(out=sqm[:], in0=xin[:, 512:1024], in1=xin[:, 512:1024])
        v.reduce_sum(out=co(K_SB), in_=sqm[:], axis=AX)
        v.reduce_sum(out=co(K_S1), in_=xin, axis=AX)
        rsync(v, v.tensor_add(out=co(K_S2), in0=co(K_S2), in1=co(K_SB)))
        rsync(v, v.tensor_scalar(out=co(K_MEAN), in0=co(K_S1),
                                 scalar1=1.0 / D, scalar2=None, op0=ALU.mult))
        rsync(v, v.tensor_mul(out=co(K_VAR), in0=co(K_MEAN), in1=co(K_MEAN)))
        rsync(v, v.scalar_tensor_tensor(out=co(K_VAR), in0=co(K_S2),
                                        scalar=1.0 / D, in1=co(K_VAR),
                                        op0=ALU.mult, op1=ALU.subtract))
        v.tensor_scalar(out=co(K_VAR), in0=co(K_VAR), scalar1=1e-5,
                        scalar2=None, op0=ALU.add).then_inc(s_var2, 1)
        v.wait_ge(s_std2, base + 1)
        rsync(v, v.reciprocal(out=co(K_RSTD), in_=co(K_STD)))

    with nc.Block() as block:
        @block.gpsimd
        def _(g):
            g.dma_start(out=idn[:], in_=idn_d).then_inc(s_wg, 16)
            g.dma_start(out=onesr[:], in_=onesr_d).then_inc(s_wg, 16)
            g.dma_start(out=g2[:], in_=g2c_d).then_inc(s_wg, 16)
            g.dma_start(out=b2[:], in_=b2c_d).then_inc(s_wg, 16)
            g.dma_start(out=gwb[:, :].rearrange("p (dc e) -> p dc e", e=8),
                        in_=gw_d.rearrange("(dc p) e -> p dc e", p=P)
                        ).then_inc(s_wg, 16)
            g.dma_start(out=gbb[:], in_=gb_d).then_inc(s_wg, 16)
            for i in range(128):
                sl = i % 2
                r0 = (i // 2) * P
                c0 = (i % 2) * 2048
                if i >= 2:
                    g.wait_ge(s_c1[sl], 16 * ((i - 2) // 2 + 1))
                g.dma_start(out=cb[:, sl * 2048:(sl + 1) * 2048],
                            in_=w1_d[r0:r0 + P, c0:c0 + 2048]
                            ).then_inc(s_c0[sl], 16)
                g.wait_ge(s_c0[sl], 16 * (i // 2 + 1))
                g.dma_start(out=w1b[r0:r0 + P, c0:c0 + 2048],
                            in_=cb[:, sl * 2048:(sl + 1) * 2048]
                            ).then_inc(s_c1[sl], 16)
            for e in range(8):
                if e >= 1:
                    g.wait_ge(s_ymc, 32 * e)
                g.dma_start(out=b2re[:], in_=b2_d[e:e + 1, :]).then_inc(s_b2, 16)
                g.dma_start(out=w2ev,
                            in_=w2_d[e * EF:(e + 1) * EF, :]
                            .rearrange("(fc p) d -> p fc d", p=P)
                            ).then_inc(s_w2, 16)

        @block.sync
        def _(sy):
            for tt in range(16):
                if tt >= 1:
                    sy.wait_ge(s_nb, tt)
                sy.dma_start(out=xb[:], in_=xmid_d[tt * P:(tt + 1) * P, :]
                             ).then_inc(s_xl, 16)
            for e in range(8):
                sy.wait_ge(s_mev, 128 * e)
                sy.dma_start(out=b1ce[:], in_=b1_d[e * P:(e + 1) * P, :]
                             ).then_inc(s_b1, 16)
                sy.wait_ge(s_c1[0], 16 * (8 * e + 8))
                sy.wait_ge(s_c1[1], 16 * (8 * e + 8))
                for gq in range(4):
                    for fb in range(32):
                        idx = (e * 4 + gq) * 32 + fb
                        sl = fb % 2
                        if idx >= 2:
                            sy.wait_ge(s_mu, idx - 1)
                        sy.dma_start(
                            out=w1cv(sl),
                            in_=w1b[e * D:(e + 1) * D, fb * P:(fb + 1) * P]
                            .rearrange("(dc p) m -> p dc m", p=P)
                            ).then_inc(s_w1[sl], 16)
            for tt in range(16):
                sy.wait_ge(s_fst, tt)
                sy.dma_start(out=xb[:], in_=xmid_d[tt * P:(tt + 1) * P, :]
                             ).then_inc(s_xl2, 16)
                sy.dma_start(out=x3b[:], in_=x3_d[tt * P:(tt + 1) * P, :]
                             ).then_inc(s_x3l, 16)
                sy.wait_ge(s_fo, tt + 1)
                sy.dma_start(out=x4_d[tt * P:(tt + 1) * P, :],
                             in_=outb[:, (tt % 2) * D:(tt % 2 + 1) * D]
                             ).then_inc(s_st[tt % 2], 16)
            sy.wait_ge(s_st[0], 16 * 8)
            sy.wait_ge(s_st[1], 16 * 8)

        @block.scalar
        def _(s):
            for tt in range(16):
                s.wait_ge(s_var2, tt + 1)
                s.activation(out=co(K_STD), in_=co(K_VAR),
                             func=AF.Sqrt).then_inc(s_std2, 1)
            for e in range(8):
                s.wait_ge(s_b1, 16 * (e + 1))
                for gq in range(4):
                    for fb in range(32):
                        idx = (e * 4 + gq) * 32 + fb
                        s.wait_ge(s_mu, idx + 1)
                        if gq > 0 or e > 0:
                            s.wait_ge(s_y, 8 * (e * 4 + gq))
                        s.activation(out=midT[:, fb * 512:(fb + 1) * 512],
                                     in_=pM[:, (fb % 2) * 512:(fb % 2 + 1) * 512],
                                     func=AF.Relu,
                                     bias=b1ce[:, fb:fb + 1]).then_inc(s_mev, 1)
                    for hb in range(8):
                        yidx = (e * 4 + gq) * 8 + hb
                        s.wait_ge(s_y, yidx + 1)
                        if yidx >= 2:
                            s.wait_ge(s_ytp, yidx - 1)
                        s.activation(out=ybf[:, (hb % 2) * 512:(hb % 2 + 1) * 512],
                                     in_=pY[:, (hb % 2) * 512:(hb % 2 + 1) * 512],
                                     func=AF.Copy).then_inc(s_ye, 1)
            for tt in range(16):
                s.wait_ge(s_var2, 16 + tt + 1)
                s.activation(out=co(K_STD), in_=co(K_VAR),
                             func=AF.Sqrt).then_inc(s_std2, 1)

        @block.tensor
        def _(t):
            t.wait_ge(s_wg, 96)
            for tt in range(16):
                t.wait_ge(s_nb, tt + 1)
                if tt >= 1:
                    t.wait_ge(s_tpc0, tt)
                for dc in range(8):
                    ins = t.transpose(out=ptp[:, dc * P:(dc + 1) * P],
                                      in_=nbf[:, dc * P:(dc + 1) * P],
                                      identity=idn[:])
                ins.then_inc(s_tp0, 1)
                t.wait_ge(s_tpc0, tt + 1)
                if tt >= 1:
                    t.wait_ge(s_gc, tt)
                for dc in range(8):
                    t.matmul(out=pg[:, 0:8],
                             lhsT=hn2T[:, dc * TOWN + tt * P: dc * TOWN + (tt + 1) * P],
                             rhs=gwb[:, dc * 8:(dc + 1) * 8],
                             start=(dc == 0), stop=False)
                t.matmul(out=pg[:, 0:8], lhsT=onesr[0:1, 0:P],
                         rhs=gbb[0:1, 0:8], start=False,
                         stop=True).then_inc(s_g, 1)
            for e in range(8):
                if e == 0:
                    t.wait_ge(s_w2, 16)
                    t.wait_ge(s_b2, 16)
                else:
                    t.wait_ge(s_w2, 16 * (e + 1))
                    t.wait_ge(s_b2, 16 * (e + 1))
                for gq in range(4):
                    for fb in range(32):
                        idx = (e * 4 + gq) * 32 + fb
                        sl = fb % 2
                        t.wait_ge(s_w1[sl], 16 * (idx // 2 + 1))
                        if idx >= 2:
                            t.wait_ge(s_mev, idx - 1)
                        for dc in range(8):
                            ins = t.matmul(
                                out=pM[:, sl * 512:(sl + 1) * 512],
                                lhsT=w1cv(sl)[:, dc, :],
                                rhs=hn2T[:, dc * TOWN + gq * 512:
                                         dc * TOWN + (gq + 1) * 512],
                                start=(dc == 0), stop=(dc == 7))
                        ins.then_inc(s_mu, 1)
                    t.wait_ge(s_mev, 128 * e + 32 * (gq + 1))
                    for hb in range(10):
                        if hb < 8:
                            yidx = (e * 4 + gq) * 8 + hb
                            if yidx >= 2:
                                t.wait_ge(s_ye, yidx - 1)
                            for fc in range(32):
                                t.matmul(
                                    out=pY[:, (hb % 2) * 512:(hb % 2 + 1) * 512],
                                    lhsT=w2ev[:, fc, hb * P:(hb + 1) * P],
                                    rhs=midT[:, fc * 512:(fc + 1) * 512],
                                    start=(fc == 0), stop=False)
                            t.matmul(
                                out=pY[:, (hb % 2) * 512:(hb % 2 + 1) * 512],
                                lhsT=b2re[0:1, hb * P:(hb + 1) * P],
                                rhs=onesr[0:1, 0:512],
                                start=False, stop=True).then_inc(s_y, 1)
                        if hb >= 2:
                            hbp = hb - 2
                            yidxp = (e * 4 + gq) * 8 + hbp
                            t.wait_ge(s_ye, yidxp + 1)
                            if yidxp >= 1:
                                t.wait_ge(s_ymc, yidxp)
                            for q in range(4):
                                ins = t.transpose(
                                    out=ptp[:, q * P:(q + 1) * P],
                                    in_=ybf[:, (hbp % 2) * 512 + q * P:
                                            (hbp % 2) * 512 + (q + 1) * P],
                                    identity=idn[:])
                            ins.then_inc(s_ytp, 1)

        @block.vector
        def _(v):
            for tt in range(16):
                v.wait_ge(s_xl, 16 * (tt + 1))
                _ln2(v, xb[:], tt)
                v.tensor_scalar(out=nbf[:], in0=xb[:], scalar1=co(K_MEAN),
                                scalar2=co(K_RSTD), op0=ALU.subtract,
                                op1=ALU.mult).then_inc(s_nb, 1)
                v.wait_ge(s_tp0, tt + 1)
                for dc in range(8):
                    ins = v.tensor_scalar(
                        out=hn2T[:, dc * TOWN + tt * P: dc * TOWN + (tt + 1) * P],
                        in0=ptp[:, dc * P:(dc + 1) * P],
                        scalar1=g2[:, dc:dc + 1], scalar2=b2[:, dc:dc + 1],
                        op0=ALU.mult, op1=ALU.add)
                ins.then_inc(s_tpc0, 1)
                v.wait_ge(s_g, tt + 1)
                rsync(v, v.tensor_copy(out=lg[:], in_=pg[:, 0:8]))
                v.nop().then_inc(s_gc, 1)
                rsync(v, v.reduce_max(out=co(K_MX), in_=lg[:], axis=AX))
                v.tensor_scalar(out=maskA[:, tt * 8:(tt + 1) * 8], in0=lg[:],
                                scalar1=co(K_MX), scalar2=None,
                                op0=ALU.is_equal)
            for e in range(8):
                for gq in range(4):
                    for hb in range(8):
                        yidx = (e * 4 + gq) * 8 + hb
                        v.wait_ge(s_ytp, yidx + 1)
                        for q in range(4):
                            tt = gq * 4 + q
                            mcol = maskA[:, tt * 8 + e: tt * 8 + e + 1]
                            dst = accM[:, tt * D + hb * P: tt * D + (hb + 1) * P]
                            if e == 0:
                                ins = v.tensor_scalar(out=dst,
                                                      in0=ptp[:, q * P:(q + 1) * P],
                                                      scalar1=mcol, scalar2=None,
                                                      op0=ALU.mult)
                            else:
                                v.tensor_scalar(out=sqm[:, 0:P],
                                                in0=ptp[:, q * P:(q + 1) * P],
                                                scalar1=mcol, scalar2=None,
                                                op0=ALU.mult)
                                ins = v.tensor_add(out=dst, in0=dst,
                                                   in1=sqm[:, 0:P])
                        ins.then_inc(s_ymc, 1)
            for tt in range(16):
                v.wait_ge(s_xl2, 16 * (tt + 1))
                v.tensor_add(out=fot[:], in0=xb[:],
                             in1=accM[:, tt * D:(tt + 1) * D]).then_inc(s_fst, 1)
                _ln2(v, fot[:], 16 + tt)
                v.wait_ge(s_x3l, 16 * (tt + 1))
                if tt >= 2:
                    v.wait_ge(s_st[tt % 2], 16 * (tt // 2))
                ob = outb[:, (tt % 2) * D:(tt % 2 + 1) * D]
                v.tensor_scalar(out=ob, in0=fot[:], scalar1=co(K_MEAN),
                                scalar2=co(K_RSTD), op0=ALU.subtract,
                                op1=ALU.mult)
                v.tensor_add(out=ob, in0=ob, in1=x3b[:]).then_inc(s_fo, 1)

    esE.close()
    esP0.close()
    es.close()
    return nc


def prep_l1moe_maps(xmid, x3, inputs):
    com = {
        "g2c": inputs["L1_ffn_ln2_g"].reshape(8, P).T.copy(),
        "b2c": inputs["L1_ffn_ln2_b"].reshape(8, P).T.copy(),
        "gw_n": np.ascontiguousarray(inputs["L1_ffn_gate_w"]),
        "gb_r": inputs["L1_ffn_gate_b"].reshape(1, 8),
        "w1f": inputs["L1_ffn_w1"].reshape(8 * D, EF),
        "b1cs": np.ascontiguousarray(
            inputs["L1_ffn_b1"].reshape(8, 32, P).transpose(0, 2, 1)
        ).reshape(8 * P, 32),
        "w2f": inputs["L1_ffn_w2"].reshape(8 * EF, D),
        "b2r": np.ascontiguousarray(inputs["L1_ffn_b2"]),
        "idn": np.eye(P, dtype=BF16NP),
        "onesr": np.ones((1, 512), BF16NP),
    }
    maps = []
    for c in range(8):
        b, h = c // 2, c % 2
        mp = dict(com)
        mp["xmid_"] = np.ascontiguousarray(xmid[b, h * TOWN:(h + 1) * TOWN])
        mp["x3o_"] = np.ascontiguousarray(x3[b, h * TOWN:(h + 1) * TOWN])
        maps.append(mp)
    return maps


def run_l1moe(xmid, x3, inputs):
    nc = build_l1moe()
    maps = prep_l1moe_maps(xmid, x3, inputs)
    res = run_bass_kernel_spmd(nc, maps, core_ids=list(range(8)))
    out = np.empty((4, S, D), np.float32)
    for c in range(8):
        b, h = c // 2, c % 2
        out[b, h * TOWN:(h + 1) * TOWN] = res.results[c]["x4o"]
    return out


# version 04
# speedup vs baseline: 1.2142x; 1.2142x over previous
import numpy as np
import ml_dtypes
from contextlib import ExitStack

import concourse.bass as bass
import concourse.mybir as mybir
from concourse.bass_utils import run_bass_kernel_spmd

F32 = mybir.dt.float32
BF16 = mybir.dt.bfloat16
AF = mybir.ActivationFunctionType
ALU = mybir.AluOpType
AX = mybir.AxisListType.X

P = 128
D = 1024
S = 4096
TOWN = 2048
BF16NP = ml_dtypes.bfloat16



def bitonic_substages(n):
    k = 2
    while k <= n:
        j = k // 2
        while j >= 1:
            yield k, j
            j //= 2
        k *= 2


def emit_bitonic(v, b0, b1, n):
    bufs = [b0, b1]
    cur = 0
    for k, j in bitonic_substages(n):
        src, dst = bufs[cur], bufs[1 - cur]
        a_cnt = 2 if k < n else 1
        s_cnt = n // (2 * k) if k < n else 1
        r_cnt = k // (2 * j)
        v6s = src[:, 0:n].rearrange("p (s a r b j) -> p s a r b j",
                                    s=s_cnt, a=a_cnt, r=r_cnt, b=2, j=j)
        v6d = dst[:, 0:n].rearrange("p (s a r b j) -> p s a r b j",
                                    s=s_cnt, a=a_cnt, r=r_cnt, b=2, j=j)
        for parity in range(a_cnt):
            direction = +1 if parity == 0 else -1
            A, B = v6s[:, :, parity, :, 0, :], v6s[:, :, parity, :, 1, :]
            Ao, Bo = v6d[:, :, parity, :, 0, :], v6d[:, :, parity, :, 1, :]
            opa = ALU.max if direction > 0 else ALU.min
            opb = ALU.min if direction > 0 else ALU.max
            v.tensor_tensor(out=Ao, in0=A, in1=B, op=opa)
            v.tensor_tensor(out=Bo, in0=A, in1=B, op=opb)
        cur = 1 - cur
    assert cur == 0



def _mk_rsync(s_rt, rt_cnt):
    def rsync(eng, inst):
        inst.then_inc(s_rt, 1)
        rt_cnt[0] += 1
        eng.wait_ge(s_rt, rt_cnt[0])
    return rsync



def build_l0attn(debug=False):
    nc = bass.Bass("TRN2", target_bir_lowering=False)
    xf = nc.dram_tensor("xf", [S, D], F32, kind="ExternalInput").ap()
    xo = nc.dram_tensor("xo", [TOWN, D], F32, kind="ExternalInput").ap()
    wqt = nc.dram_tensor("wqt", [D, D], F32, kind="ExternalInput").ap()
    wkt = nc.dram_tensor("wkt", [D, D], F32, kind="ExternalInput").ap()
    wvt = nc.dram_tensor("wvt", [D, D], F32, kind="ExternalInput").ap()
    idn_d = nc.dram_tensor("idn", [P, P], BF16, kind="ExternalInput").ap()
    x1o = nc.dram_tensor("x1o", [TOWN, D], F32, kind="ExternalOutput").ap()
    vscr = nc.dram_tensor("vscr", [S, D], BF16, kind="Internal").ap()
    dbg = {}
    if debug:
        for nm, w, dt in (("d_wk", 8 * D, BF16), ("d_abf", 2 * D, BF16),
                          ("d_aT", 8 * 512 * 2, BF16), ("d_kT", 8 * S, BF16),
                          ("d_qT", 8 * TOWN, BF16), ("d_scores", S, F32),
                          ("d_bs0", S, F32), ("d_bs1", S, F32),
                          ("d_abf2", S, BF16), ("d_cols", 32, F32),
                          ("d_outb", 2 * D, F32), ("d_aTg", 2 * S, BF16)):
            dbg[nm] = nc.dram_tensor(nm, [P, w], dt, kind="ExternalOutput").ap()

    es = ExitStack()
    sbt = lambda nm, shp, dt=F32: es.enter_context(nc.sbuf_tensor(nm, shp, dt))
    sem = lambda nm: es.enter_context(nc.semaphore(nm))

    kT = sbt("kT", [P, 8 * S], BF16)
    qT = sbt("qT", [P, 8 * TOWN], BF16)
    cols = sbt("cols", [P, 32])
    idn = sbt("idnb", [P, P], BF16)
    co = lambda i: cols[:, i:i + 1]
    (C_S1, C_S2, C_MEAN, C_VAR, C_STD, C_RSTD, C_NMAX, C_SSUM, C_SUMZ, C_STEP,
     C_KC, C_RK, C_NRK, C_ALPHA, C_CT, C_OS1, C_OS2, C_OMEAN, C_OVAR, C_OSTD,
     C_ORSTD, C_MR, C_STEPM, C_ALPHA2) = range(24)

    esA = ExitStack()
    sbA = lambda nm, shp, dt=F32: esA.enter_context(nc.sbuf_tensor(nm, shp, dt))
    psA = lambda nm, shp, dt=F32: esA.enter_context(nc.psum_tensor(nm, shp, dt))
    wq_b = sbA("wq_b", [P, 8 * D], BF16)
    wk_b = sbA("wk_b", [P, 8 * D], BF16)
    wv_b = sbA("wv_b", [P, 8 * D], BF16)
    xbuf = sbA("xbuf", [P, 2 * D])
    sqb = sbA("sqb", [P, D])
    a_bf = sbA("a_bf", [P, 2 * D], BF16)
    aT = sbA("aT", [P, 2 * 8 * 512], BF16)
    vbuf = sbA("vbuf", [P, 2 * D], BF16)
    ptp = psA("ptp", [P, 2 * D], BF16)
    pk = psA("pk", [P, 2 * 512])
    pv = psA("pv", [P, D])

    s_wld = sem("s_wld")
    s_wk = sem("s_wk")
    s_wv = sem("s_wv")
    s_wq = sem("s_wq")
    s_xld = [sem("s_xld0"), sem("s_xld1")]
    s_var = sem("s_var")
    s_std = sem("s_std")
    s_rt = sem("s_rt")
    s_app = sem("s_app")
    s_tp = sem("s_tp")
    s_tpc = sem("s_tpc")
    s_mm = sem("s_mm")
    s_kev = sem("s_kev")
    s_vev = sem("s_vev")
    s_vst = [sem("s_vst0"), sem("s_vst1")]

    NG = 48
    rt_cnt = [0]
    rsync = _mk_rsync(s_rt, rt_cnt)

    def src_rows(g):
        return xf[g * P:(g + 1) * P, :] if g < 32 else xo[(g - 32) * P:(g - 31) * P, :]

    def aT_view(ch):
        return aT[:, (ch % 2) * S:(ch % 2) * S + S].rearrange("p (dc t) -> p dc t", dc=8)

    def mm_base(ch):
        return ch * 12 if ch <= 8 else 96 + (ch - 8) * 8


    with nc.Block() as block:
        @block.gpsimd
        def _(g):
            g.dma_start(out=idn[:], in_=idn_d).then_inc(s_wld, 16)
            for wsrc, wdst, ws in ((wkt, wk_b, s_wk), (wvt, wv_b, s_wv),
                                   (wqt, wq_b, s_wq)):
                g.dma_start(out=wdst[:, :].rearrange("p (dc d) -> p dc d", dc=8),
                            in_=wsrc.rearrange("(dc p) d -> p dc d", p=P)
                            ).then_inc(ws, 16)

        @block.sync
        def _(sy):
            for g in range(NG):
                if g >= 2:
                    sy.wait_ge(s_app, g - 1)
                sy.dma_start(out=xbuf[:, (g % 2) * D:(g % 2 + 1) * D],
                             in_=src_rows(g)).then_inc(s_xld[g % 2], 16)
                if g % 4 == 3 and g // 4 < 8:
                    ch = g // 4
                    for tt in range(4):
                        vi = ch * 4 + tt
                        sy.wait_ge(s_vev, vi + 1)
                        r0 = ch * 512 + tt * P
                        sy.dma_start(out=vscr[r0:r0 + P, :],
                                     in_=vbuf[:, (vi % 2) * D:(vi % 2 + 1) * D]
                                     ).then_inc(s_vst[vi % 2], 16)

        @block.vector
        def _(v):
            for g in range(NG):
                b = (g % 2) * D
                xt = xbuf[:, b:b + D]
                if g >= 1:
                    gp = g - 1
                    v.wait_ge(s_tp, gp + 1)
                    ch, tt = gp // 4, gp % 4
                    v.tensor_copy(
                        out=aT_view(ch)[:, :, tt * P:(tt + 1) * P],
                        in_=ptp[:, (gp % 2) * D:(gp % 2 + 1) * D].rearrange(
                            "p (dc t) -> p dc t", dc=8)).then_inc(s_tpc, 1)
                v.wait_ge(s_xld[g % 2], 16 * (g // 2 + 1))
                v.tensor_mul(out=sqb[:], in0=xt, in1=xt)
                v.reduce_sum(out=co(C_S1), in_=xt, axis=AX)
                v.reduce_sum(out=co(C_S2), in_=sqb[:], axis=AX)
                rsync(v, v.tensor_scalar(out=co(C_MEAN), in0=co(C_S1),
                                         scalar1=1.0 / D, scalar2=None,
                                         op0=ALU.mult))
                rsync(v, v.tensor_mul(out=co(C_VAR), in0=co(C_MEAN),
                                      in1=co(C_MEAN)))
                rsync(v, v.scalar_tensor_tensor(out=co(C_VAR), in0=co(C_S2),
                                                scalar=1.0 / D, in1=co(C_VAR),
                                                op0=ALU.mult, op1=ALU.subtract))
                v.tensor_scalar(out=co(C_VAR), in0=co(C_VAR), scalar1=1e-5,
                                scalar2=None, op0=ALU.add).then_inc(s_var, 1)
                v.wait_ge(s_std, g + 1)
                rsync(v, v.reciprocal(out=co(C_RSTD), in_=co(C_STD)))
                v.tensor_scalar(out=a_bf[:, b:b + D], in0=xt,
                                scalar1=co(C_MEAN), scalar2=co(C_RSTD),
                                op0=ALU.subtract, op1=ALU.mult).then_inc(s_app, 1)
            gp = NG - 1
            v.wait_ge(s_tp, gp + 1)
            ch, tt = gp // 4, gp % 4
            v.tensor_copy(out=aT_view(ch)[:, :, tt * P:(tt + 1) * P],
                          in_=ptp[:, (gp % 2) * D:(gp % 2 + 1) * D].rearrange(
                              "p (dc t) -> p dc t", dc=8)).then_inc(s_tpc, 1)

        def _kq_evac(s, ch):
            for mh in range(8):
                s.wait_ge(s_mm, mm_base(ch) + mh + 1)
                src = pk[:, (mh % 2) * 512:(mh % 2) * 512 + 512]
                if ch < 8:
                    dst = kT[:, mh * S + ch * 512: mh * S + ch * 512 + 512]
                else:
                    chq = ch - 8
                    dst = qT[:, mh * TOWN + chq * 512: mh * TOWN + chq * 512 + 512]
                s.activation(out=dst, in_=src, func=AF.Copy).then_inc(s_kev, 1)

        @block.scalar
        def _(s):
            for g in range(NG):
                s.wait_ge(s_var, g + 1)
                s.activation(out=co(C_STD), in_=co(C_VAR),
                             func=AF.Sqrt).then_inc(s_std, 1)
                if g % 4 == 3:
                    ch = g // 4
                    _kq_evac(s, ch)
                    if ch < 8:
                        for tt in range(4):
                            s.wait_ge(s_mm, mm_base(ch) + 8 + tt + 1)
                            vi = ch * 4 + tt
                            if vi >= 2:
                                s.wait_ge(s_vst[vi % 2], 16 * ((vi - 2) // 2 + 1))
                            s.activation(out=vbuf[:, (vi % 2) * D:(vi % 2 + 1) * D],
                                         in_=pv[:], func=AF.Copy).then_inc(s_vev, 1)

        @block.tensor
        def _(t):
            t.wait_ge(s_wld, 16)
            for g in range(NG):
                t.wait_ge(s_app, g + 1)
                if g >= 2:
                    t.wait_ge(s_tpc, g - 1)
                b = (g % 2) * D
                pb = (g % 2) * D
                for dc in range(8):
                    ins = t.transpose(out=ptp[:, pb + dc * P: pb + (dc + 1) * P],
                                      in_=a_bf[:, b + dc * P: b + (dc + 1) * P],
                                      identity=idn[:])
                ins.then_inc(s_tp, 1)
                if g % 4 == 3:
                    ch = g // 4
                    t.wait_ge(s_tpc, 4 * (ch + 1))
                    aTc = aT_view(ch)
                    if ch < 8:
                        t.wait_ge(s_wk, 16)
                        t.wait_ge(s_wv, 16)
                        for mh in range(8):
                            ei = ch * 8 + mh
                            if ei >= 2:
                                t.wait_ge(s_kev, ei - 1)
                            for dc in range(8):
                                ins = t.matmul(
                                    out=pk[:, (mh % 2) * 512:(mh % 2) * 512 + 512],
                                    lhsT=wk_b[:, dc * D + mh * P: dc * D + (mh + 1) * P],
                                    rhs=aTc[:, dc, :],
                                    start=(dc == 0), stop=(dc == 7))
                            ins.then_inc(s_mm, 1)
                        for tt in range(4):
                            vi = ch * 4 + tt
                            if vi >= 1:
                                t.wait_ge(s_vev, vi)
                            for dc in range(8):
                                for nh in range(2):
                                    ins = t.matmul(
                                        out=pv[:, nh * 512:(nh + 1) * 512],
                                        lhsT=aTc[:, dc, tt * P:(tt + 1) * P],
                                        rhs=wv_b[:, dc * D + nh * 512: dc * D + (nh + 1) * 512],
                                        start=(dc == 0), stop=(dc == 7))
                            ins.then_inc(s_mm, 1)
                    else:
                        t.wait_ge(s_wq, 16)
                        for mh in range(8):
                            ei = 64 + (ch - 8) * 8 + mh
                            if ei >= 2:
                                t.wait_ge(s_kev, ei - 1)
                            for dc in range(8):
                                ins = t.matmul(
                                    out=pk[:, (mh % 2) * 512:(mh % 2) * 512 + 512],
                                    lhsT=wq_b[:, dc * D + mh * P: dc * D + (mh + 1) * P],
                                    rhs=aTc[:, dc, :],
                                    start=(dc == 0), stop=(dc == 7))
                            ins.then_inc(s_mm, 1)

        esA.close()
        esB = ExitStack()
        sbB = lambda nm, shp, dt=F32: esB.enter_context(nc.sbuf_tensor(nm, shp, dt))
        psB = lambda nm, shp, dt=F32: esB.enter_context(nc.psum_tensor(nm, shp, dt))
        scores = sbB("scores", [P, S])
        bs0 = sbB("bs0", [P, S], BF16)
        bs1 = sbB("bs1", [P, S], BF16)
        t1s = sbB("t1s", [P, D])
        onesb = sbB("onesb", [P, D])
        abf = sbB("abf", [P, S], BF16)
        aTg = sbB("aTg", [P, 2 * S], BF16)
        vch = sbB("vch", [P, 2 * D], BF16)
        outb = sbB("outb", [P, 2 * D])
        sqB2 = sbB("sqB2", [P, D])
        xro = sbB("xro", [P, 2 * D])
        pS = psB("pS", [P, 2 * 512])
        ptB = psB("ptB", [P, 2 * D], BF16)
        pav = psB("pav", [P, 2 * D])

        s_S = sem("s_S")
        s_sev = sem("s_sev")
        s_stat = sem("s_stat")
        s_zif = sem("s_zif")
        s_abf = sem("s_abf")
        s_at = sem("s_at")
        s_atc = sem("s_atc")
        s_vld = [sem("s_vld0"), sem("s_vld1")]
        s_av = sem("s_av")
        s_ovar = sem("s_ovar")
        s_ostd = sem("s_ostd")
        s_oev = sem("s_oev")
        s_xro = [sem("s_xro0"), sem("s_xro1")]
        s_st = [sem("s_st0"), sem("s_st1")]

        @block.tensor
        def _(t):
            t.wait_ge(s_mm, 128)
            t.wait_ge(s_kev, 96)
            t.wait_ge(s_tpc, NG)
            t.wait_ge(s_vev, 32)
            t.wait_ge(s_vst[0], 16 * 16)
            t.wait_ge(s_vst[1], 16 * 16)
            t.wait_ge(s_xld[0], 16 * 24)
            t.wait_ge(s_xld[1], 16 * 24)
            t.wait_ge(s_app, NG)
            for qt in range(16):
                qi = qt % 2
                g2 = qt // 2
                for nch in range(8):
                    gn = qt * 8 + nch
                    if gn >= 2:
                        t.wait_ge(s_sev, gn - 1)
                    for hc in range(8):
                        ins = t.matmul(
                            out=pS[:, (gn % 2) * 512:(gn % 2) * 512 + 512],
                            lhsT=qT[:, hc * TOWN + qt * P: hc * TOWN + (qt + 1) * P],
                            rhs=kT[:, hc * S + nch * 512: hc * S + nch * 512 + 512],
                            start=(hc == 0), stop=(hc == 7))
                    ins.then_inc(s_S, 1)
                t.wait_ge(s_abf, qt + 1)
                for hb in range(4):
                    bi = qt * 4 + hb
                    if bi >= 2:
                        t.wait_ge(s_atc, bi - 1)
                    for jj in range(8):
                        jc = hb * 8 + jj
                        ins = t.transpose(
                            out=ptB[:, (hb % 2) * D + jj * P:(hb % 2) * D + (jj + 1) * P],
                            in_=abf[:, jc * P:(jc + 1) * P],
                            identity=idn[:])
                    ins.then_inc(s_at, 1)
                if qi == 1:
                    t.wait_ge(s_atc, (qt + 1) * 4)
                    if g2 >= 1:
                        t.wait_ge(s_oev, 2 * g2)
                    for jc in range(32):
                        li = g2 * 32 + jc
                        t.wait_ge(s_vld[jc % 2], 16 * (li // 2 + 1))
                        for q2 in range(2):
                            for nh in range(2):
                                ins = t.matmul(
                                    out=pav[:, q2 * D + nh * 512: q2 * D + (nh + 1) * 512],
                                    lhsT=aTg[:, q2 * S + jc * P: q2 * S + (jc + 1) * P],
                                    rhs=vch[:, (jc % 2) * D + nh * 512:(jc % 2) * D + (nh + 1) * 512],
                                    start=(jc == 0), stop=(jc == 31))
                        ins.then_inc(s_av, 1)

        @block.scalar
        def _(s):
            s.wait_ge(s_mm, 128)
            s.wait_ge(s_kev, 96)
            s.wait_ge(s_tpc, NG)
            s.wait_ge(s_vev, 32)
            s.wait_ge(s_vst[0], 16 * 16)
            s.wait_ge(s_vst[1], 16 * 16)
            s.wait_ge(s_xld[0], 16 * 24)
            s.wait_ge(s_xld[1], 16 * 24)
            s.wait_ge(s_app, NG)
            for qt in range(16):
                for nch in range(8):
                    gn = qt * 8 + nch
                    s.wait_ge(s_S, gn + 1)
                    if nch == 0:
                        s.wait_ge(s_abf, qt)
                    s.activation(out=scores[:, nch * 512:(nch + 1) * 512],
                                 in_=pS[:, (gn % 2) * 512:(gn % 2) * 512 + 512],
                                 func=AF.Copy, scale=1.0 / 32.0).then_inc(s_sev, 1)
                if qt % 2 == 1:
                    for q2 in range(2):
                        o = qt - 1 + q2
                        s.wait_ge(s_ovar, o + 1)
                        s.activation(out=co(C_OSTD), in_=co(C_OVAR),
                                     func=AF.Sqrt).then_inc(s_ostd, 1)

        @block.gpsimd
        def _(g):
            g.wait_ge(s_mm, 128)
            g.wait_ge(s_kev, 96)
            g.wait_ge(s_tpc, NG)
            g.wait_ge(s_vev, 32)
            g.wait_ge(s_vst[0], 16 * 16)
            g.wait_ge(s_vst[1], 16 * 16)
            g.wait_ge(s_xld[0], 16 * 24)
            g.wait_ge(s_xld[1], 16 * 24)
            g.wait_ge(s_app, NG)
            for qt in range(16):
                g.wait_ge(s_stat, qt + 1)
                with nc.allow_low_precision(reason="bf16 sort keys"):
                    g.tensor_scalar(out=bs0[:], in0=scores[:],
                                    scalar1=co(C_NMAX), scalar2=None,
                                    op0=ALU.add).then_inc(s_zif, 1)

        @block.vector
        def _(v):
            v.wait_ge(s_mm, 128)
            v.wait_ge(s_kev, 96)
            v.wait_ge(s_tpc, NG)
            v.wait_ge(s_vev, 32)
            v.wait_ge(s_vst[0], 16 * 16)
            v.wait_ge(s_vst[1], 16 * 16)
            v.wait_ge(s_xld[0], 16 * 24)
            v.wait_ge(s_xld[1], 16 * 24)
            v.wait_ge(s_app, NG)
            v.memset(onesb[:], 1.0)
            for qt in range(16):
                qi = qt % 2
                g2 = qt // 2
                v.wait_ge(s_sev, (qt + 1) * 8)
                v.reduce_max(out=co(C_NMAX), in_=scores[:], axis=AX, negate=True)
                rsync(v, v.reduce_sum(out=co(C_SSUM), in_=scores[:], axis=AX))
                rsync(v, v.scalar_tensor_tensor(out=co(C_SUMZ), in0=co(C_NMAX),
                                                scalar=float(S), in1=co(C_SSUM),
                                                op0=ALU.mult, op1=ALU.add))
                rsync(v, v.tensor_scalar(out=co(C_STEP), in0=co(C_SUMZ),
                                         scalar1=-1.0, scalar2=1.0 / S,
                                         op0=ALU.add, op1=ALU.mult))
                v.tensor_sub(out=co(C_STEPM), in0=co(C_STEP),
                             in1=co(C_NMAX)).then_inc(s_stat, 1)
                v.wait_ge(s_zif, qt + 1)
                v.tensor_scalar(out=bs1[:], in0=scores[:], scalar1=co(C_STEPM),
                                scalar2=None, op0=ALU.is_gt)
                rsync(v, v.reduce_sum(out=co(C_KC), in_=bs1[:], axis=AX))
                rsync(v, v.reciprocal(out=co(C_RK), in_=co(C_KC)))
                rsync(v, v.tensor_scalar(out=co(C_NRK), in0=co(C_RK),
                                         scalar1=-1.0, scalar2=None,
                                         op0=ALU.mult))
                rsync(v, v.tensor_add(out=co(C_ALPHA), in0=co(C_STEP),
                                      in1=co(C_RK)))
                v.tensor_add(out=co(C_ALPHA2), in0=co(C_ALPHA), in1=co(C_NMAX))
                emit_bitonic(v, bs0, bs1, S)
                for seg in range(4):
                    ins = v.tensor_tensor_scan(
                        out=t1s[:],
                        data0=onesb[:], data1=bs0[:, seg * D:(seg + 1) * D],
                        initial=0.0, op0=ALU.mult, op1=ALU.add)
                    if seg > 0:
                        ins = v.tensor_scalar(out=t1s[:], in0=t1s[:],
                                              scalar1=co(C_CT), scalar2=None,
                                              op0=ALU.add)
                    if seg < 3:
                        rsync(v, ins)
                        rsync(v, v.tensor_copy(out=co(C_CT),
                                               in_=t1s[:, D - 1:D]))
                    v.tensor_scalar(out=t1s[:], in0=t1s[:],
                                    scalar1=co(C_NRK), scalar2=None,
                                    op0=ALU.mult)
                    v.tensor_add(out=t1s[:], in0=t1s[:],
                                 in1=scores[:, seg * D:(seg + 1) * D])
                    ins = v.tensor_scalar(out=abf[:, seg * D:(seg + 1) * D],
                                          in0=t1s[:], scalar1=co(C_ALPHA2),
                                          scalar2=0.0, op0=ALU.add,
                                          op1=ALU.max)
                ins.then_inc(s_abf, 1)
                for hb in range(4):
                    v.wait_ge(s_at, qt * 4 + hb + 1)
                    v.tensor_copy(out=aTg[:, qi * S + hb * D: qi * S + (hb + 1) * D],
                                  in_=ptB[:, (hb % 2) * D:(hb % 2 + 1) * D]
                                  ).then_inc(s_atc, 1)
                if qi == 1:
                    for q2 in range(2):
                        o = qt - 1 + q2
                        v.wait_ge(s_av, 32 * (g2 + 1))
                        if o >= 2:
                            v.wait_ge(s_st[o % 2], 16 * (o // 2))
                        ob = outb[:, (o % 2) * D:(o % 2 + 1) * D]
                        v.tensor_copy(out=ob, in_=pav[:, q2 * D:(q2 + 1) * D])
                        v.tensor_mul(out=sqB2[:], in0=ob, in1=ob)
                        v.reduce_sum(out=co(C_OS1), in_=ob, axis=AX)
                        rsync(v, v.reduce_sum(out=co(C_OS2), in_=sqB2[:], axis=AX))
                        rsync(v, v.tensor_scalar(out=co(C_OMEAN), in0=co(C_OS1),
                                                 scalar1=1.0 / D, scalar2=None,
                                                 op0=ALU.mult))
                        rsync(v, v.tensor_mul(out=co(C_OVAR), in0=co(C_OMEAN),
                                              in1=co(C_OMEAN)))
                        rsync(v, v.scalar_tensor_tensor(out=co(C_OVAR), in0=co(C_OS2),
                                                        scalar=1.0 / D, in1=co(C_OVAR),
                                                        op0=ALU.mult, op1=ALU.subtract))
                        v.tensor_scalar(out=co(C_OVAR), in0=co(C_OVAR), scalar1=1e-5,
                                        scalar2=None, op0=ALU.add).then_inc(s_ovar, 1)
                        v.wait_ge(s_ostd, o + 1)
                        rsync(v, v.reciprocal(out=co(C_ORSTD), in_=co(C_OSTD)))
                        rsync(v, v.tensor_mul(out=co(C_MR), in0=co(C_OMEAN),
                                              in1=co(C_ORSTD)))
                        v.wait_ge(s_xro[o % 2], 16 * (o // 2 + 1))
                        v.tensor_scalar(out=ob, in0=ob, scalar1=co(C_ORSTD),
                                        scalar2=co(C_MR), op0=ALU.mult,
                                        op1=ALU.subtract)
                        v.tensor_add(out=ob, in0=ob,
                                     in1=xro[:, (o % 2) * D:(o % 2 + 1) * D]
                                     ).then_inc(s_oev, 1)

        @block.sync
        def _(sy):
            sy.wait_ge(s_mm, 128)
            sy.wait_ge(s_kev, 96)
            sy.wait_ge(s_tpc, NG)
            sy.wait_ge(s_vev, 32)
            sy.wait_ge(s_vst[0], 16 * 16)
            sy.wait_ge(s_vst[1], 16 * 16)
            sy.wait_ge(s_xld[0], 16 * 24)
            sy.wait_ge(s_xld[1], 16 * 24)
            sy.wait_ge(s_app, NG)
            for g2 in range(8):
                for jc in range(32):
                    li = g2 * 32 + jc
                    if li >= 2:
                        sy.wait_ge(s_av, li - 1)
                    sy.dma_start(out=vch[:, (jc % 2) * D:(jc % 2 + 1) * D],
                                 in_=vscr[jc * P:(jc + 1) * P, :]
                                 ).then_inc(s_vld[jc % 2], 16)
                for qi2 in range(2):
                    o = g2 * 2 + qi2
                    if o >= 2:
                        sy.wait_ge(s_oev, o - 1)
                    sy.dma_start(out=xro[:, (o % 2) * D:(o % 2 + 1) * D],
                                 in_=xo[o * P:(o + 1) * P, :]).then_inc(s_xro[o % 2], 16)
                    sy.wait_ge(s_oev, o + 1)
                    sy.dma_start(out=x1o[o * P:(o + 1) * P, :],
                                 in_=outb[:, (o % 2) * D:(o % 2 + 1) * D]
                                 ).then_inc(s_st[o % 2], 16)
            sy.wait_ge(s_st[0], 16 * 8)
            sy.wait_ge(s_st[1], 16 * 8)
            if debug:
                for nm, buf in (("d_kT", kT), ("d_qT", qT), ("d_scores", scores),
                                ("d_bs0", bs0), ("d_bs1", bs1), ("d_abf2", abf),
                                ("d_cols", cols), ("d_outb", outb), ("d_aTg", aTg)):
                    sy.dma_start(out=dbg[nm], in_=buf[:]).then_inc(s_st[0], 16)
                sy.wait_ge(s_st[0], 16 * 16)

        esB.close()
    es.close()
    return nc




def _prep_l0attn_inputs(inputs):
    x0 = np.ascontiguousarray(inputs["L0_attn_x"])
    wqt = np.ascontiguousarray(inputs["L0_attn_Wq"].T)
    wkt = np.ascontiguousarray(inputs["L0_attn_Wk"].T)
    wvt = np.ascontiguousarray(inputs["L0_attn_Wv"].T)
    idn = np.eye(P, dtype=BF16NP)
    maps = []
    for c in range(8):
        b, h = c // 2, c % 2
        maps.append({
            "xf": x0[b],
            "xo": np.ascontiguousarray(x0[b, h * TOWN:(h + 1) * TOWN]),
            "wqt": wqt, "wkt": wkt, "wvt": wvt, "idn": idn,
        })
    return maps


def run_l0attn(inputs):
    nc = build_l0attn()
    maps = _prep_l0attn_inputs(inputs)
    res = run_bass_kernel_spmd(nc, maps, core_ids=list(range(8)))
    x1 = np.empty((4, S, D), np.float32)
    for c in range(8):
        b, h = c // 2, c % 2
        x1[b, h * TOWN:(h + 1) * TOWN] = res.results[c]["x1o"]
    return x1


def kernel(**inputs):
    inputs = {k: np.asarray(v) for k, v in inputs.items()}
    x1 = run_l0attn(inputs)
    mid = run_l0ffn_layer(0, x1, inputs)
    ff = run_l0ffn_layer(1, mid, inputs)
    x3 = run_l1attn(x1, ff, inputs)
    xmid = run_l1ffn_attn(x3, inputs)
    x4 = run_l1moe(xmid, x3, inputs)
    return x4



NT_B = 21
NQ_B = 17
NF_B = 16
HID2 = 2816
W1W = 5632
FCW = [512, 512, 512, 512, 512, 256]


P1LVL = [5]


def build_l0ffn(apply_ln, upto=5, debug=False):
    p1lvl = P1LVL[0]
    nc = bass.Bass("TRN2", target_bir_lowering=False)
    yh = nc.dram_tensor("yh", [2688, D], F32, kind="ExternalInput").ap()
    wqkv = nc.dram_tensor("wqkv", [D, 1536], F32, kind="ExternalInput").ap()
    wo_ = nc.dram_tensor("wo_", [512, D], F32, kind="ExternalInput").ap()
    lngc = nc.dram_tensor("lngc", [P, 8], F32, kind="ExternalInput").ap()
    lnbc = nc.dram_tensor("lnbc", [P, 8], F32, kind="ExternalInput").ap()
    qs_d = nc.dram_tensor("qs", [1, 64], F32, kind="ExternalInput").ap()
    ks_d = nc.dram_tensor("ks", [1, 64], F32, kind="ExternalInput").ap()
    w1p = nc.dram_tensor("w1p", [D, W1W], F32, kind="ExternalInput").ap()
    b1p = nc.dram_tensor("b1p", [1, W1W], F32, kind="ExternalInput").ap()
    w2p = nc.dram_tensor("w2p", [HID2, D], F32, kind="ExternalInput").ap()
    b2r = nc.dram_tensor("b2r", [1, D], F32, kind="ExternalInput").ap()
    tabs_d = {nm: nc.dram_tensor(nm, [2688, 64], F32, kind="ExternalInput").ap()
              for nm in ("csq", "ssq", "csk", "ssk")}
    maskT = nc.dram_tensor("maskT", [NQ_B, 5 * P, P], F32, kind="ExternalInput").ap()
    idn_d = nc.dram_tensor("idn", [P, P], BF16, kind="ExternalInput").ap()
    msh_d = nc.dram_tensor("mshift", [P, P], F32, kind="ExternalInput").ap()
    mla_d = nc.dram_tensor("mlast", [P, P], F32, kind="ExternalInput").ap()
    onesr_d = nc.dram_tensor("onesr", [1, P], BF16, kind="ExternalInput").ap()
    yout = nc.dram_tensor("yout", [TOWN, D], F32, kind="ExternalOutput").ap()
    xf1scr = nc.dram_tensor("xf1scr", [NQ_B * P, D], F32, kind="Internal").ap()
    dbg = {}
    if debug:
        for nm, w, dt in (("dB_kTr", NT_B * 512, BF16), ("dB_qTr", NQ_B * 512, BF16),
                          ("dB_vtok", NT_B * 512, BF16), ("dB_xf1", NQ_B * 8 * P, BF16),
                          ("dB_acc", NF_B * D, F32), ("dB_r2", NF_B, F32),
                          ("dB_obf", NQ_B * 512, BF16), ("dB_den", NQ_B * 8, F32),
                          ("dB_slf", 5 * P, F32),
                          ("dB_xft", NQ_B * D, BF16),
                          ("dB_rows", NF_B * P, F32), ("dB_statc", NQ_B * 4, F32)):
            dbg[nm] = nc.dram_tensor(nm, [P, w], dt, kind="ExternalOutput").ap()
    s_dbg = nc.semaphore("sB_dbg").__enter__()

    es = ExitStack()
    sbt = lambda nm, shp, dt=F32: es.enter_context(nc.sbuf_tensor(nm, shp, dt))
    sem = lambda nm: es.enter_context(nc.semaphore(nm))

    idn = sbt("idnb", [P, P], BF16)
    msh = sbt("mshb", [P, P], F32)
    mla = sbt("mlab", [P, P], F32)
    cols = sbt("colsB", [P, 64])
    co = lambda i: cols[:, i:i + 1]
    (B_S1, B_S2, B_MEAN, B_VAR, B_STD, B_RSTD,
     B_MU2, B_EX2, B_VAR2, B_STD2, B_R2, B_NMU2, B_DEN, B_T1, B_T2) = range(15)
    B_NRM = 16
    B_SNRM = 32
    statml = sbt("statml", [P, NT_B * 2])
    statc = sbt("statc", [P, NQ_B * 4])
    r2col = sbt("r2col", [P, NF_B])
    lngb = sbt("lngb", [P, 8])
    lnbb = sbt("lnbb", [P, 8])
    gcolb = sbt("gcolb", [P, 8], BF16)
    lcolb = sbt("lcolb", [P, 8], BF16)
    onecb = sbt("onecb", [P, 1], BF16)
    onesr = sbt("onesrB", [1, P], BF16)
    xf1T = sbt("xf1TB", [P, 8 * NQ_B * P], BF16)
    rows2a = sbt("rows2aB", [1, NF_B * P], BF16)
    rows2b = sbt("rows2bB", [1, NF_B * P], BF16)
    rw2bf = sbt("rw2bfB", [P, 2], BF16)
    esR = ExitStack()
    qTr = esR.enter_context(nc.sbuf_tensor("qTrB", [P, NQ_B * 512], BF16))
    kTr = esR.enter_context(nc.sbuf_tensor("kTrB", [P, NT_B * 512], BF16))
    vtok = esR.enter_context(nc.sbuf_tensor("vtokB", [P, NT_B * 512], BF16))
    pbig = es.enter_context(nc.psum_tensor("pbigB", [P, 2048], F32))
    pex = es.enter_context(nc.psum_tensor("pexB", [P, 512], F32))
    ptp1 = es.enter_context(nc.psum_tensor("ptp1B", [P, D], BF16))
    prot = es.enter_context(nc.psum_tensor("protB", [P, D], BF16))

    s_w = sem("sB_w")
    s_ld = [sem("sB_ld0"), sem("sB_ld1")]
    s_var = sem("sB_var")
    s_std = sem("sB_std")
    s_rt = sem("sB_rt")
    s_app = sem("sB_app")
    s_tp = sem("sB_tp")
    s_tpc = sem("sB_tpc")
    s_qkv = sem("sB_qkv")
    s_qke = sem("sB_qke")
    s_nrm = sem("sB_nrm")
    s_nsq = sem("sB_nsq")
    s_rot = sem("sB_rot")
    s_rtp = sem("sB_rtp")
    s_rtc = sem("sB_rtc")
    rt_cnt = [0]
    rsync = _mk_rsync(s_rt, rt_cnt)

    es1 = ExitStack()
    sb1 = lambda nm, shp, dt=F32: es1.enter_context(nc.sbuf_tensor(nm, shp, dt))
    wqkv_b = sb1("wqkv_b", [P, 8 * 1536], BF16)
    ybuf = sb1("ybufB", [P, 2 * D])
    sqb = sb1("sqbB", [P, 512])
    a_bf = sb1("a_bfB", [P, 2 * D], BF16)
    yT = sb1("yTB", [P, 8 * 2688], BF16)
    qkvb = sb1("qkvbB", [P, 1536])
    qrot = sb1("qrotB", [P, 2 * 512], BF16)
    krot = sb1("krotB", [P, 2 * 512], BF16)
    tq1 = sb1("tq1B", [P, NT_B * 64], BF16)
    tq2 = sb1("tq2B", [P, NT_B * 64], BF16)
    tk1 = sb1("tk1B", [P, NT_B * 64], BF16)
    tk2 = sb1("tk2B", [P, NT_B * 64], BF16)
    qhat = sb1("qhatB", [P, 512], BF16)[:, :]
    rotb = sb1("rotbB", [P, 512], BF16)[:, :]
    qsx = sb1("qsxB", [P, 64], BF16)
    ksx = sb1("ksxB", [P, 64], BF16)

    pqkv = pbig[:, 0:1536]

    def yT_v(dc):
        return yT[:, dc * 2688:(dc + 1) * 2688]

    def qkv_lhsT(g, dc):
        if dc < 4:
            return yT_v(dc)[:, g * P:(g + 1) * P]
        lo = max(g * P - 1, 0)
        return yT_v(dc)[:, lo:lo + P]

    def _qkv_mm(t, g):
        t.wait_ge(s_tpc, g + 1)
        if g >= 1:
            t.wait_ge(s_qke, g)
        for nch in range(3):
            for dc in range(8):
                ins = t.matmul(out=pqkv[:, nch * 512:(nch + 1) * 512],
                               lhsT=qkv_lhsT(g, dc),
                               rhs=wqkv_b[:, dc * 1536 + nch * 512:
                                          dc * 1536 + (nch + 1) * 512],
                               start=(dc == 0), stop=(dc == 7))
        ins.then_inc(s_qkv, 1)

    def _rot_tp(t, g):
        t.wait_ge(s_rot, g + 1)
        if g >= 1:
            t.wait_ge(s_rtc, g)
        qb = (g % 2) * 512
        for bc in range(4):
            ins = t.transpose(out=prot[:, bc * P:(bc + 1) * P],
                              in_=krot[:, qb + bc * P:qb + (bc + 1) * P],
                              identity=idn[:])
        if g >= 4 and P1LVL[0] >= 4:
            for bc in range(4):
                ins = t.transpose(out=prot[:, 512 + bc * P:512 + (bc + 1) * P],
                                  in_=qrot[:, qb + bc * P:qb + (bc + 1) * P],
                                  identity=idn[:])
        ins.then_inc(s_rtp, 1)

    def _rot_dve(v, g):
        v.wait_ge(s_qkv, g + 1)
        qb = 0
        v.tensor_copy(out=qkvb[:, qb:qb + 1536], in_=pqkv[:]).then_inc(s_qke, 1)
        v.tensor_copy(out=vtok[:, g * 512:(g + 1) * 512],
                      in_=qkvb[:, qb + 1024:qb + 1536])
        v.tensor_mul(out=sqb[:, 0:512], in0=qkvb[:, qb + 512:qb + 1024],
                     in1=qkvb[:, qb + 512:qb + 1024])
        v.reduce_sum(out=cols[:, B_NRM + 8:B_NRM + 16],
                     in_=sqb[:, 0:512].rearrange("p (h c) -> p h c", h=8), axis=AX)
        v.tensor_mul(out=sqb[:, 0:512], in0=qkvb[:, qb:qb + 512],
                     in1=qkvb[:, qb:qb + 512])
        v.reduce_sum(out=cols[:, B_NRM:B_NRM + 8],
                     in_=sqb[:, 0:512].rearrange("p (h c) -> p h c", h=8),
                     axis=AX).then_inc(s_nrm, 1)
        v.wait_ge(s_nsq, g + 1)
        rsync(v, v.tensor_scalar(out=cols[:, B_SNRM:B_SNRM + 16],
                                 in0=cols[:, B_SNRM:B_SNRM + 16],
                                 scalar1=1e-12, scalar2=None, op0=ALU.max))
        rsync(v, v.reciprocal(out=cols[:, B_SNRM:B_SNRM + 16],
                              in_=cols[:, B_SNRM:B_SNRM + 16]))
        if P1LVL[0] < 2:
            v.nop().then_inc(s_rot, 1)
            return
        if g >= 2 and P1LVL[0] >= 3:
            v.wait_ge(s_rtp, g - 1)
        srcs = [(qkvb[:, qb + 512:qb + 1024], tk1, tk2, krot, 8)]
        if g >= 4:
            srcs.append((qkvb[:, qb:qb + 512], tq1, tq2, qrot, 0))
        for src, t1, t2, dst, nbo in srcs:
            nb = B_SNRM + nbo
            for h in range(8):
                v.tensor_scalar(out=qhat[:, h * 64:(h + 1) * 64] if False else qhat[:, h * 64:(h + 1) * 64],
                                in0=src[:, h * 64:(h + 1) * 64],
                                scalar1=cols[:, nb + h:nb + h + 1],
                                scalar2=None, op0=ALU.mult)
            qh3 = qhat.rearrange("p (h c) -> p h c", h=8)
            rb3 = rotb.rearrange("p (h c) -> p h c", h=8)
            v.tensor_scalar(out=rb3[:, :, 0:32], in0=qh3[:, :, 32:64],
                            scalar1=-1.0, scalar2=None, op0=ALU.mult)
            v.tensor_copy(out=rb3[:, :, 32:64], in_=qh3[:, :, 0:32])
            tt1 = t1[:, g * 64:(g + 1) * 64]
            tt2 = t2[:, g * 64:(g + 1) * 64]
            for h in range(8):
                v.tensor_mul(out=qh3[:, h, :], in0=qh3[:, h, :], in1=tt1)
                v.tensor_mul(out=rb3[:, h, :], in0=rb3[:, h, :], in1=tt2)
            ins = v.tensor_add(out=dst[:, (g % 2) * 512:(g % 2 + 1) * 512],
                               in0=qhat, in1=rotb)
        ins.then_inc(s_rot, 1)

    def _rtc_dve(v, g):
        v.wait_ge(s_rtp, g + 1)
        ins = v.tensor_copy(out=kTr[:, g * 512:(g + 1) * 512], in_=prot[:, 0:512])
        if g >= 4 and P1LVL[0] >= 5:
            for bc in range(4):
                ins = v.tensor_copy(
                    out=qTr[:, (g - 4) * 512 + bc * P:(g - 4) * 512 + (bc + 1) * P],
                    in_=prot[:, 512 + bc * P:512 + (bc + 1) * P])
        ins.then_inc(s_rtc, 1)

    def _ln_tile(v, g, yt, dest_bf):
        if apply_ln:
            v.tensor_mul(out=sqb[:], in0=yt[:, 0:512], in1=yt[:, 0:512])
            v.reduce_sum(out=co(B_S2), in_=sqb[:], axis=AX)
            v.tensor_mul(out=sqb[:], in0=yt[:, 512:1024], in1=yt[:, 512:1024])
            v.reduce_sum(out=co(B_EX2), in_=sqb[:], axis=AX)
            v.reduce_sum(out=co(B_S1), in_=yt, axis=AX)
            rsync(v, v.tensor_add(out=co(B_S2), in0=co(B_S2), in1=co(B_EX2)))
            rsync(v, v.tensor_scalar(out=co(B_MEAN), in0=co(B_S1),
                                     scalar1=1.0 / D, scalar2=None,
                                     op0=ALU.mult))
            rsync(v, v.tensor_mul(out=co(B_VAR), in0=co(B_MEAN), in1=co(B_MEAN)))
            rsync(v, v.scalar_tensor_tensor(out=co(B_VAR), in0=co(B_S2),
                                            scalar=1.0 / D, in1=co(B_VAR),
                                            op0=ALU.mult, op1=ALU.subtract))
            v.tensor_scalar(out=co(B_VAR), in0=co(B_VAR), scalar1=1e-5,
                            scalar2=None, op0=ALU.add).then_inc(s_var, 1)
            v.wait_ge(s_std, g + 1)
            rsync(v, v.reciprocal(out=co(B_RSTD), in_=co(B_STD)))
            v.tensor_copy(out=statml[:, g * 2:g * 2 + 1], in_=co(B_MEAN))
            v.tensor_copy(out=statml[:, g * 2 + 1:g * 2 + 2], in_=co(B_RSTD))
            v.tensor_scalar(out=dest_bf, in0=yt, scalar1=co(B_MEAN),
                            scalar2=co(B_RSTD), op0=ALU.subtract,
                            op1=ALU.mult).then_inc(s_app, 1)
        else:
            v.tensor_copy(out=dest_bf, in_=yt).then_inc(s_app, 1)

    with nc.Block() as block:
        @block.gpsimd
        def _(g):
            g.dma_start(out=idn[:], in_=idn_d).then_inc(s_w, 16)
            g.dma_start(out=msh[:], in_=msh_d).then_inc(s_w, 16)
            g.dma_start(out=mla[:], in_=mla_d).then_inc(s_w, 16)
            g.dma_start(out=wqkv_b[:, :].rearrange("p (dc c) -> p dc c", dc=8),
                        in_=wqkv.rearrange("(dc p) c -> p dc c", p=P)
                        ).then_inc(s_w, 16)
            for i, nm in enumerate(("csq", "ssq", "csk", "ssk")):
                dst = (tq1, tq2, tk1, tk2)[i]
                g.dma_start(out=dst[:, :].rearrange("p (t c) -> p t c", c=64),
                            in_=tabs_d[nm].rearrange("(t p) c -> p t c", p=P)
                            ).then_inc(s_w, 16)
            g.dma_start(out=qsx[0:1, :], in_=qs_d).then_inc(s_w, 16)
            g.dma_start(out=ksx[0:1, :], in_=ks_d).then_inc(s_w, 16)
            g.dma_start(out=lngb[:], in_=lngc).then_inc(s_w, 16)
            g.dma_start(out=lnbb[:], in_=lnbc).then_inc(s_w, 16)
            g.dma_start(out=onesr[:], in_=onesr_d).then_inc(s_w, 16)

        @block.sync
        def _(sy):
            for g in range(NT_B):
                if g >= 2:
                    if p1lvl >= 1:
                        sy.wait_ge(s_qke, g - 1)
                    else:
                        sy.wait_ge(s_app, g - 1)
                sy.dma_start(out=ybuf[:, (g % 2) * D:(g % 2 + 1) * D],
                             in_=yh[g * P:(g + 1) * P, :]).then_inc(s_ld[g % 2], 16)

        @block.vector
        def _(v):
            v.memset(onecb[:], 1.0)
            v.wait_ge(s_tp, 1)
            v.tensor_copy(out=qsx[:], in_=pbig[:, 1536:1600])
            v.tensor_copy(out=ksx[:], in_=pbig[:, 1600:1664])
            v.tensor_copy(out=qhat[:, 0:32], in_=qsx[:, 32:64])
            v.tensor_copy(out=qhat[:, 32:64], in_=qsx[:, 0:32])
            v.tensor_copy(out=rotb[:, 0:32], in_=ksx[:, 32:64])
            v.tensor_copy(out=rotb[:, 32:64], in_=ksx[:, 0:32])
            for tt in range(NT_B):
                sl = slice(tt * 64, (tt + 1) * 64)
                v.tensor_mul(out=tq1[:, sl], in0=tq1[:, sl], in1=qsx[:])
                v.tensor_mul(out=tq2[:, sl], in0=tq2[:, sl], in1=qhat[:, 0:64])
                v.tensor_mul(out=tk1[:, sl], in0=tk1[:, sl], in1=ksx[:])
                v.tensor_mul(out=tk2[:, sl], in0=tk2[:, sl], in1=rotb[:, 0:64])
            v.tensor_copy(out=gcolb[:], in_=lngb[:])
            v.tensor_copy(out=lcolb[:], in_=lnbb[:]).then_inc(s_tpc, 0) if False \
                else v.tensor_copy(out=lcolb[:], in_=lnbb[:])
            for g in range(NT_B):
                if g >= 1:
                    gp = g - 1
                    v.wait_ge(s_tp, gp + 2)
                    v.tensor_copy(
                        out=yT[:, :].rearrange("p (dc t) -> p dc t", dc=8)[
                            :, :, gp * P:(gp + 1) * P],
                        in_=ptp1[:, :].rearrange("p (dc t) -> p dc t", dc=8)
                    ).then_inc(s_tpc, 1)
                    if p1lvl >= 1:
                        _rot_dve(v, gp)
                b = (g % 2) * D
                yt = ybuf[:, b:b + D]
                v.wait_ge(s_ld[g % 2], 16 * (g // 2 + 1))
                _ln_tile(v, g, yt, a_bf[:, b:b + D])
                if g >= 2 and p1lvl >= 3:
                    _rtc_dve(v, g - 2)
            gp = NT_B - 1
            v.wait_ge(s_tp, gp + 2)
            v.tensor_copy(
                out=yT[:, :].rearrange("p (dc t) -> p dc t", dc=8)[
                    :, :, gp * P:(gp + 1) * P],
                in_=ptp1[:, :].rearrange("p (dc t) -> p dc t", dc=8)
            ).then_inc(s_tpc, 1)
            if p1lvl >= 1:
                _rot_dve(v, gp)
            if p1lvl >= 3:
                _rtc_dve(v, NT_B - 2)
                _rtc_dve(v, NT_B - 1)

        @block.scalar
        def _(s):
            for g in range(NT_B):
                if apply_ln:
                    s.wait_ge(s_var, g + 1)
                    s.activation(out=co(B_STD), in_=co(B_VAR),
                                 func=AF.Sqrt).then_inc(s_std, 1)
                if p1lvl >= 1:
                    s.wait_ge(s_nrm, g + 1)
                    s.activation(out=cols[:, B_SNRM:B_SNRM + 16],
                                 in_=cols[:, B_NRM:B_NRM + 16],
                                 func=AF.Sqrt).then_inc(s_nsq, 1)

        @block.tensor
        def _(t):
            t.wait_ge(s_w, 208)
            t.matmul(out=pbig[:, 1536:1600], lhsT=onesr[0:1, :],
                     rhs=qsx[0:1, :], start=True, stop=False)
            t.matmul(out=pbig[:, 1600:1664], lhsT=onesr[0:1, :],
                     rhs=ksx[0:1, :], start=False, stop=True).then_inc(s_tp, 1)
            for g in range(NT_B):
                if g >= 1 and p1lvl >= 1:
                    _qkv_mm(t, g - 1)
                t.wait_ge(s_app, g + 1)
                if g >= 1:
                    t.wait_ge(s_tpc, g)
                b = (g % 2) * D
                for dc in range(8):
                    ins = t.transpose(out=ptp1[:, dc * P:(dc + 1) * P],
                                      in_=a_bf[:, b + dc * P: b + (dc + 1) * P],
                                      identity=idn[:])
                ins.then_inc(s_tp, 1)
                if g >= 2 and p1lvl >= 3:
                    _rot_tp(t, g - 2)
            if p1lvl >= 1:
                _qkv_mm(t, NT_B - 1)
            if p1lvl >= 3:
                _rot_tp(t, NT_B - 2)
                _rot_tp(t, NT_B - 1)

        if upto <= 1:
            @block.sync
            def _(sy):
                p1x = [(s_tpc, NT_B), (s_app, NT_B)]
                if p1lvl >= 1:
                    p1x += [(s_qkv, NT_B), (s_qke, NT_B), (s_rot, NT_B),
                            (s_nsq, NT_B)]
                if p1lvl >= 3:
                    p1x += [(s_rtc, NT_B)]
                for sm, vv in p1x:
                    sy.wait_ge(sm, vv)
                n = 0
                if debug:
                    for nm, buf in (("dB_kTr", kTr), ("dB_qTr", qTr), ("dB_vtok", vtok)):
                        sy.dma_start(out=dbg[nm], in_=buf[:]).then_inc(s_dbg, 16)
                        n += 16
                sy.dma_start(out=yout[0:P, :], in_=ybuf[:, 0:D]).then_inc(s_dbg, 16)
                sy.wait_ge(s_dbg, n + 16)
            es1.close()
            esR.close()
            es.close()
            return nc
        es1.close()
        es2 = ExitStack()
        sb2 = lambda nm, shp, dt=F32: es2.enter_context(nc.sbuf_tensor(nm, shp, dt))
        wo_b = sb2("wo_bB", [P, 4 * D], BF16)
        mkb = sb2("mkbB", [P, 2 * 5 * P])
        slf = sb2("slfB", [P, 2 * P])
        slb = sb2("slbB", [P, 5 * P], BF16)
        o_bf = sb2("o_bfB", [P, 512], BF16)
        oT = sb2("oTB", [P, 512], BF16)
        ybuf2 = sb2("ybuf2B", [P, 2 * D])
        sqb2 = sb2("sqb2B", [P, D])
        xf1t = sb2("xf1tB", [P, 2 * D])
        xf1bf = sb2("xf1bfB", [P, D], BF16)
        if debug:
            dbg_slf = sb2("dbgslfB", [P, 5 * P])
            dbg_obf = sb2("dbgobfB", [P, NQ_B * 512], BF16)
            dbg_den = sb2("dbgdenB", [P, NQ_B * 8])
            dbg_xft = sb2("dbgxftB", [P, NQ_B * D], BF16)

        pS2h = [pex[:, 0:128], pbig[:, 0:128]]
        po2 = pbig[:, 512:640]
        pwo2 = pbig[:, 1024:2048]

        s_wo_ld = sem("sB_wold")
        s_mk = [sem("sB_mk0"), sem("sB_mk1")]
        s_ld2 = [sem("sB_ld2a"), sem("sB_ld2b")]
        s_S2 = sem("sB_S2")
        s_msk = sem("sB_msk")
        s_exp = sem("sB_exp")
        s_pv = sem("sB_pv")
        s_oev = sem("sB_oev")
        s_otp = sem("sB_otp")
        s_otc = sem("sB_otc")
        s_wo2 = sem("sB_wo2")
        s_xft = sem("sB_xft")
        s_xtp = sem("sB_xtp")
        s_xtc = sem("sB_xtc")
        s_xfs = [sem("sB_xfs0"), sem("sB_xfs1")]

        def p1gate(e):
            e.wait_ge(s_qkv, NT_B)
            e.wait_ge(s_qke, NT_B)
            e.wait_ge(s_tpc, NT_B)
            e.wait_ge(s_rtc, NT_B)
            e.wait_ge(s_rot, NT_B)
            e.wait_ge(s_app, NT_B)
            e.wait_ge(s_nsq, NT_B)
            e.wait_ge(s_ld[0], 16 * 11)
            e.wait_ge(s_ld[1], 16 * 10)

        @block.gpsimd
        def _(g):
            p1gate(g)
            g.dma_start(out=wo_b[:, :].rearrange("p (dc c) -> p dc c", dc=4),
                        in_=wo_.rearrange("(dc p) c -> p dc c", p=P)
                        ).then_inc(s_wo_ld, 16)

        @block.sync
        def _(sy):
            p1gate(sy)
            for qt in range(NQ_B):
                if qt >= 2:
                    sy.wait_ge(s_msk, (qt - 1) * 40)
                sy.dma_start(out=mkb[:, (qt % 2) * 640:(qt % 2 + 1) * 640]
                             .rearrange("p (c q) -> p c q", c=5),
                             in_=maskT[qt].rearrange("(c p) q -> p c q", p=P)
                             ).then_inc(s_mk[qt % 2], 16)
                if qt >= 2:
                    sy.wait_ge(s_xft, qt - 1)
                sy.dma_start(out=ybuf2[:, (qt % 2) * D:(qt % 2 + 1) * D],
                             in_=yh[(qt + 4) * P:(qt + 5) * P, :]
                             ).then_inc(s_ld2[qt % 2], 16)
                sy.wait_ge(s_xft, qt + 1)
                sy.dma_start(out=xf1scr[qt * P:(qt + 1) * P, :],
                             in_=xf1t[:, (qt % 2) * D:(qt % 2 + 1) * D]
                             ).then_inc(s_xfs[qt % 2], 16)

        @block.tensor
        def _(t):
            t.wait_ge(s_wo_ld, 16)
            for qt in range(NQ_B):
                t.wait_ge(s_mk[qt % 2], 16 * (qt // 2 + 1))
                for h in range(8):
                    hb = (h % 2) * 64
                    hc = (h // 2) * P
                    for jt in range(5):
                        idx = (qt * 8 + h) * 5 + jt
                        if idx >= 2:
                            t.wait_ge(s_msk, idx - 1)
                        ins = t.matmul(
                            out=pS2h[jt % 2],
                            lhsT=kTr[hb:hb + 64, (qt + jt) * 512 + hc:
                                     (qt + jt) * 512 + hc + P],
                            rhs=qTr[hb:hb + 64, qt * 512 + hc: qt * 512 + hc + P],
                            start=True, stop=True)
                        ins.then_inc(s_S2, 1)
                    if qt * 8 + h >= 1:
                        t.wait_ge(s_oev, qt * 8 + h)
                    for jt in range(5):
                        idx = (qt * 8 + h) * 5 + jt
                        t.wait_ge(s_exp, idx + 1)
                        t.matmul(out=po2[:, 0:64],
                                 lhsT=slb[:, jt * P:(jt + 1) * P],
                                 rhs=vtok[:, (qt + jt) * 512 + h * 64:
                                          (qt + jt) * 512 + h * 64 + 64],
                                 start=(jt == 0), stop=False)
                        ins = t.matmul(out=po2[:, 64:65],
                                       lhsT=slb[:, jt * P:(jt + 1) * P],
                                       rhs=onecb[:, 0:1],
                                       start=False, stop=(jt == 4))
                    ins.then_inc(s_pv, 1)
                t.wait_ge(s_oev, qt * 8 + 8)
                if qt >= 1:
                    t.wait_ge(s_otc, qt)
                for bc in range(4):
                    ins = t.transpose(out=ptp1[:, bc * P:(bc + 1) * P],
                                      in_=o_bf[:, bc * P:(bc + 1) * P],
                                      identity=idn[:])
                ins.then_inc(s_otp, 1)
                t.wait_ge(s_otc, qt + 1)
                if qt >= 1:
                    t.wait_ge(s_xft, qt)
                for dc in range(4):
                    for nh in range(2):
                        ins = t.matmul(
                            out=pwo2[:, nh * 512:(nh + 1) * 512],
                            lhsT=oT[:, dc * P:(dc + 1) * P],
                            rhs=wo_b[:, dc * D + nh * 512: dc * D + (nh + 1) * 512],
                            start=(dc == 0), stop=(dc == 3))
                ins.then_inc(s_wo2, 1)
                t.wait_ge(s_xft, qt + 1)
                if qt >= 1:
                    t.wait_ge(s_xtc, qt)
                for dc in range(8):
                    ins = t.transpose(out=prot[:, dc * P:(dc + 1) * P],
                                      in_=xf1bf[:, dc * P:(dc + 1) * P],
                                      identity=idn[:])
                ins.then_inc(s_xtp, 1)

        @block.vector
        def _(v):
            p1gate(v)
            for qt in range(NQ_B):
                v.wait_ge(s_mk[qt % 2], 16 * (qt // 2 + 1))
                for h in range(8):
                    for jt in range(5):
                        idx = (qt * 8 + h) * 5 + jt
                        v.wait_ge(s_S2, idx + 1)
                        if idx >= 2:
                            v.wait_ge(s_exp, idx - 1)
                        v.tensor_add(out=slf[:, (jt % 2) * P:(jt % 2 + 1) * P],
                                     in0=pS2h[jt % 2],
                                     in1=mkb[:, (qt % 2) * 640 + jt * P:
                                             (qt % 2) * 640 + (jt + 1) * P]
                                     ).then_inc(s_msk, 1)
                        if debug and qt == 3 and h == 0:
                            v.tensor_copy(out=dbg_slf[:, jt * P:(jt + 1) * P],
                                          in_=slf[:, (jt % 2) * P:(jt % 2 + 1) * P])
                    v.wait_ge(s_pv, qt * 8 + h + 1)
                    rsync(v, v.tensor_copy(out=co(B_DEN), in_=po2[:, 64:65]))
                    rsync(v, v.reciprocal(out=co(B_DEN), in_=co(B_DEN)))
                    v.tensor_scalar(out=o_bf[:, h * 64:(h + 1) * 64],
                                    in0=po2[:, 0:64], scalar1=co(B_DEN),
                                    scalar2=None, op0=ALU.mult).then_inc(s_oev, 1)
                    if debug:
                        v.tensor_copy(out=dbg_den[:, qt * 8 + h:qt * 8 + h + 1],
                                      in_=co(B_DEN))
                        v.tensor_copy(out=dbg_obf[:, qt * 512 + h * 64:
                                                  qt * 512 + (h + 1) * 64],
                                      in_=o_bf[:, h * 64:(h + 1) * 64])
                v.wait_ge(s_otp, qt + 1)
                for bc in range(4):
                    ins = v.tensor_copy(out=oT[:, bc * P:(bc + 1) * P],
                                        in_=ptp1[:, bc * P:(bc + 1) * P])
                ins.then_inc(s_otc, 1)
                xb = (qt % 2) * D
                v.wait_ge(s_ld2[qt % 2], 16 * (qt // 2 + 1))
                v.wait_ge(s_wo2, qt + 1)
                if qt >= 2:
                    v.wait_ge(s_xfs[qt % 2], 16 * ((qt - 2) // 2 + 1))
                yt2 = ybuf2[:, xb:xb + D]
                dst = xf1t[:, xb:xb + D]
                if apply_ln:
                    g_ = qt + 4
                    v.tensor_scalar(out=dst, in0=yt2,
                                    scalar1=statml[:, g_ * 2:g_ * 2 + 1],
                                    scalar2=statml[:, g_ * 2 + 1:g_ * 2 + 2],
                                    op0=ALU.subtract, op1=ALU.mult)
                    v.tensor_add(out=dst, in0=dst, in1=pwo2[:])
                else:
                    v.tensor_add(out=dst, in0=yt2, in1=pwo2[:])
                v.tensor_mul(out=sqb2[:], in0=dst, in1=dst)
                v.reduce_sum(out=statc[:, qt * 4:qt * 4 + 1], in_=dst[:, 0:512], axis=AX)
                v.reduce_sum(out=statc[:, qt * 4 + 1:qt * 4 + 2], in_=dst[:, 512:1024], axis=AX)
                v.reduce_sum(out=statc[:, qt * 4 + 2:qt * 4 + 3], in_=sqb2[:, 0:512], axis=AX)
                v.reduce_sum(out=statc[:, qt * 4 + 3:qt * 4 + 4], in_=sqb2[:, 512:1024], axis=AX)
                if debug:
                    v.tensor_copy(out=dbg_xft[:, qt * D:(qt + 1) * D], in_=dst)
                v.tensor_copy(out=xf1bf[:], in_=dst).then_inc(s_xft, 1)
                v.wait_ge(s_xtp, qt + 1)
                v.tensor_copy(
                    out=xf1T[:, :].rearrange("p (dc t) -> p dc t", dc=8)[
                        :, :, qt * P:(qt + 1) * P],
                    in_=prot[:, :].rearrange("p (dc t) -> p dc t", dc=8)
                ).then_inc(s_xtc, 1)

        @block.scalar
        def _(s):
            p1gate(s)
            for qt in range(NQ_B):
                for h in range(8):
                    hh = qt * 8 + h
                    if hh >= 1:
                        s.wait_ge(s_pv, hh)
                    for jt in range(5):
                        idx = hh * 5 + jt
                        s.wait_ge(s_msk, idx + 1)
                        s.activation(out=slb[:, jt * P:(jt + 1) * P],
                                     in_=slf[:, (jt % 2) * P:(jt % 2 + 1) * P],
                                     func=AF.Exp).then_inc(s_exp, 1)

        if upto <= 2:
            @block.sync
            def _(sy):
                sy.wait_ge(s_xtc, NQ_B)
                sy.wait_ge(s_xfs[0], 16 * 9)
                sy.wait_ge(s_xfs[1], 16 * 8)
                n = 0
                if debug:
                    for nm2, buf2 in (("dB_kTr", kTr), ("dB_qTr", qTr),
                                      ("dB_vtok", vtok)):
                        sy.dma_start(out=dbg[nm2], in_=buf2[:]).then_inc(s_dbg, 16)
                    n += 48
                    sy.dma_start(out=dbg["dB_xf1"], in_=xf1T[:]).then_inc(s_dbg, 16)
                    sy.dma_start(out=dbg["dB_obf"], in_=dbg_obf[:]).then_inc(s_dbg, 16)
                    sy.dma_start(out=dbg["dB_den"], in_=dbg_den[:]).then_inc(s_dbg, 16)
                    sy.dma_start(out=dbg["dB_xft"], in_=dbg_xft[:]).then_inc(s_dbg, 16)
                    sy.dma_start(out=dbg["dB_slf"], in_=dbg_slf[:]).then_inc(s_dbg, 16)
                    n += 64
                sy.dma_start(out=yout[0:P, :], in_=xf1t[:, 0:D]).then_inc(s_dbg, 16)
                sy.wait_ge(s_dbg, n + 16)
            es2.close()
            esR.close()
            es.close()
            return nc
        s_sh = sem("sB_sh")
        s_sv2 = sem("sB_sv2")
        s_ss2 = sem("sB_ss2")
        s_sr2 = sem("sB_sr2")
        s_sc2 = sem("sB_sc2")

        def _st_tp(t, f):
            t.wait_ge(s_sr2, f + 1)
            if f >= 1:
                t.wait_ge(s_sc2, f)
            t.transpose(out=prot[0:1, 0:P], in_=rw2bf[:, 0:1], identity=idn[:])
            ins = t.transpose(out=prot[0:1, P:2 * P], in_=rw2bf[:, 1:2],
                              identity=idn[:])
            ins.then_inc(s_otp, 1)

        @block.tensor
        def _(t):
            t.wait_ge(s_xtc, NQ_B)
            t.wait_ge(s_xfs[0], 16 * 9)
            t.wait_ge(s_xfs[1], 16 * 8)
            pst3 = pbig[:, 512:516]
            for f in range(NF_B):
                if f >= 1:
                    t.wait_ge(s_sv2, f)
                t.matmul(out=pst3, lhsT=msh[:], rhs=statc[:, (f + 1) * 4:(f + 2) * 4],
                         start=True, stop=False)
                t.matmul(out=pst3, lhsT=mla[:], rhs=statc[:, f * 4:(f + 1) * 4],
                         start=False, stop=True).then_inc(s_sh, 1)
                if f >= 1:
                    _st_tp(t, f - 1)
            _st_tp(t, NF_B - 1)

        @block.vector
        def _(v):
            for f in range(NF_B):
                if f >= 1:
                    v.wait_ge(s_otp, NQ_B + f)
                    v.tensor_copy(out=rows2a[0:1, (f - 1) * P:f * P],
                                  in_=prot[0:1, 0:P])
                    v.tensor_copy(out=rows2b[0:1, (f - 1) * P:f * P],
                                  in_=prot[0:1, P:2 * P]).then_inc(s_sc2, 1)
                v.wait_ge(s_sh, f + 1)
                pst3 = pbig[:, 512:516]
                rsync(v, v.tensor_add(out=co(B_MU2),
                                      in0=statc[:, (f + 1) * 4:(f + 1) * 4 + 1],
                                      in1=pst3[:, 1:2]))
                rsync(v, v.tensor_add(out=co(B_EX2),
                                      in0=statc[:, (f + 1) * 4 + 2:(f + 1) * 4 + 3],
                                      in1=pst3[:, 3:4]))
                rsync(v, v.tensor_scalar(out=co(B_MU2), in0=co(B_MU2),
                                         scalar1=1.0 / D, scalar2=None, op0=ALU.mult))
                rsync(v, v.tensor_mul(out=co(B_VAR2), in0=co(B_MU2), in1=co(B_MU2)))
                rsync(v, v.scalar_tensor_tensor(out=co(B_VAR2), in0=co(B_EX2),
                                                scalar=1.0 / D, in1=co(B_VAR2),
                                                op0=ALU.mult, op1=ALU.subtract))
                v.tensor_scalar(out=co(B_VAR2), in0=co(B_VAR2), scalar1=1e-5,
                                scalar2=None, op0=ALU.add).then_inc(s_sv2, 1)
                v.wait_ge(s_ss2, f + 1)
                rsync(v, v.reciprocal(out=co(B_R2), in_=co(B_STD2)))
                v.tensor_copy(out=r2col[:, f:f + 1], in_=co(B_R2))
                rsync(v, v.tensor_scalar(out=co(B_NMU2), in0=co(B_MU2),
                                         scalar1=-1.0, scalar2=None, op0=ALU.mult))
                v.tensor_copy(out=rw2bf[:, 0:1], in_=co(B_NMU2))
                v.tensor_copy(out=rw2bf[:, 1:2], in_=co(B_STD2)).then_inc(s_sr2, 1)
            v.wait_ge(s_otp, NQ_B + NF_B)
            v.tensor_copy(out=rows2a[0:1, (NF_B - 1) * P:NF_B * P],
                          in_=prot[0:1, 0:P])
            v.tensor_copy(out=rows2b[0:1, (NF_B - 1) * P:NF_B * P],
                          in_=prot[0:1, P:2 * P]).then_inc(s_sc2, 1)

        @block.scalar
        def _(s):
            for f in range(NF_B):
                s.wait_ge(s_sv2, f + 1)
                s.activation(out=co(B_STD2), in_=co(B_VAR2),
                             func=AF.Sqrt).then_inc(s_ss2, 1)

        if upto <= 3:
            @block.sync
            def _(sy):
                sy.wait_ge(s_sc2, NF_B)
                n = 0
                if debug:
                    sy.dma_start(out=dbg["dB_r2"], in_=r2col[:]).then_inc(s_dbg, 16)
                    sy.dma_start(out=dbg["dB_statc"], in_=statc[:]).then_inc(s_dbg, 16)
                    n += 32
                sy.dma_start(out=yout[0:P, :], in_=ybuf2[:, 0:D]).then_inc(s_dbg, 16)
                sy.wait_ge(s_dbg, n + 16)
            es2.close()
            esR.close()
            es.close()
            return nc
        es2.close()
        esR.close()
        es4 = ExitStack()
        sb4 = lambda nm, shp, dt=F32: es4.enter_context(nc.sbuf_tensor(nm, shp, dt))
        gw1a = sb4("gw1aB", [P, 8 * 512], BF16)
        gw1g = sb4("gw1gB", [P, 8 * 512], BF16)
        w2c = sb4("w2cB", [P, 4 * D], BF16)
        acc_sb = sb4("accB", [P, NF_B * D])
        a_sb = sb4("a_sbB", [P, 512])
        g_sb = sb4("g_sbB", [P, 512])
        gel = sb4("gelB", [P, 512])
        mid = sb4("midB", [P, 512], BF16)
        midT = sb4("midTB", [P, 512], BF16)
        browa = sb4("browaB", [1, 512], BF16)
        crowa = sb4("crowaB", [1, 512], BF16)
        browg = sb4("browgB", [1, 512], BF16)
        crowg = sb4("crowgB", [1, 512], BF16)
        crf = sb4("crfB", [1, 2 * 512])
        b1c = sb4("b1cB", [1, 2 * 512])
        b2rw = sb4("b2rwB", [1, D], BF16)
        onerw = sb4("onerwB", [1, P], BF16)
        xrr = sb4("xrrB", [P, 2 * D])
        youtb = sb4("youtbB", [P, 2 * D])

        psA = pbig[:, 0:512]
        psG = pbig[:, 512:1024]
        pw2 = pbig[:, 1024:2048]
        pbc = pbig[0:1, 1536:2048]

        s_w4 = sem("sB_w4")
        s_r2a = sem("sB_r2a")
        s_xg = sem("sB_xg")
        s_bc = sem("sB_bc")
        s_bcc = sem("sB_bcc")
        s_h1 = sem("sB_h1")
        s_ev4 = sem("sB_ev4")
        s_gel = sem("sB_gel")
        s_mid = sem("sB_mid")
        s_mtp = sem("sB_mtp")
        s_mtc = sem("sB_mtc")
        s_w2c = sem("sB_w2c")
        s_acc = sem("sB_acc")
        s_xrl = [sem("sB_xrl0"), sem("sB_xrl1")]
        s_yst = [sem("sB_yst0"), sem("sB_yst1")]
        s_fin = sem("sB_fin")

        NCI = 6
        CW = FCW

        def coff(ci):
            return sum(CW[:ci])

        def p3gate(e):
            e.wait_ge(s_sc2, NF_B)
            e.wait_ge(s_xtc, NQ_B)

        @block.gpsimd
        def _(g):
            p3gate(g)
            g.dma_start(out=b2rw[:], in_=b2r).then_inc(s_w4, 16)
            for ci in range(NCI):
                w = CW[ci]
                if ci >= 1:
                    g.wait_ge(s_h1, NF_B * ci)
                    g.wait_ge(s_w2c, NF_B * ci)
                    g.wait_ge(s_bcc, 4 * ci)
                co_ = coff(ci)
                g.dma_start(out=gw1a[:, :].rearrange("p (dc w) -> p dc w", dc=8)[:, :, 0:w],
                            in_=w1p[:, co_:co_ + w].rearrange("(dc p) w -> p dc w", p=P)
                            ).then_inc(s_w4, 16)
                g.dma_start(out=gw1g[:, :].rearrange("p (dc w) -> p dc w", dc=8)[:, :, 0:w],
                            in_=w1p[:, HID2 + co_:HID2 + co_ + w].rearrange(
                                "(dc p) w -> p dc w", p=P)).then_inc(s_w4, 16)
                g.dma_start(out=w2c[:, :].rearrange("p (fs d) -> p fs d", fs=4)[
                                :, 0:w // P, :],
                            in_=w2p[co_:co_ + w, :].rearrange("(fs p) d -> p fs d", p=P)
                            ).then_inc(s_w4, 16)
                g.dma_start(out=b1c[0:1, 0:w], in_=b1p[0:1, co_:co_ + w]
                            ).then_inc(s_w4, 16)
                g.dma_start(out=b1c[0:1, 512:512 + w],
                            in_=b1p[0:1, HID2 + co_:HID2 + co_ + w]).then_inc(s_w4, 16)

        @block.vector
        def _(v):
            p3gate(v)
            v.memset(onerw[:], 1.0)
            for dc in range(8):
                ins = v.tensor_scalar(
                    out=xf1T[:, dc * (NQ_B * P):(dc + 1) * (NQ_B * P)],
                    in0=xf1T[:, dc * (NQ_B * P):(dc + 1) * (NQ_B * P)],
                    scalar1=lngb[:, dc:dc + 1], scalar2=None, op0=ALU.mult)
            ins.then_inc(s_xg, 1)
            for ci in range(NCI):
                w = CW[ci]
                for st, (dst, addb, boff) in enumerate(
                        ((browa, False, 0), (crowa, True, 0),
                         (browg, False, 512), (crowg, True, 512))):
                    v.wait_ge(s_bc, ci * 4 + st + 1)
                    if addb:
                        v.tensor_add(out=crf[0:1, 0:w], in0=pbc[0:1, 0:w],
                                     in1=b1c[0:1, boff:boff + w])
                        v.tensor_copy(out=dst[0:1, 0:w],
                                      in_=crf[0:1, 0:w]).then_inc(s_bcc, 1)
                    else:
                        v.tensor_copy(out=dst[0:1, 0:w],
                                      in_=pbc[0:1, 0:w]).then_inc(s_bcc, 1)
                for f in range(NF_B):
                    it = ci * NF_B + f
                    v.wait_ge(s_h1, it + 1)
                    v.tensor_scalar(out=a_sb[:, 0:w], in0=psA[:, 0:w],
                                    scalar1=r2col[:, f:f + 1], scalar2=None,
                                    op0=ALU.mult)
                    v.tensor_scalar(out=g_sb[:, 0:w], in0=psG[:, 0:w],
                                    scalar1=r2col[:, f:f + 1], scalar2=None,
                                    op0=ALU.mult).then_inc(s_ev4, 1)
                    v.wait_ge(s_gel, it + 1)
                    v.tensor_mul(out=mid[:, 0:w], in0=a_sb[:, 0:w],
                                 in1=gel[:, 0:w]).then_inc(s_mid, 1)
                    v.wait_ge(s_mtp, it + 1)
                    for bc in range(w // P):
                        ins = v.tensor_copy(out=midT[:, bc * P:(bc + 1) * P],
                                            in_=ptp1[:, bc * P:(bc + 1) * P])
                    ins.then_inc(s_mtc, 1)
                    v.wait_ge(s_w2c, it + 1)
                    acc = acc_sb[:, f * D:(f + 1) * D]
                    if ci == 0:
                        v.tensor_copy(out=acc, in_=pw2[:]).then_inc(s_acc, 1)
                    else:
                        v.tensor_add(out=acc, in0=acc, in1=pw2[:]).then_inc(s_acc, 1)
            for f in range(NF_B):
                v.wait_ge(s_xrl[f % 2], 16 * (f // 2 + 1))
                if f >= 2:
                    v.wait_ge(s_yst[f % 2], 16 * (f // 2))
                v.tensor_add(out=youtb[:, (f % 2) * D:(f % 2 + 1) * D],
                             in0=acc_sb[:, f * D:(f + 1) * D],
                             in1=xrr[:, (f % 2) * D:(f % 2 + 1) * D]
                             ).then_inc(s_fin, 1)

        @block.tensor
        def _(t):
            p3gate(t)
            t.wait_ge(s_xg, 1)
            for ci in range(NCI):
                w = CW[ci]
                t.wait_ge(s_w4, 16 + (ci + 1) * 80)
                if ci >= 1:
                    t.wait_ge(s_acc, NF_B * ci)
                for st, (lcol, srcb) in enumerate(
                        ((gcolb, gw1a), (lcolb, gw1a), (gcolb, gw1g), (lcolb, gw1g))):
                    if ci * 4 + st >= 1:
                        t.wait_ge(s_bcc, ci * 4 + st)
                    for dc in range(8):
                        ins = t.matmul(out=pbc[0:1, 0:w],
                                       lhsT=lcol[:, dc:dc + 1],
                                       rhs=srcb[:, dc * 512:dc * 512 + w],
                                       start=(dc == 0), stop=(dc == 7))
                    ins.then_inc(s_bc, 1)
                t.wait_ge(s_bcc, ci * 4 + 4)
                for f in range(NF_B):
                    it = ci * NF_B + f
                    if it >= 1:
                        t.wait_ge(s_ev4, it)
                    tok = (f + 1) * P
                    for half, (ps, brow, crow, gsrc) in enumerate(
                            ((psA, browa, crowa, gw1a), (psG, browg, crowg, gw1g))):
                        for dc in range(8):
                            off = tok if dc < 4 else tok - 1
                            t.matmul(out=ps[:, 0:w],
                                     lhsT=xf1T[:, dc * (NQ_B * P) + off:
                                               dc * (NQ_B * P) + off + P],
                                     rhs=gsrc[:, dc * 512:dc * 512 + w],
                                     start=(dc == 0), stop=False)
                        t.matmul(out=ps[:, 0:w],
                                 lhsT=rows2a[0:1, f * P:(f + 1) * P],
                                 rhs=brow[0:1, 0:w], start=False, stop=False)
                        ins = t.matmul(out=ps[:, 0:w],
                                       lhsT=rows2b[0:1, f * P:(f + 1) * P],
                                       rhs=crow[0:1, 0:w], start=False, stop=True)
                    ins.then_inc(s_h1, 1)
                    t.wait_ge(s_mid, it + 1)
                    if it >= 1:
                        t.wait_ge(s_mtc, it)
                    for fs in range(w // P):
                        ins = t.transpose(out=ptp1[:, fs * P:(fs + 1) * P],
                                          in_=mid[:, fs * P:(fs + 1) * P],
                                          identity=idn[:])
                    ins.then_inc(s_mtp, 1)
                    t.wait_ge(s_mtc, it + 1)
                    if it >= 1:
                        t.wait_ge(s_acc, it)
                    nfs = w // P
                    for fs in range(nfs):
                        for nh in range(2):
                            ins = t.matmul(
                                out=pw2[:, nh * 512:(nh + 1) * 512],
                                lhsT=midT[:, fs * P:(fs + 1) * P],
                                rhs=w2c[:, fs * D + nh * 512: fs * D + (nh + 1) * 512],
                                start=(fs == 0),
                                stop=(fs == nfs - 1 and ci != NCI - 1))
                    if ci == NCI - 1:
                        for nh in range(2):
                            ins = t.matmul(out=pw2[:, nh * 512:(nh + 1) * 512],
                                           lhsT=onerw[0:1, :],
                                           rhs=b2rw[0:1, nh * 512:(nh + 1) * 512],
                                           start=False, stop=True)
                    ins.then_inc(s_w2c, 1)

        @block.scalar
        def _(s):
            p3gate(s)
            for ci in range(NCI):
                w = CW[ci]
                for f in range(NF_B):
                    it = ci * NF_B + f
                    s.wait_ge(s_ev4, it + 1)
                    s.activation(out=gel[:, 0:w], in_=g_sb[:, 0:w],
                                 func=AF.Gelu).then_inc(s_gel, 1)

        @block.sync
        def _(sy):
            p3gate(sy)
            for f in range(NF_B):
                if f >= 2:
                    sy.wait_ge(s_fin, f - 1)
                sy.dma_start(out=xrr[:, (f % 2) * D:(f % 2 + 1) * D],
                             in_=xf1scr[(f + 1) * P:(f + 2) * P, :]
                             ).then_inc(s_xrl[f % 2], 16)
                sy.wait_ge(s_fin, f + 1)
                sy.dma_start(out=yout[f * P:(f + 1) * P, :],
                             in_=youtb[:, (f % 2) * D:(f % 2 + 1) * D]
                             ).then_inc(s_yst[f % 2], 16)
            sy.wait_ge(s_yst[0], 16 * 8)
            sy.wait_ge(s_yst[1], 16 * 8)

        es4.close()
    es.close()
    return nc


def _l0ffn_host_prep(l, inputs):
    w1 = inputs["L0_ffn_w1"][l]
    b1 = inputs["L0_ffn_b1"][l]
    w2 = inputs["L0_ffn_w2"][l]
    HID = 2730
    w1p = np.zeros((D, W1W), np.float32)
    w1p[:, 0:HID] = w1[:, 0:HID]
    w1p[:, HID2:HID2 + HID] = w1[:, HID:]
    b1p = np.zeros((1, W1W), np.float32)
    b1p[0, 0:HID] = b1[0:HID]
    b1p[0, HID2:HID2 + HID] = b1[HID:]
    w2p = np.zeros((HID2, D), np.float32)
    w2p[0:HID] = w2
    lng = inputs["L0_ffn_ln_g"][l].reshape(8, P).T.copy()
    lnb = inputs["L0_ffn_ln_b"][l].reshape(8, P).T.copy()
    com = {
        "wqkv": np.ascontiguousarray(inputs["L0_ffn_wqkv"][l]),
        "wo_": np.ascontiguousarray(inputs["L0_ffn_wo"][l]),
        "lngc": np.ascontiguousarray(lng), "lnbc": np.ascontiguousarray(lnb),
        "qs": inputs["L0_ffn_q_scale"][l].reshape(1, 64).astype(np.float32),
        "ks": inputs["L0_ffn_k_scale"][l].reshape(1, 64).astype(np.float32),
        "w1p": w1p, "b1p": b1p, "w2p": w2p,
        "b2r": inputs["L0_ffn_b2"][l].reshape(1, D).astype(np.float32),
        "idn": np.eye(P, dtype=BF16NP),
        "mshift": np.eye(P, k=1, dtype=np.float32),
        "mlast": np.zeros((P, P), np.float32),
        "onesr": np.ones((1, P), BF16NP),
    }
    com["mlast"][127, 0] = 1.0
    return com


def _l0ffn_tabs(half):
    t = (half * TOWN - 640 + np.arange(2688)).astype(np.float64)
    hi = np.arange(0, 64, 2).astype(np.float64)
    inv = 1.0 / 10000.0 ** (hi / 64.0)
    fr = t[:, None] * inv[None, :]
    fr = np.concatenate([fr, fr], 1)
    base = (hi + 0.4 * 64) / (1.4 * 64)
    power = (t - 2048.0) / 256.0
    sc = base[None, :] ** power[:, None]
    sc = np.concatenate([sc, sc], 1)
    return {"csq": (np.cos(fr) * sc).astype(np.float32),
            "ssq": (np.sin(fr) * sc).astype(np.float32),
            "csk": (np.cos(fr) / sc).astype(np.float32),
            "ssk": (np.sin(fr) / sc).astype(np.float32)}


def _l0ffn_mask(half):
    m = np.full((NQ_B, 5 * P, P), -30000.0, np.float32)
    for mt in range(NQ_B):
        jj = np.arange(5 * P)[:, None]
        i = np.arange(P)[None, :]
        delta = 512 + i - jj
        ok = (delta >= 0) & (delta <= 512)
        if half == 0:
            ok &= (mt * P + jj) >= 640
        m[mt][ok] = 0.0
    if half == 0:
        m[0][:] = 0.0
    return m


def run_l0ffn(l, ystream, inputs):
    nc = build_l0ffn(apply_ln=(l == 0) or None)
    raise RuntimeError


def prep_l0ffn_maps(l, apply_ln, ystream, inputs):
    com = _l0ffn_host_prep(l, inputs)
    tabs = [_l0ffn_tabs(0), _l0ffn_tabs(1)]
    masks = [_l0ffn_mask(0), _l0ffn_mask(1)]
    maps = []
    for c in range(8):
        b, h = c // 2, c % 2
        yhv = np.zeros((2688, D), np.float32)
        lo = h * TOWN - 640
        src_lo = max(lo, 0)
        yhv[src_lo - lo:] = ystream[b, src_lo:h * TOWN + TOWN]
        mp = dict(com)
        mp["yh"] = yhv
        mp.update(tabs[h])
        mp["maskT"] = masks[h]
        maps.append(mp)
    return maps


def run_l0ffn_layer(l, ystream, inputs):
    nc = build_l0ffn(apply_ln=(l == 0))
    maps = prep_l0ffn_maps(l, l == 0, ystream, inputs)
    res = run_bass_kernel_spmd(nc, maps, core_ids=list(range(8)))
    out = np.empty((4, S, D), np.float32)
    for c in range(8):
        b, h = c // 2, c % 2
        out[b, h * TOWN:(h + 1) * TOWN] = res.results[c]["yout"]
    return out



NHC = 16


def build_l1attn():
    nc = bass.Bass("TRN2", target_bir_lowering=False)
    x1f = nc.dram_tensor("x1f", [S, D], F32, kind="ExternalInput").ap()
    fff = nc.dram_tensor("fff", [S, D], F32, kind="ExternalInput").ap()
    x1o_ = nc.dram_tensor("x1o_", [TOWN, D], F32, kind="ExternalInput").ap()
    ffo_ = nc.dram_tensor("ffo_", [TOWN, D], F32, kind="ExternalInput").ap()
    wq_d = nc.dram_tensor("wq", [D, D], F32, kind="ExternalInput").ap()
    wk_d = nc.dram_tensor("wk", [D, D], F32, kind="ExternalInput").ap()
    wv_d = nc.dram_tensor("wv", [D, D], F32, kind="ExternalInput").ap()
    wo_d = nc.dram_tensor("wo", [D, D], F32, kind="ExternalInput").ap()
    g1c_d = nc.dram_tensor("g1c", [P, 8], F32, kind="ExternalInput").ap()
    b1c_d = nc.dram_tensor("b1c", [P, 8], F32, kind="ExternalInput").ap()
    idn_d = nc.dram_tensor("idn", [P, P], BF16, kind="ExternalInput").ap()
    onesr_d = nc.dram_tensor("onesr", [1, 512], BF16, kind="ExternalInput").ap()
    x3o = nc.dram_tensor("x3o", [TOWN, D], F32, kind="ExternalOutput").ap()
    x2scr = nc.dram_tensor("x2scr", [TOWN, D], F32, kind="Internal").ap()
    vscrC = nc.dram_tensor("vscrC", [S, D], BF16, kind="Internal").ap()

    es = ExitStack()
    sbt = lambda nm, shp, dt=F32: es.enter_context(nc.sbuf_tensor(nm, shp, dt))
    sem = lambda nm: es.enter_context(nc.semaphore(nm))

    idn = sbt("idnC", [P, P], BF16)
    onesr = sbt("onesrC", [1, 512], BF16)
    onescl = sbt("onesclC", [P, 1], BF16)
    g1 = sbt("g1C", [P, 8])
    b1 = sbt("b1C", [P, 8])
    bcol = sbt("bcolC", [P, 8], BF16)
    cols = sbt("colsC", [P, 32])
    co = lambda i: cols[:, i:i + 1]
    (K_S1, K_S2, K_SB, K_MEAN, K_VAR, K_STD, K_RSTD, K_M1, K_R1, K_M2, K_R2,
     K_OS1, K_OS2, K_OM, K_OV, K_OSD, K_OR, K_MR, K_DEN) = range(19)
    brq = sbt("brqC", [1, D], BF16)
    brk = sbt("brkC", [1, D], BF16)
    brv = sbt("brvC", [1, D], BF16)
    aoT = sbt("aoTC", [P, 8 * TOWN], BF16)
    denr = sbt("denrC", [1, 512], BF16)

    esK = ExitStack()
    kT = esK.enter_context(nc.sbuf_tensor("kTC", [P, 8 * S], BF16))
    qT = esK.enter_context(nc.sbuf_tensor("qTC", [P, 8 * TOWN], BF16))

    pbig = es.enter_context(nc.psum_tensor("pbigC", [P, 2048], F32))
    pex = es.enter_context(nc.psum_tensor("pexC", [P, 512], F32))
    ptp = es.enter_context(nc.psum_tensor("ptpC", [P, D], BF16))
    pden = es.enter_context(nc.psum_tensor("pdenC", [1, 512], F32))

    s_w = sem("sC_w")
    s_ld = [sem("sC_ld0"), sem("sC_ld1")]
    s_fld = [sem("sC_f0"), sem("sC_f1")]
    s_var = sem("sC_var")
    s_std = sem("sC_std")
    s_rt = sem("sC_rt")
    s_app = sem("sC_app")
    s_tp = sem("sC_tp")
    s_tpc = sem("sC_tpc")
    s_mm = sem("sC_mm")
    s_kev = sem("sC_kev")
    s_wsc = sem("sC_wsc")
    s_bc = sem("sC_bc")
    s_bcc = sem("sC_bcc")
    s_x2s = [sem("sC_x2s0"), sem("sC_x2s1")]
    s_vstC = [sem("sC_vst0"), sem("sC_vst1")]
    rt_cnt = [0]

    def mmkev_v(ch, tt):
        return ch * 12 + 8 + tt + 1

    rsync = _mk_rsync(s_rt, rt_cnt)

    esA = ExitStack()
    sbA = lambda nm, shp, dt=F32: esA.enter_context(nc.sbuf_tensor(nm, shp, dt))
    wkb = sbA("wkbC", [P, 8 * D], BF16)
    wvb = sbA("wvbC", [P, 8 * D], BF16)
    wqb = wkb
    xbuf = sbA("xbufC", [P, 2 * D])
    fbuf = sbA("fbufC", [P, 2 * D])
    x2t = sbA("x2tC", [P, D])
    sqc = sbA("sqcC", [P, 512])
    a_bf = sbA("a_bfC", [P, 2 * D], BF16)
    aT = sbA("aTC", [P, 8 * 512], BF16)
    vbufC = sbA("vbufC", [P, 2 * D], BF16)

    pk = [pbig[:, 0:512], pbig[:, 512:1024]]
    pv = pbig[:, 1024:2048]

    NG = 48

    def aT_view(ch):
        return aT[:, :].rearrange("p (dc t) -> p dc t", dc=8)

    def mm_base(ch):
        return ch * 13 if ch < 8 else 104 + (ch - 8) * 9

    def mm_base(ch):
        return ch * 12 if ch <= 8 else 96 + (ch - 8) * 8

    with nc.Block() as block:
        @block.gpsimd
        def _(g):
            g.dma_start(out=idn[:], in_=idn_d).then_inc(s_w, 16)
            g.dma_start(out=onesr[:], in_=onesr_d).then_inc(s_w, 16)
            g.dma_start(out=g1[:], in_=g1c_d).then_inc(s_w, 16)
            g.dma_start(out=b1[:], in_=b1c_d).then_inc(s_w, 16)
            for wsrc, wdst in ((wk_d, wkb), (wv_d, wvb)):
                g.dma_start(out=wdst[:, :].rearrange("p (dc d) -> p dc d", dc=8),
                            in_=wsrc.rearrange("(dc p) d -> p dc d", p=P)
                            ).then_inc(s_w, 16)
            g.wait_ge(s_mm, 96)
            g.dma_start(out=wkb[:, :].rearrange("p (dc d) -> p dc d", dc=8),
                        in_=wq_d.rearrange("(dc p) d -> p dc d", p=P)
                        ).then_inc(s_w, 16)

        @block.vector
        def _(v):
            v.wait_ge(s_w, 96)
            v.memset(onescl[:], 1.0)
            v.tensor_copy(out=bcol[:], in_=b1[:]).then_inc(s_wsc, 1)
            for wi, (wdst, dst) in enumerate(((wkb, brk), (wvb, brv))):
                for nh in range(2):
                    v.wait_ge(s_bc, wi * 2 + nh + 1)
                    v.tensor_copy(out=dst[0:1, nh * 512:(nh + 1) * 512],
                                  in_=pden[0:1, 0:512]).then_inc(s_bcc, 1)
                for dc in range(8):
                    ins = v.tensor_scalar(
                        out=wdst[:, dc * D:(dc + 1) * D],
                        in0=wdst[:, dc * D:(dc + 1) * D],
                        scalar1=g1[:, dc:dc + 1], scalar2=1.0,
                        op0=ALU.mult, op1=ALU.mult)
                ins.then_inc(s_wsc, 1)

        @block.tensor
        def _(t):
            t.wait_ge(s_wsc, 1)
            for wi, wdst in enumerate((wkb, wvb)):
                for nh in range(2):
                    for dc in range(8):
                        ins = t.matmul(
                            out=pden[0:1, 0:512],
                            lhsT=bcol[:, dc:dc + 1],
                            rhs=wdst[:, dc * D + nh * 512:dc * D + (nh + 1) * 512],
                            start=(dc == 0), stop=(dc == 7))
                    ins.then_inc(s_bc, 1)
                    t.wait_ge(s_bcc, wi * 2 + nh + 1)

        @block.sync
        def _(sy):
            for g in range(NG):
                if g >= 2:
                    sy.wait_ge(s_app, g - 1)
                if g < 32:
                    sy.dma_start(out=xbuf[:, (g % 2) * D:(g % 2 + 1) * D],
                                 in_=x1f[g * P:(g + 1) * P, :]).then_inc(s_ld[g % 2], 16)
                    sy.dma_start(out=fbuf[:, (g % 2) * D:(g % 2 + 1) * D],
                                 in_=fff[g * P:(g + 1) * P, :]).then_inc(s_fld[g % 2], 16)
                if g % 4 == 3 and g // 4 < 8 and g >= 3:
                    ch = g // 4
                    for tt in range(4):
                        vi = ch * 4 + tt
                        sy.wait_ge(s_kev, mmkev_v(ch, tt))
                        sy.dma_start(out=vscrC[vi * P:(vi + 1) * P, :],
                                     in_=vbufC[:, (vi % 2) * D:(vi % 2 + 1) * D]
                                     ).then_inc(s_vstC[vi % 2], 16)
                if g >= 32:
                    go = g - 32
                    sy.dma_start(out=xbuf[:, (g % 2) * D:(g % 2 + 1) * D],
                                 in_=x1o_[go * P:(go + 1) * P, :]).then_inc(s_ld[g % 2], 16)
                    sy.dma_start(out=fbuf[:, (g % 2) * D:(g % 2 + 1) * D],
                                 in_=ffo_[go * P:(go + 1) * P, :]).then_inc(s_fld[g % 2], 16)
                    sy.wait_ge(s_app, g + 1)
                    sy.dma_start(out=x2scr[go * P:(go + 1) * P, :],
                                 in_=x2t[:]).then_inc(s_x2s[go % 2], 16)

        def _lnc(v, xin, c_mean, c_rstd, g, phase):
            v.tensor_mul(out=sqc[:], in0=xin[:, 0:512], in1=xin[:, 0:512])
            v.reduce_sum(out=co(K_S2), in_=sqc[:], axis=AX)
            v.tensor_mul(out=sqc[:], in0=xin[:, 512:1024], in1=xin[:, 512:1024])
            v.reduce_sum(out=co(K_SB), in_=sqc[:], axis=AX)
            v.reduce_sum(out=co(K_S1), in_=xin, axis=AX)
            rsync(v, v.tensor_add(out=co(K_S2), in0=co(K_S2), in1=co(K_SB)))
            rsync(v, v.tensor_scalar(out=co(c_mean), in0=co(K_S1),
                                     scalar1=1.0 / D, scalar2=None, op0=ALU.mult))
            rsync(v, v.tensor_mul(out=co(K_VAR), in0=co(c_mean), in1=co(c_mean)))
            rsync(v, v.scalar_tensor_tensor(out=co(K_VAR), in0=co(K_S2),
                                            scalar=1.0 / D, in1=co(K_VAR),
                                            op0=ALU.mult, op1=ALU.subtract))
            v.tensor_scalar(out=co(K_VAR), in0=co(K_VAR), scalar1=1e-5,
                            scalar2=None, op0=ALU.add).then_inc(s_var, 1)
            v.wait_ge(s_std, 3 * g + phase)
            rsync(v, v.reciprocal(out=co(c_rstd), in_=co(K_STD)))

        @block.vector
        def _(v):
            for g in range(NG):
                b = (g % 2) * D
                xt = xbuf[:, b:b + D]
                ft = fbuf[:, b:b + D]
                if g >= 1:
                    gp = g - 1
                    v.wait_ge(s_tp, gp + 1)
                    ch, tt = gp // 4, gp % 4
                    if tt == 0 and ch >= 1:
                        v.wait_ge(s_mm, mm_base(ch))
                    v.tensor_copy(
                        out=aT_view(ch)[:, :, tt * P:(tt + 1) * P],
                        in_=ptp[:, :].rearrange("p (dc t) -> p dc t", dc=8)
                    ).then_inc(s_tpc, 1)
                if g == 36:
                    for nh in range(2):
                        v.wait_ge(s_bc, 4 + nh + 1)
                        v.tensor_scalar(out=brq[0:1, nh * 512:(nh + 1) * 512],
                                        in0=pden[0:1, 0:512],
                                        scalar1=float(1.0 / 8.0), scalar2=None,
                                        op0=ALU.mult).then_inc(s_bcc, 1)
                    sc2 = float(1.0 / 8.0)
                    for dc in range(8):
                        ins = v.tensor_scalar(out=wkb[:, dc * D:(dc + 1) * D],
                                              in0=wkb[:, dc * D:(dc + 1) * D],
                                              scalar1=g1[:, dc:dc + 1],
                                              scalar2=sc2, op0=ALU.mult,
                                              op1=ALU.mult)
                    ins.then_inc(s_wsc, 1)
                v.wait_ge(s_ld[g % 2], 16 * (g // 2 + 1))
                v.wait_ge(s_fld[g % 2], 16 * (g // 2 + 1))
                if g >= 33:
                    go1 = g - 33
                    v.wait_ge(s_x2s[go1 % 2], 16 * (go1 // 2 + 1))
                x2s = x2t[:]
                _lnc(v, ft, K_MEAN, K_RSTD, g, 1)
                v.tensor_scalar(out=x2s, in0=ft, scalar1=co(K_MEAN),
                                scalar2=co(K_RSTD), op0=ALU.subtract, op1=ALU.mult)
                v.tensor_add(out=x2s, in0=x2s, in1=xt)
                _lnc(v, x2s, K_M1, K_R1, g, 2)
                v.tensor_scalar(out=xt, in0=x2s, scalar1=co(K_M1),
                                scalar2=co(K_R1), op0=ALU.subtract, op1=ALU.mult)
                _lnc(v, xt, K_M2, K_R2, g, 3)
                v.tensor_scalar(out=a_bf[:, b:b + D], in0=xt,
                                scalar1=co(K_M2), scalar2=co(K_R2),
                                op0=ALU.subtract, op1=ALU.mult).then_inc(s_app, 1)
            gp = NG - 1
            v.wait_ge(s_tp, gp + 1)
            ch, tt = gp // 4, gp % 4
            v.tensor_copy(out=aT_view(ch)[:, :, tt * P:(tt + 1) * P],
                          in_=ptp[:, :].rearrange("p (dc t) -> p dc t", dc=8)
                          ).then_inc(s_tpc, 1)

        @block.scalar
        def _(s):
            for g in range(NG):
                for ph in range(3):
                    s.wait_ge(s_var, 3 * g + ph + 1)
                    s.activation(out=co(K_STD), in_=co(K_VAR),
                                 func=AF.Sqrt).then_inc(s_std, 1)
                if g % 4 == 3:
                    ch = g // 4
                    for mh in range(8):
                        s.wait_ge(s_mm, mm_base(ch) + mh + 1)
                        src = pk[mh % 2][:, 0:512]
                        if ch < 8:
                            dst = kT[:, mh * S + ch * 512: mh * S + ch * 512 + 512]
                        else:
                            chq = ch - 8
                            dst = qT[:, mh * TOWN + chq * 512: mh * TOWN + chq * 512 + 512]
                        s.activation(out=dst, in_=src, func=AF.Copy).then_inc(s_kev, 1)
                    if ch < 8:
                        for tt in range(4):
                            s.wait_ge(s_mm, mm_base(ch) + 8 + tt + 1)
                            vi = ch * 4 + tt
                            if vi >= 2:
                                s.wait_ge(s_vstC[vi % 2], 16 * ((vi - 2) // 2 + 1))
                            s.activation(out=vbufC[:, (vi % 2) * D:(vi % 2 + 1) * D],
                                         in_=pv[:], func=AF.Copy).then_inc(s_kev, 1)

        @block.tensor
        def _(t):
            t.wait_ge(s_w, 96)
            t.wait_ge(s_bcc, 4)
            t.wait_ge(s_wsc, 3)
            for g in range(NG):
                t.wait_ge(s_app, g + 1)
                if g >= 2:
                    t.wait_ge(s_tpc, g - 1)
                b = (g % 2) * D
                for dc in range(8):
                    ins = t.transpose(out=ptp[:, dc * P:(dc + 1) * P],
                                      in_=a_bf[:, b + dc * P: b + (dc + 1) * P],
                                      identity=idn[:])
                ins.then_inc(s_tp, 1)
                if g % 4 == 3:
                    ch = g // 4
                    if ch == 8:
                        t.wait_ge(s_w, 112)
                        for nh in range(2):
                            for dc in range(8):
                                ins = t.matmul(
                                    out=pden[0:1, 0:512],
                                    lhsT=bcol[:, dc:dc + 1],
                                    rhs=wkb[:, dc * D + nh * 512:
                                            dc * D + (nh + 1) * 512],
                                    start=(dc == 0), stop=(dc == 7))
                            ins.then_inc(s_bc, 1)
                            t.wait_ge(s_bcc, 4 + nh + 1)
                        t.wait_ge(s_wsc, 4)
                    t.wait_ge(s_tpc, 4 * (ch + 1))
                    aTc = aT_view(ch)
                    if ch < 8:
                        kev_base = ch * 12
                        for mh in range(8):
                            ei = kev_base + mh
                            if ei >= 2:
                                t.wait_ge(s_kev, ei - 1)
                            for dc in range(8):
                                t.matmul(out=pk[mh % 2][:, 0:512],
                                         lhsT=wkb[:, dc * D + mh * P: dc * D + (mh + 1) * P],
                                         rhs=aTc[:, dc, :],
                                         start=(dc == 0), stop=False)
                            ins = t.matmul(out=pk[mh % 2][:, 0:512],
                                           lhsT=brk[0:1, mh * P:(mh + 1) * P],
                                           rhs=onesr[0:1, 0:512],
                                           start=False, stop=True)
                            ins.then_inc(s_mm, 1)
                        for tt in range(4):
                            vi = ch * 4 + tt
                            if vi >= 1:
                                t.wait_ge(s_kev, kev_base + 8 + tt)
                            for dc in range(8):
                                for nh in range(2):
                                    t.matmul(
                                        out=pv[:, nh * 512:(nh + 1) * 512],
                                        lhsT=aTc[:, dc, tt * P:(tt + 1) * P],
                                        rhs=wvb[:, dc * D + nh * 512: dc * D + (nh + 1) * 512],
                                        start=(dc == 0), stop=False)
                            for nh in range(2):
                                ins = t.matmul(out=pv[:, nh * 512:(nh + 1) * 512],
                                               lhsT=onesr[0:1, tt * P:(tt + 1) * P]
                                               if False else onesr[0:1, 0:P],
                                               rhs=brv[0:1, nh * 512:(nh + 1) * 512],
                                               start=False, stop=True)
                            ins.then_inc(s_mm, 1)
                    else:
                        for mh in range(8):
                            ei = 96 + (ch - 8) * 8 + mh
                            if ei >= 2:
                                t.wait_ge(s_kev, ei - 1)
                            for dc in range(8):
                                t.matmul(out=pk[mh % 2][:, 0:512],
                                         lhsT=wqb[:, dc * D + mh * P: dc * D + (mh + 1) * P],
                                         rhs=aTc[:, dc, :],
                                         start=(dc == 0), stop=False)
                            ins = t.matmul(out=pk[mh % 2][:, 0:512],
                                           lhsT=brq[0:1, mh * P:(mh + 1) * P],
                                           rhs=onesr[0:1, 0:512],
                                           start=False, stop=True)
                            ins.then_inc(s_mm, 1)

        esA.close()
        esB = ExitStack()
        sbB = lambda nm, shp, dt=F32: esB.enter_context(nc.sbuf_tensor(nm, shp, dt))
        vtk = esB.enter_context(nc.sbuf_tensor("vtkC", [P, 32 * 1040], BF16))
        slb = sbB("slbC", [P, 4 * 512], BF16)
        otb = sbB("otbC", [P, 512], BF16)
        s_vlC = sem("sC_vl")
        s_S = sem("sC_S")
        s_ex = sem("sC_ex")
        s_oc = sem("sC_oc")
        s_ot = sem("sC_ot")
        s_den = sem("sC_den")
        s_bca = sem("sC_bca")
        s_ao = sem("sC_ao")

        pS = [pex[:, 0:512], pbig[:, 0:512]]
        pOT = pbig[:, 512:1024]
        pbcast = pbig[:, 1024:1536]

        def gateB(e):
            e.wait_ge(s_kev, 128)
            e.wait_ge(s_tpc, NG)
            e.wait_ge(s_mm, 128)
            e.wait_ge(s_x2s[0], 16 * 8)
            e.wait_ge(s_x2s[1], 16 * 8)
            e.wait_ge(s_vstC[0], 16 * 16)
            e.wait_ge(s_vstC[1], 16 * 16)

        @block.gpsimd
        def _(g):
            gateB(g)
            g.memset(vtk[:], 1.0)
            for hh2 in range(16):
                g.dma_start(
                    out=vtk[:, :].rearrange("p (t h e) -> p t h e",
                                            t=32, h=16)[:, :, hh2, 0:64],
                    in_=vscrC.rearrange("(t p) (h d) -> p t h d",
                                        p=P, d=64)[:, :, hh2, :]
                    ).then_inc(s_vlC, 16)

        @block.tensor
        def _(t):
            gateB(t)
            t.wait_ge(s_vlC, 256)
            for h in range(NHC):
                hb = (h % 2) * 64
                hc = h // 2
                for qc in range(4):
                    ii = h * 4 + qc
                    for tt2 in range(34):
                        if tt2 < 32:
                            jc = tt2
                            gidx = ii * 32 + jc
                            if gidx >= 2:
                                t.wait_ge(s_ex, gidx - 1)
                            t.matmul(out=pS[jc % 2],
                                     lhsT=kT[hb:hb + 64, hc * S + jc * P: hc * S + (jc + 1) * P],
                                     rhs=qT[hb:hb + 64, hc * TOWN + qc * 512: hc * TOWN + (qc + 1) * 512],
                                     start=True, stop=True).then_inc(s_S, 1)
                        if tt2 == 2 and ii >= 1:
                            t.wait_ge(s_ao, ii)
                        if tt2 >= 2:
                            jc = tt2 - 2
                            gidx = ii * 32 + jc
                            t.wait_ge(s_ex, gidx + 1)
                            t.matmul(out=pOT[0:65, :],
                                     lhsT=vtk[:, jc * 1040 + h * 65:
                                              jc * 1040 + h * 65 + 65],
                                     rhs=slb[:, (jc % 4) * 512:(jc % 4 + 1) * 512],
                                     start=(jc == 0), stop=(jc == 31)
                                     ).then_inc(s_oc, 1)
                    t.wait_ge(s_den, ii + 1)
                    t.matmul(out=pbcast[0:64, :], lhsT=onesr[0:1, 0:64],
                             rhs=denr[0:1, :], start=True,
                             stop=True).then_inc(s_bca, 1)

        @block.scalar
        def _(s):
            gateB(s)
            for h in range(NHC):
                for qc in range(4):
                    ii = h * 4 + qc
                    for jc in range(32):
                        gidx = ii * 32 + jc
                        s.wait_ge(s_S, gidx + 1)
                        if gidx >= 4:
                            s.wait_ge(s_oc, gidx - 3)
                        s.activation(out=slb[:, (jc % 4) * 512:(jc % 4 + 1) * 512],
                                     in_=pS[jc % 2], func=AF.Exp).then_inc(s_ex, 1)
                    s.wait_ge(s_oc, (ii + 1) * 32)
                    s.activation(out=otb[0:64, :], in_=pOT[0:64, :],
                                 func=AF.Copy).then_inc(s_ot, 1)

        @block.vector
        def _(v):
            gateB(v)
            for h in range(NHC):
                hb = (h % 2) * 64
                hc = h // 2
                for qc in range(4):
                    ii = h * 4 + qc
                    v.wait_ge(s_oc, (ii + 1) * 32)
                    rsync(v, v.tensor_copy(out=denr[0:1, :],
                                           in_=pOT[64:65, :]))
                    with nc.allow_low_precision(reason="bf16 softmax denom recip feeding bf16 multiply"):
                        rsync(v, v.reciprocal(out=denr[0:1, :], in_=denr[0:1, :]))
                    v.nop().then_inc(s_den, 1)
                    v.wait_ge(s_bca, ii + 1)
                    v.wait_ge(s_ot, ii + 1)
                    v.tensor_tensor(
                        out=aoT[hb:hb + 64, hc * TOWN + qc * 512: hc * TOWN + (qc + 1) * 512],
                        in0=otb[0:64, :], in1=pbcast[0:64, :],
                        op=ALU.mult).then_inc(s_ao, 1)

        esB.close()
        esK.close()
        esC = ExitStack()
        sbC = lambda nm, shp, dt=F32: esC.enter_context(nc.sbuf_tensor(nm, shp, dt))
        wob = sbC("wobC", [P, 8 * D], BF16)
        x2r = sbC("x2rC", [P, 2 * D])
        outc = sbC("outcC", [P, 2 * D])
        sqd = sbC("sqdC", [P, D])
        s_wo = sem("sC_wo")
        s_wol = sem("sC_wol")
        s_ov = sem("sC_ov")
        s_osd = sem("sC_osd")
        s_oev = sem("sC_oev")
        s_x2l = [sem("sC_x2l0"), sem("sC_x2l1")]
        s_st = [sem("sC_st0"), sem("sC_st1")]
        pwoC = pbig[:, 0:1024]

        @block.gpsimd
        def _(g):
            g.wait_ge(s_ao, 64)
            g.dma_start(out=wob[:, :].rearrange("p (dc d) -> p dc d", dc=8),
                        in_=wo_d.rearrange("(dc p) d -> p dc d", p=P)
                        ).then_inc(s_wol, 16)

        @block.sync
        def _(sy):
            sy.wait_ge(s_ao, 64)
            for o in range(16):
                if o >= 2:
                    sy.wait_ge(s_oev, o - 1)
                sy.dma_start(out=x2r[:, (o % 2) * D:(o % 2 + 1) * D],
                             in_=x2scr[o * P:(o + 1) * P, :]).then_inc(s_x2l[o % 2], 16)
                sy.wait_ge(s_oev, o + 1)
                sy.dma_start(out=x3o[o * P:(o + 1) * P, :],
                             in_=outc[:, (o % 2) * D:(o % 2 + 1) * D]
                             ).then_inc(s_st[o % 2], 16)
            sy.wait_ge(s_st[0], 16 * 8)
            sy.wait_ge(s_st[1], 16 * 8)

        @block.tensor
        def _(t):
            t.wait_ge(s_wol, 16)
            for o in range(16):
                if o >= 1:
                    t.wait_ge(s_ov, 2 * o - 1)
                for cc in range(8):
                    for nh in range(2):
                        ins = t.matmul(
                            out=pwoC[:, nh * 512:(nh + 1) * 512],
                            lhsT=aoT[:, cc * TOWN + o * P: cc * TOWN + (o + 1) * P],
                            rhs=wob[:, cc * D + nh * 512: cc * D + (nh + 1) * 512],
                            start=(cc == 0), stop=(cc == 7))
                ins.then_inc(s_wo, 1)

        @block.vector
        def _(v):
            for o in range(16):
                ob = outc[:, (o % 2) * D:(o % 2 + 1) * D]
                v.wait_ge(s_wo, o + 1)
                if o >= 2:
                    v.wait_ge(s_st[o % 2], 16 * (o // 2))
                v.tensor_copy(out=ob, in_=pwoC[:]).then_inc(s_ov, 1)
                v.tensor_mul(out=sqd[:], in0=ob, in1=ob)
                v.reduce_sum(out=co(K_OS1), in_=ob, axis=AX)
                v.reduce_sum(out=co(K_OS2), in_=sqd[:], axis=AX)
                rsync(v, v.tensor_scalar(out=co(K_OM), in0=co(K_OS1),
                                         scalar1=1.0 / D, scalar2=None, op0=ALU.mult))
                rsync(v, v.tensor_mul(out=co(K_OV), in0=co(K_OM), in1=co(K_OM)))
                rsync(v, v.scalar_tensor_tensor(out=co(K_OV), in0=co(K_OS2),
                                                scalar=1.0 / D, in1=co(K_OV),
                                                op0=ALU.mult, op1=ALU.subtract))
                v.tensor_scalar(out=co(K_OV), in0=co(K_OV), scalar1=1e-5,
                                scalar2=None, op0=ALU.add).then_inc(s_ov, 1)
                v.wait_ge(s_osd, o + 1)
                rsync(v, v.reciprocal(out=co(K_OR), in_=co(K_OSD)))
                v.wait_ge(s_x2l[o % 2], 16 * (o // 2 + 1))
                v.tensor_scalar(out=ob, in0=ob, scalar1=co(K_OM),
                                scalar2=co(K_OR), op0=ALU.subtract, op1=ALU.mult)
                v.tensor_add(out=ob, in0=ob,
                             in1=x2r[:, (o % 2) * D:(o % 2 + 1) * D]
                             ).then_inc(s_oev, 1)

        @block.scalar
        def _(s):
            for o in range(16):
                s.wait_ge(s_ov, 2 * o + 2)
                s.activation(out=co(K_OSD), in_=co(K_OV),
                             func=AF.Sqrt).then_inc(s_osd, 1)

        esC.close()
    es.close()
    return nc


def prep_l1attn_maps(x1, ffout, inputs):
    com = {
        "wq": np.ascontiguousarray(inputs["L1_attn_Wq"]),
        "wk": np.ascontiguousarray(inputs["L1_attn_Wk"]),
        "wv": np.ascontiguousarray(inputs["L1_attn_Wv"]),
        "wo": np.ascontiguousarray(inputs["L1_attn_Wo"]),
        "g1c": inputs["L1_attn_ln_gamma"].reshape(8, P).T.copy(),
        "b1c": inputs["L1_attn_ln_beta"].reshape(8, P).T.copy(),
        "idn": np.eye(P, dtype=BF16NP),
        "onesr": np.ones((1, 512), BF16NP),
    }
    maps = []
    for c in range(8):
        b, h = c // 2, c % 2
        mp = dict(com)
        mp["x1f"] = x1[b]
        mp["fff"] = ffout[b]
        mp["x1o_"] = np.ascontiguousarray(x1[b, h * TOWN:(h + 1) * TOWN])
        mp["ffo_"] = np.ascontiguousarray(ffout[b, h * TOWN:(h + 1) * TOWN])
        maps.append(mp)
    return maps


def run_l1attn(x1, ffout, inputs):
    nc = build_l1attn()
    maps = prep_l1attn_maps(x1, ffout, inputs)
    res = run_bass_kernel_spmd(nc, maps, core_ids=list(range(8)))
    out = np.empty((4, S, D), np.float32)
    for c in range(8):
        b, h = c // 2, c % 2
        out[b, h * TOWN:(h + 1) * TOWN] = res.results[c]["x3o"]
    return out


def build_l1ffn_attn():
    nc = bass.Bass("TRN2", target_bir_lowering=False)
    x1f = nc.dram_tensor("x1f", [S, D], F32, kind="ExternalInput").ap()
    x1o_ = nc.dram_tensor("x1o_", [TOWN, D], F32, kind="ExternalInput").ap()
    bq_d = nc.dram_tensor("bq_r", [1, D], F32, kind="ExternalInput").ap()
    bk_d = nc.dram_tensor("bk_r", [1, D], F32, kind="ExternalInput").ap()
    bv_d = nc.dram_tensor("bv_r", [1, D], F32, kind="ExternalInput").ap()
    bo_d = nc.dram_tensor("bo_r", [1, D], F32, kind="ExternalInput").ap()
    wq_d = nc.dram_tensor("wq", [D, D], F32, kind="ExternalInput").ap()
    wk_d = nc.dram_tensor("wk", [D, D], F32, kind="ExternalInput").ap()
    wv_d = nc.dram_tensor("wv", [D, D], F32, kind="ExternalInput").ap()
    wo_d = nc.dram_tensor("wo", [D, D], F32, kind="ExternalInput").ap()
    g1c_d = nc.dram_tensor("g1c", [P, 8], F32, kind="ExternalInput").ap()
    b1c_d = nc.dram_tensor("b1c", [P, 8], F32, kind="ExternalInput").ap()
    idn_d = nc.dram_tensor("idn", [P, P], BF16, kind="ExternalInput").ap()
    onesr_d = nc.dram_tensor("onesr", [1, 512], BF16, kind="ExternalInput").ap()
    x3o = nc.dram_tensor("x3o", [TOWN, D], F32, kind="ExternalOutput").ap()
    x2scr = nc.dram_tensor("x2scr", [TOWN, D], F32, kind="Internal").ap()
    vscrC = nc.dram_tensor("vscrC", [S, D], BF16, kind="Internal").ap()

    es = ExitStack()
    sbt = lambda nm, shp, dt=F32: es.enter_context(nc.sbuf_tensor(nm, shp, dt))
    sem = lambda nm: es.enter_context(nc.semaphore(nm))

    idn = sbt("idnC", [P, P], BF16)
    onesr = sbt("onesrC", [1, 512], BF16)
    onescl = sbt("onesclC", [P, 1], BF16)
    g1 = sbt("g1C", [P, 8])
    b1 = sbt("b1C", [P, 8])
    bcol = sbt("bcolC", [P, 8], BF16)
    cols = sbt("colsC", [P, 32])
    co = lambda i: cols[:, i:i + 1]
    (K_S1, K_S2, K_SB, K_MEAN, K_VAR, K_STD, K_RSTD, K_M1, K_R1, K_M2, K_R2,
     K_OS1, K_OS2, K_OM, K_OV, K_OSD, K_OR, K_MR, K_DEN) = range(19)
    brq = sbt("brqC", [1, D], BF16)
    brk = sbt("brkC", [1, D], BF16)
    brv = sbt("brvC", [1, D], BF16)
    aoT = sbt("aoTC", [P, 8 * TOWN], BF16)
    denr = sbt("denrC", [1, 512], BF16)

    esK = ExitStack()
    kT = esK.enter_context(nc.sbuf_tensor("kTC", [P, 8 * S], BF16))
    qT = esK.enter_context(nc.sbuf_tensor("qTC", [P, 8 * TOWN], BF16))

    pbig = es.enter_context(nc.psum_tensor("pbigC", [P, 2048], F32))
    pex = es.enter_context(nc.psum_tensor("pexC", [P, 512], F32))
    ptp = es.enter_context(nc.psum_tensor("ptpC", [P, D], BF16))
    pden = es.enter_context(nc.psum_tensor("pdenC", [1, 512], F32))

    s_w = sem("sC_w")
    s_bb = sem("sC_bb")
    s_ld = [sem("sC_ld0"), sem("sC_ld1")]
    s_fld = [sem("sC_f0"), sem("sC_f1")]
    s_var = sem("sC_var")
    s_std = sem("sC_std")
    s_rt = sem("sC_rt")
    s_app = sem("sC_app")
    s_tp = sem("sC_tp")
    s_tpc = sem("sC_tpc")
    s_mm = sem("sC_mm")
    s_kev = sem("sC_kev")
    s_wsc = sem("sC_wsc")
    s_bc = sem("sC_bc")
    s_bcc = sem("sC_bcc")
    s_x2s = [sem("sC_x2s0"), sem("sC_x2s1")]
    s_vstC = [sem("sC_vst0"), sem("sC_vst1")]
    rt_cnt = [0]

    def mmkev_v(ch, tt):
        return ch * 12 + 8 + tt + 1

    rsync = _mk_rsync(s_rt, rt_cnt)

    esA = ExitStack()
    sbA = lambda nm, shp, dt=F32: esA.enter_context(nc.sbuf_tensor(nm, shp, dt))
    wkb = sbA("wkbC", [P, 8 * D], BF16)
    wvb = sbA("wvbC", [P, 8 * D], BF16)
    wqb = wkb
    xbuf = sbA("xbufC", [P, 2 * D])
    trow = sbA("trowD", [1, 512])
    bbq = sbA("bbqD", [1, D], BF16)
    bbk = sbA("bbkD", [1, D], BF16)
    bbv = sbA("bbvD", [1, D], BF16)
    x2t = sbA("x2tC", [P, D])
    sqc = sbA("sqcC", [P, 512])
    a_bf = sbA("a_bfC", [P, 2 * D], BF16)
    aT = sbA("aTC", [P, 8 * 512], BF16)
    vbufC = sbA("vbufC", [P, 2 * D], BF16)

    pk = [pbig[:, 0:512], pbig[:, 512:1024]]
    pv = pbig[:, 1024:2048]

    NG = 48

    def aT_view(ch):
        return aT[:, :].rearrange("p (dc t) -> p dc t", dc=8)

    def mm_base(ch):
        return ch * 13 if ch < 8 else 104 + (ch - 8) * 9

    def mm_base(ch):
        return ch * 12 if ch <= 8 else 96 + (ch - 8) * 8

    with nc.Block() as block:
        @block.gpsimd
        def _(g):
            g.dma_start(out=bbq[:], in_=bq_d).then_inc(s_bb, 16)
            g.dma_start(out=bbk[:], in_=bk_d).then_inc(s_bb, 16)
            g.dma_start(out=bbv[:], in_=bv_d).then_inc(s_bb, 16)
            g.dma_start(out=idn[:], in_=idn_d).then_inc(s_w, 16)
            g.dma_start(out=onesr[:], in_=onesr_d).then_inc(s_w, 16)
            g.dma_start(out=g1[:], in_=g1c_d).then_inc(s_w, 16)
            g.dma_start(out=b1[:], in_=b1c_d).then_inc(s_w, 16)
            for wsrc, wdst in ((wk_d, wkb), (wv_d, wvb)):
                g.dma_start(out=wdst[:, :].rearrange("p (dc d) -> p dc d", dc=8),
                            in_=wsrc.rearrange("(dc p) d -> p dc d", p=P)
                            ).then_inc(s_w, 16)
            g.wait_ge(s_mm, 96)
            g.dma_start(out=wkb[:, :].rearrange("p (dc d) -> p dc d", dc=8),
                        in_=wq_d.rearrange("(dc p) d -> p dc d", p=P)
                        ).then_inc(s_w, 16)

        @block.vector
        def _(v):
            v.wait_ge(s_w, 96)
            v.wait_ge(s_bb, 48)
            v.memset(onescl[:], 1.0)
            v.tensor_copy(out=bcol[:], in_=b1[:]).then_inc(s_wsc, 1)
            for wi, (wdst, dst, brow) in enumerate(((wkb, brk, bbk),
                                                    (wvb, brv, bbv))):
                for nh in range(2):
                    v.wait_ge(s_bc, wi * 2 + nh + 1)
                    with nc.allow_low_precision(reason="bf16 attn bias rows"):
                        v.tensor_tensor(out=dst[0:1, nh * 512:(nh + 1) * 512],
                                        in0=pden[0:1, 0:512],
                                        in1=brow[0:1, nh * 512:(nh + 1) * 512],
                                        op=ALU.add).then_inc(s_bcc, 1)
                for dc in range(8):
                    ins = v.tensor_scalar(
                        out=wdst[:, dc * D:(dc + 1) * D],
                        in0=wdst[:, dc * D:(dc + 1) * D],
                        scalar1=g1[:, dc:dc + 1], scalar2=1.0,
                        op0=ALU.mult, op1=ALU.mult)
                ins.then_inc(s_wsc, 1)

        @block.tensor
        def _(t):
            t.wait_ge(s_wsc, 1)
            for wi, wdst in enumerate((wkb, wvb)):
                for nh in range(2):
                    for dc in range(8):
                        ins = t.matmul(
                            out=pden[0:1, 0:512],
                            lhsT=bcol[:, dc:dc + 1],
                            rhs=wdst[:, dc * D + nh * 512:dc * D + (nh + 1) * 512],
                            start=(dc == 0), stop=(dc == 7))
                    ins.then_inc(s_bc, 1)
                    t.wait_ge(s_bcc, wi * 2 + nh + 1)

        @block.sync
        def _(sy):
            for g in range(NG):
                if g >= 2:
                    sy.wait_ge(s_app, g - 1)
                if g < 32:
                    sy.dma_start(out=xbuf[:, (g % 2) * D:(g % 2 + 1) * D],
                                 in_=x1f[g * P:(g + 1) * P, :]).then_inc(s_ld[g % 2], 16)
                if g % 4 == 3 and g // 4 < 8 and g >= 3:
                    ch = g // 4
                    for tt in range(4):
                        vi = ch * 4 + tt
                        sy.wait_ge(s_kev, mmkev_v(ch, tt))
                        sy.dma_start(out=vscrC[vi * P:(vi + 1) * P, :],
                                     in_=vbufC[:, (vi % 2) * D:(vi % 2 + 1) * D]
                                     ).then_inc(s_vstC[vi % 2], 16)
                if g >= 32:
                    go = g - 32
                    sy.dma_start(out=xbuf[:, (g % 2) * D:(g % 2 + 1) * D],
                                 in_=x1o_[go * P:(go + 1) * P, :]).then_inc(s_ld[g % 2], 16)
                    sy.wait_ge(s_app, g + 1)
                    sy.dma_start(out=x2scr[go * P:(go + 1) * P, :],
                                 in_=x2t[:]).then_inc(s_x2s[go % 2], 16)

        def _lnc(v, xin, c_mean, c_rstd, g, phase):
            v.tensor_mul(out=sqc[:], in0=xin[:, 0:512], in1=xin[:, 0:512])
            v.reduce_sum(out=co(K_S2), in_=sqc[:], axis=AX)
            v.tensor_mul(out=sqc[:], in0=xin[:, 512:1024], in1=xin[:, 512:1024])
            v.reduce_sum(out=co(K_SB), in_=sqc[:], axis=AX)
            v.reduce_sum(out=co(K_S1), in_=xin, axis=AX)
            rsync(v, v.tensor_add(out=co(K_S2), in0=co(K_S2), in1=co(K_SB)))
            rsync(v, v.tensor_scalar(out=co(c_mean), in0=co(K_S1),
                                     scalar1=1.0 / D, scalar2=None, op0=ALU.mult))
            rsync(v, v.tensor_mul(out=co(K_VAR), in0=co(c_mean), in1=co(c_mean)))
            rsync(v, v.scalar_tensor_tensor(out=co(K_VAR), in0=co(K_S2),
                                            scalar=1.0 / D, in1=co(K_VAR),
                                            op0=ALU.mult, op1=ALU.subtract))
            v.tensor_scalar(out=co(K_VAR), in0=co(K_VAR), scalar1=1e-5,
                            scalar2=None, op0=ALU.add).then_inc(s_var, 1)
            v.wait_ge(s_std, 2 * g + phase)
            rsync(v, v.reciprocal(out=co(c_rstd), in_=co(K_STD)))

        @block.vector
        def _(v):
            for g in range(NG):
                b = (g % 2) * D
                xt = xbuf[:, b:b + D]
                if g >= 1:
                    gp = g - 1
                    v.wait_ge(s_tp, gp + 1)
                    ch, tt = gp // 4, gp % 4
                    if tt == 0 and ch >= 1:
                        v.wait_ge(s_mm, mm_base(ch))
                    v.tensor_copy(
                        out=aT_view(ch)[:, :, tt * P:(tt + 1) * P],
                        in_=ptp[:, :].rearrange("p (dc t) -> p dc t", dc=8)
                    ).then_inc(s_tpc, 1)
                if g == 36:
                    for nh in range(2):
                        v.wait_ge(s_bc, 4 + nh + 1)
                        rsync(v, v.tensor_tensor(out=trow[0:1, :],
                                                 in0=pden[0:1, 0:512],
                                                 in1=bbq[0:1, nh * 512:(nh + 1) * 512],
                                                 op=ALU.add))
                        v.tensor_scalar(out=brq[0:1, nh * 512:(nh + 1) * 512],
                                        in0=trow[0:1, :],
                                        scalar1=float(1.0 / 8.0), scalar2=None,
                                        op0=ALU.mult).then_inc(s_bcc, 1)
                    sc2 = float(1.0 / 8.0)
                    for dc in range(8):
                        ins = v.tensor_scalar(out=wkb[:, dc * D:(dc + 1) * D],
                                              in0=wkb[:, dc * D:(dc + 1) * D],
                                              scalar1=g1[:, dc:dc + 1],
                                              scalar2=sc2, op0=ALU.mult,
                                              op1=ALU.mult)
                    ins.then_inc(s_wsc, 1)
                v.wait_ge(s_ld[g % 2], 16 * (g // 2 + 1))
                if g >= 33:
                    go1 = g - 33
                    v.wait_ge(s_x2s[go1 % 2], 16 * (go1 // 2 + 1))
                x2s = x2t[:]
                _lnc(v, xt, K_MEAN, K_RSTD, g, 1)
                v.tensor_scalar(out=x2s, in0=xt, scalar1=co(K_MEAN),
                                scalar2=co(K_RSTD), op0=ALU.subtract, op1=ALU.mult)
                _lnc(v, x2s, K_M2, K_R2, g, 2)
                v.tensor_scalar(out=a_bf[:, b:b + D], in0=x2s,
                                scalar1=co(K_M2), scalar2=co(K_R2),
                                op0=ALU.subtract, op1=ALU.mult).then_inc(s_app, 1)
            gp = NG - 1
            v.wait_ge(s_tp, gp + 1)
            ch, tt = gp // 4, gp % 4
            v.tensor_copy(out=aT_view(ch)[:, :, tt * P:(tt + 1) * P],
                          in_=ptp[:, :].rearrange("p (dc t) -> p dc t", dc=8)
                          ).then_inc(s_tpc, 1)

        @block.scalar
        def _(s):
            for g in range(NG):
                for ph in range(2):
                    s.wait_ge(s_var, 2 * g + ph + 1)
                    s.activation(out=co(K_STD), in_=co(K_VAR),
                                 func=AF.Sqrt).then_inc(s_std, 1)
                if g % 4 == 3:
                    ch = g // 4
                    for mh in range(8):
                        s.wait_ge(s_mm, mm_base(ch) + mh + 1)
                        src = pk[mh % 2][:, 0:512]
                        if ch < 8:
                            dst = kT[:, mh * S + ch * 512: mh * S + ch * 512 + 512]
                        else:
                            chq = ch - 8
                            dst = qT[:, mh * TOWN + chq * 512: mh * TOWN + chq * 512 + 512]
                        s.activation(out=dst, in_=src, func=AF.Copy).then_inc(s_kev, 1)
                    if ch < 8:
                        for tt in range(4):
                            s.wait_ge(s_mm, mm_base(ch) + 8 + tt + 1)
                            vi = ch * 4 + tt
                            if vi >= 2:
                                s.wait_ge(s_vstC[vi % 2], 16 * ((vi - 2) // 2 + 1))
                            s.activation(out=vbufC[:, (vi % 2) * D:(vi % 2 + 1) * D],
                                         in_=pv[:], func=AF.Copy).then_inc(s_kev, 1)

        @block.tensor
        def _(t):
            t.wait_ge(s_w, 96)
            t.wait_ge(s_bcc, 4)
            t.wait_ge(s_wsc, 3)
            for g in range(NG):
                t.wait_ge(s_app, g + 1)
                if g >= 2:
                    t.wait_ge(s_tpc, g - 1)
                b = (g % 2) * D
                for dc in range(8):
                    ins = t.transpose(out=ptp[:, dc * P:(dc + 1) * P],
                                      in_=a_bf[:, b + dc * P: b + (dc + 1) * P],
                                      identity=idn[:])
                ins.then_inc(s_tp, 1)
                if g % 4 == 3:
                    ch = g // 4
                    if ch == 8:
                        t.wait_ge(s_w, 112)
                        for nh in range(2):
                            for dc in range(8):
                                ins = t.matmul(
                                    out=pden[0:1, 0:512],
                                    lhsT=bcol[:, dc:dc + 1],
                                    rhs=wkb[:, dc * D + nh * 512:
                                            dc * D + (nh + 1) * 512],
                                    start=(dc == 0), stop=(dc == 7))
                            ins.then_inc(s_bc, 1)
                            t.wait_ge(s_bcc, 4 + nh + 1)
                        t.wait_ge(s_wsc, 4)
                    t.wait_ge(s_tpc, 4 * (ch + 1))
                    aTc = aT_view(ch)
                    if ch < 8:
                        kev_base = ch * 12
                        for mh in range(8):
                            ei = kev_base + mh
                            if ei >= 2:
                                t.wait_ge(s_kev, ei - 1)
                            for dc in range(8):
                                t.matmul(out=pk[mh % 2][:, 0:512],
                                         lhsT=wkb[:, dc * D + mh * P: dc * D + (mh + 1) * P],
                                         rhs=aTc[:, dc, :],
                                         start=(dc == 0), stop=False)
                            ins = t.matmul(out=pk[mh % 2][:, 0:512],
                                           lhsT=brk[0:1, mh * P:(mh + 1) * P],
                                           rhs=onesr[0:1, 0:512],
                                           start=False, stop=True)
                            ins.then_inc(s_mm, 1)
                        for tt in range(4):
                            vi = ch * 4 + tt
                            if vi >= 1:
                                t.wait_ge(s_kev, kev_base + 8 + tt)
                            for dc in range(8):
                                for nh in range(2):
                                    t.matmul(
                                        out=pv[:, nh * 512:(nh + 1) * 512],
                                        lhsT=aTc[:, dc, tt * P:(tt + 1) * P],
                                        rhs=wvb[:, dc * D + nh * 512: dc * D + (nh + 1) * 512],
                                        start=(dc == 0), stop=False)
                            for nh in range(2):
                                ins = t.matmul(out=pv[:, nh * 512:(nh + 1) * 512],
                                               lhsT=onesr[0:1, tt * P:(tt + 1) * P]
                                               if False else onesr[0:1, 0:P],
                                               rhs=brv[0:1, nh * 512:(nh + 1) * 512],
                                               start=False, stop=True)
                            ins.then_inc(s_mm, 1)
                    else:
                        for mh in range(8):
                            ei = 96 + (ch - 8) * 8 + mh
                            if ei >= 2:
                                t.wait_ge(s_kev, ei - 1)
                            for dc in range(8):
                                t.matmul(out=pk[mh % 2][:, 0:512],
                                         lhsT=wqb[:, dc * D + mh * P: dc * D + (mh + 1) * P],
                                         rhs=aTc[:, dc, :],
                                         start=(dc == 0), stop=False)
                            ins = t.matmul(out=pk[mh % 2][:, 0:512],
                                           lhsT=brq[0:1, mh * P:(mh + 1) * P],
                                           rhs=onesr[0:1, 0:512],
                                           start=False, stop=True)
                            ins.then_inc(s_mm, 1)

        esA.close()
        esB = ExitStack()
        sbB = lambda nm, shp, dt=F32: esB.enter_context(nc.sbuf_tensor(nm, shp, dt))
        vtk = esB.enter_context(nc.sbuf_tensor("vtkC", [P, 32 * 1040], BF16))
        slb = sbB("slbC", [P, 4 * 512], BF16)
        otb = sbB("otbC", [P, 512], BF16)
        s_vlC = sem("sC_vl")
        s_S = sem("sC_S")
        s_ex = sem("sC_ex")
        s_oc = sem("sC_oc")
        s_ot = sem("sC_ot")
        s_den = sem("sC_den")
        s_bca = sem("sC_bca")
        s_ao = sem("sC_ao")

        pS = [pex[:, 0:512], pbig[:, 0:512]]
        pOT = pbig[:, 512:1024]
        pbcast = pbig[:, 1024:1536]

        def gateB(e):
            e.wait_ge(s_kev, 128)
            e.wait_ge(s_tpc, NG)
            e.wait_ge(s_mm, 128)
            e.wait_ge(s_x2s[0], 16 * 8)
            e.wait_ge(s_x2s[1], 16 * 8)
            e.wait_ge(s_vstC[0], 16 * 16)
            e.wait_ge(s_vstC[1], 16 * 16)

        @block.gpsimd
        def _(g):
            gateB(g)
            g.memset(vtk[:], 1.0)
            for hh2 in range(16):
                g.dma_start(
                    out=vtk[:, :].rearrange("p (t h e) -> p t h e",
                                            t=32, h=16)[:, :, hh2, 0:64],
                    in_=vscrC.rearrange("(t p) (h d) -> p t h d",
                                        p=P, d=64)[:, :, hh2, :]
                    ).then_inc(s_vlC, 16)

        @block.tensor
        def _(t):
            gateB(t)
            t.wait_ge(s_vlC, 256)
            for h in range(NHC):
                hb = (h % 2) * 64
                hc = h // 2
                for qc in range(4):
                    ii = h * 4 + qc
                    for tt2 in range(34):
                        if tt2 < 32:
                            jc = tt2
                            gidx = ii * 32 + jc
                            if gidx >= 2:
                                t.wait_ge(s_ex, gidx - 1)
                            t.matmul(out=pS[jc % 2],
                                     lhsT=kT[hb:hb + 64, hc * S + jc * P: hc * S + (jc + 1) * P],
                                     rhs=qT[hb:hb + 64, hc * TOWN + qc * 512: hc * TOWN + (qc + 1) * 512],
                                     start=True, stop=True).then_inc(s_S, 1)
                        if tt2 == 2 and ii >= 1:
                            t.wait_ge(s_ao, ii)
                        if tt2 >= 2:
                            jc = tt2 - 2
                            gidx = ii * 32 + jc
                            t.wait_ge(s_ex, gidx + 1)
                            t.matmul(out=pOT[0:65, :],
                                     lhsT=vtk[:, jc * 1040 + h * 65:
                                              jc * 1040 + h * 65 + 65],
                                     rhs=slb[:, (jc % 4) * 512:(jc % 4 + 1) * 512],
                                     start=(jc == 0), stop=(jc == 31)
                                     ).then_inc(s_oc, 1)
                    t.wait_ge(s_den, ii + 1)
                    t.matmul(out=pbcast[0:64, :], lhsT=onesr[0:1, 0:64],
                             rhs=denr[0:1, :], start=True,
                             stop=True).then_inc(s_bca, 1)

        @block.scalar
        def _(s):
            gateB(s)
            for h in range(NHC):
                for qc in range(4):
                    ii = h * 4 + qc
                    for jc in range(32):
                        gidx = ii * 32 + jc
                        s.wait_ge(s_S, gidx + 1)
                        if gidx >= 4:
                            s.wait_ge(s_oc, gidx - 3)
                        s.activation(out=slb[:, (jc % 4) * 512:(jc % 4 + 1) * 512],
                                     in_=pS[jc % 2], func=AF.Exp).then_inc(s_ex, 1)
                    s.wait_ge(s_oc, (ii + 1) * 32)
                    s.activation(out=otb[0:64, :], in_=pOT[0:64, :],
                                 func=AF.Copy).then_inc(s_ot, 1)

        @block.vector
        def _(v):
            gateB(v)
            for h in range(NHC):
                hb = (h % 2) * 64
                hc = h // 2
                for qc in range(4):
                    ii = h * 4 + qc
                    v.wait_ge(s_oc, (ii + 1) * 32)
                    rsync(v, v.tensor_copy(out=denr[0:1, :],
                                           in_=pOT[64:65, :]))
                    with nc.allow_low_precision(reason="bf16 softmax denom recip feeding bf16 multiply"):
                        rsync(v, v.reciprocal(out=denr[0:1, :], in_=denr[0:1, :]))
                    v.nop().then_inc(s_den, 1)
                    v.wait_ge(s_bca, ii + 1)
                    v.wait_ge(s_ot, ii + 1)
                    v.tensor_tensor(
                        out=aoT[hb:hb + 64, hc * TOWN + qc * 512: hc * TOWN + (qc + 1) * 512],
                        in0=otb[0:64, :], in1=pbcast[0:64, :],
                        op=ALU.mult).then_inc(s_ao, 1)

        esB.close()
        esK.close()
        esC = ExitStack()
        sbC = lambda nm, shp, dt=F32: esC.enter_context(nc.sbuf_tensor(nm, shp, dt))
        wob = sbC("wobC", [P, 8 * D], BF16)
        bbo = sbC("bboC", [1, D], BF16)
        x2r = sbC("x2rC", [P, 2 * D])
        outc = sbC("outcC", [P, 2 * D])
        s_wo = sem("sC_wo")
        s_wol = sem("sC_wol")
        s_bol = sem("sC_bol")
        s_ov = sem("sC_ov")
        s_oev = sem("sC_oev")
        s_x2l = [sem("sC_x2l0"), sem("sC_x2l1")]
        s_st = [sem("sC_st0"), sem("sC_st1")]
        pwoC = pbig[:, 0:1024]

        @block.gpsimd
        def _(g):
            g.wait_ge(s_ao, 64)
            g.dma_start(out=bbo[:], in_=bo_d).then_inc(s_bol, 16)
            g.dma_start(out=wob[:, :].rearrange("p (dc d) -> p dc d", dc=8),
                        in_=wo_d.rearrange("(dc p) d -> p dc d", p=P)
                        ).then_inc(s_wol, 16)

        @block.sync
        def _(sy):
            sy.wait_ge(s_ao, 64)
            for o in range(16):
                if o >= 2:
                    sy.wait_ge(s_oev, o - 1)
                sy.dma_start(out=x2r[:, (o % 2) * D:(o % 2 + 1) * D],
                             in_=x2scr[o * P:(o + 1) * P, :]).then_inc(s_x2l[o % 2], 16)
                sy.wait_ge(s_oev, o + 1)
                sy.dma_start(out=x3o[o * P:(o + 1) * P, :],
                             in_=outc[:, (o % 2) * D:(o % 2 + 1) * D]
                             ).then_inc(s_st[o % 2], 16)
            sy.wait_ge(s_st[0], 16 * 8)
            sy.wait_ge(s_st[1], 16 * 8)

        @block.tensor
        def _(t):
            t.wait_ge(s_wol, 16)
            t.wait_ge(s_bol, 16)
            for o in range(16):
                if o >= 1:
                    t.wait_ge(s_ov, o)
                for nh in range(2):
                    for cc in range(8):
                        t.matmul(
                            out=pwoC[:, nh * 512:(nh + 1) * 512],
                            lhsT=aoT[:, cc * TOWN + o * P: cc * TOWN + (o + 1) * P],
                            rhs=wob[:, cc * D + nh * 512: cc * D + (nh + 1) * 512],
                            start=(cc == 0), stop=False)
                    ins = t.matmul(
                        out=pwoC[:, nh * 512:(nh + 1) * 512],
                        lhsT=onesr[0:1, 0:P],
                        rhs=bbo[0:1, nh * 512:(nh + 1) * 512],
                        start=False, stop=True)
                ins.then_inc(s_wo, 1)

        @block.vector
        def _(v):
            for o in range(16):
                ob = outc[:, (o % 2) * D:(o % 2 + 1) * D]
                v.wait_ge(s_wo, o + 1)
                if o >= 2:
                    v.wait_ge(s_st[o % 2], 16 * (o // 2))
                v.tensor_copy(out=ob, in_=pwoC[:]).then_inc(s_ov, 1)
                v.wait_ge(s_x2l[o % 2], 16 * (o // 2 + 1))
                v.tensor_add(out=ob, in0=ob,
                             in1=x2r[:, (o % 2) * D:(o % 2 + 1) * D]
                             ).then_inc(s_oev, 1)

        esC.close()
    es.close()
    return nc



def prep_l1fa_maps(x3, inputs):
    com = {
        "wq": np.ascontiguousarray(inputs["L1_ffn_wq"]),
        "wk": np.ascontiguousarray(inputs["L1_ffn_wk"]),
        "wv": np.ascontiguousarray(inputs["L1_ffn_wv"]),
        "wo": np.ascontiguousarray(inputs["L1_ffn_wo"]),
        "g1c": inputs["L1_ffn_ln1_g"].reshape(8, P).T.copy(),
        "b1c": inputs["L1_ffn_ln1_b"].reshape(8, P).T.copy(),
        "bq_r": inputs["L1_ffn_bq"].reshape(1, D),
        "bk_r": inputs["L1_ffn_bk"].reshape(1, D),
        "bv_r": inputs["L1_ffn_bv"].reshape(1, D),
        "bo_r": inputs["L1_ffn_bo"].reshape(1, D),
        "idn": np.eye(P, dtype=BF16NP),
        "onesr": np.ones((1, 512), BF16NP),
    }
    maps = []
    for c in range(8):
        b, h = c // 2, c % 2
        mp = dict(com)
        mp["x1f"] = x3[b]
        mp["x1o_"] = np.ascontiguousarray(x3[b, h * TOWN:(h + 1) * TOWN])
        maps.append(mp)
    return maps


def run_l1ffn_attn(x3, inputs):
    nc = build_l1ffn_attn()
    maps = prep_l1fa_maps(x3, inputs)
    res = run_bass_kernel_spmd(nc, maps, core_ids=list(range(8)))
    out = np.empty((4, S, D), np.float32)
    for c in range(8):
        b, h = c // 2, c % 2
        out[b, h * TOWN:(h + 1) * TOWN] = res.results[c]["x3o"]
    return out


EF = 4096


def build_l1moe():
    nc = bass.Bass("TRN2", target_bir_lowering=False)
    xmid_d = nc.dram_tensor("xmid_", [TOWN, D], F32, kind="ExternalInput").ap()
    x3_d = nc.dram_tensor("x3o_", [TOWN, D], F32, kind="ExternalInput").ap()
    g2c_d = nc.dram_tensor("g2c", [P, 8], F32, kind="ExternalInput").ap()
    b2c_d = nc.dram_tensor("b2c", [P, 8], F32, kind="ExternalInput").ap()
    gw_d = nc.dram_tensor("gw_n", [D, 8], F32, kind="ExternalInput").ap()
    gb_d = nc.dram_tensor("gb_r", [1, 8], F32, kind="ExternalInput").ap()
    w1_d = nc.dram_tensor("w1f", [8 * D, EF], F32, kind="ExternalInput").ap()
    b1_d = nc.dram_tensor("b1cs", [8 * P, 32], F32, kind="ExternalInput").ap()
    w2_d = nc.dram_tensor("w2f", [8 * EF, D], F32, kind="ExternalInput").ap()
    b2_d = nc.dram_tensor("b2r", [8, D], F32, kind="ExternalInput").ap()
    idn_d = nc.dram_tensor("idn", [P, P], BF16, kind="ExternalInput").ap()
    onesr_d = nc.dram_tensor("onesr", [1, 512], BF16, kind="ExternalInput").ap()
    x4_d = nc.dram_tensor("x4o", [TOWN, D], F32, kind="ExternalOutput").ap()
    w1b = nc.dram_tensor("w1bscr", [8 * D, EF], BF16, kind="Internal").ap()

    es = ExitStack()
    sbt = lambda nm, shp, dt=F32: es.enter_context(nc.sbuf_tensor(nm, shp, dt))
    sem = lambda nm: es.enter_context(nc.semaphore(nm))

    idn = sbt("idnM", [P, P], BF16)
    onesr = sbt("onesrM", [1, 512], BF16)
    g2 = sbt("g2M", [P, 8])
    b2 = sbt("b2M", [P, 8])
    gwb = sbt("gwbM", [P, 64], BF16)
    gbb = sbt("gbbM", [1, 8], BF16)
    cols = sbt("colsM", [P, 16])
    co = lambda i: cols[:, i:i + 1]
    (K_S1, K_S2, K_SB, K_MEAN, K_VAR, K_STD, K_RSTD, K_MX) = range(8)
    lg = sbt("lgM", [P, 8])
    maskA = sbt("maskAM", [P, 8 * 16])
    hn2T = sbt("hn2TM", [P, 8 * TOWN], BF16)
    accM = sbt("accM", [P, 16 * D], BF16)
    xb = sbt("xbM", [P, D])
    x3b = sbt("x3bM", [P, D])
    outb = sbt("outbM", [P, 2 * D])
    fot = sbt("fotM", [P, D])
    sqm = sbt("sqmM", [P, 512])
    nbf = sbt("nbfM", [P, D], BF16)
    ybf = sbt("ybfM", [P, 2 * 512], BF16)

    ptp = es.enter_context(nc.psum_tensor("ptpM", [P, D], BF16))
    pg = es.enter_context(nc.psum_tensor("pgM", [P, 8], F32))
    pM = es.enter_context(nc.psum_tensor("pMM", [P, 1024], F32))
    pY = es.enter_context(nc.psum_tensor("pYM", [P, 1024], F32))

    s_wg = sem("sM_wg")
    s_c0 = [sem("sM_c00"), sem("sM_c01")]
    s_c1 = [sem("sM_c10"), sem("sM_c11")]
    s_xl = sem("sM_xl")
    s_var2 = sem("sM_var")
    s_std2 = sem("sM_std")
    s_rt = sem("sM_rt")
    s_nb = sem("sM_nb")
    s_tp0 = sem("sM_tp0")
    s_tpc0 = sem("sM_tpc0")
    s_g = sem("sM_g")
    s_gc = sem("sM_gc")
    s_b1 = sem("sM_b1")
    s_w1 = [sem("sM_w10"), sem("sM_w11"), sem("sM_w12"), sem("sM_w13")]
    s_mu = sem("sM_mu")
    s_mev = sem("sM_mev")
    s_w2 = sem("sM_w2")
    s_b2 = sem("sM_b2")
    s_y = sem("sM_y")
    s_ye = sem("sM_ye")
    s_ytp = sem("sM_ytp")
    s_ymc = sem("sM_ymc")
    s_xl2 = sem("sM_xl2")
    s_x3l = sem("sM_x3l")
    s_fst = sem("sM_fst")
    s_fo = sem("sM_fo")
    s_st = [sem("sM_st0"), sem("sM_st1")]
    rt_cnt = [0]
    rsync = _mk_rsync(s_rt, rt_cnt)

    esP0 = ExitStack()
    cb = esP0.enter_context(nc.sbuf_tensor("cbM", [P, 2 * 2048], BF16))

    esE = ExitStack()
    w2e = esE.enter_context(nc.sbuf_tensor("w2eM", [P, 32 * D], BF16))
    w1c = esE.enter_context(nc.sbuf_tensor("w1cM", [P, 4 * 8 * P], BF16))
    midT = esE.enter_context(nc.sbuf_tensor("midTM", [P, 32 * 512], BF16))
    b1ce = esE.enter_context(nc.sbuf_tensor("b1ceM", [P, 32], F32))
    b2re = esE.enter_context(nc.sbuf_tensor("b2reM", [1, D], BF16))

    w1cv = lambda sl: w1c[:, :].rearrange("p (s dc m) -> p s dc m", s=4, dc=8)[:, sl]
    w2ev = w2e[:, :].rearrange("p (fc d) -> p fc d", fc=32)

    def _ln2(v, xin, base):
        v.tensor_mul(out=sqm[:], in0=xin[:, 0:512], in1=xin[:, 0:512])
        v.reduce_sum(out=co(K_S2), in_=sqm[:], axis=AX)
        v.tensor_mul(out=sqm[:], in0=xin[:, 512:1024], in1=xin[:, 512:1024])
        v.reduce_sum(out=co(K_SB), in_=sqm[:], axis=AX)
        v.reduce_sum(out=co(K_S1), in_=xin, axis=AX)
        rsync(v, v.tensor_add(out=co(K_S2), in0=co(K_S2), in1=co(K_SB)))
        rsync(v, v.tensor_scalar(out=co(K_MEAN), in0=co(K_S1),
                                 scalar1=1.0 / D, scalar2=None, op0=ALU.mult))
        rsync(v, v.tensor_mul(out=co(K_VAR), in0=co(K_MEAN), in1=co(K_MEAN)))
        rsync(v, v.scalar_tensor_tensor(out=co(K_VAR), in0=co(K_S2),
                                        scalar=1.0 / D, in1=co(K_VAR),
                                        op0=ALU.mult, op1=ALU.subtract))
        v.tensor_scalar(out=co(K_VAR), in0=co(K_VAR), scalar1=1e-5,
                        scalar2=None, op0=ALU.add).then_inc(s_var2, 1)
        v.wait_ge(s_std2, base + 1)
        rsync(v, v.reciprocal(out=co(K_RSTD), in_=co(K_STD)))

    with nc.Block() as block:
        @block.gpsimd
        def _(g):
            g.dma_start(out=idn[:], in_=idn_d).then_inc(s_wg, 16)
            g.dma_start(out=onesr[:], in_=onesr_d).then_inc(s_wg, 16)
            g.dma_start(out=g2[:], in_=g2c_d).then_inc(s_wg, 16)
            g.dma_start(out=b2[:], in_=b2c_d).then_inc(s_wg, 16)
            g.dma_start(out=gwb[:, :].rearrange("p (dc e) -> p dc e", e=8),
                        in_=gw_d.rearrange("(dc p) e -> p dc e", p=P)
                        ).then_inc(s_wg, 16)
            g.dma_start(out=gbb[:], in_=gb_d).then_inc(s_wg, 16)
            for i in range(128):
                sl = i % 2
                r0 = (i // 2) * P
                c0 = (i % 2) * 2048
                if i >= 2:
                    g.wait_ge(s_c1[sl], 16 * ((i - 2) // 2 + 1))
                g.dma_start(out=cb[:, sl * 2048:(sl + 1) * 2048],
                            in_=w1_d[r0:r0 + P, c0:c0 + 2048]
                            ).then_inc(s_c0[sl], 16)
                g.wait_ge(s_c0[sl], 16 * (i // 2 + 1))
                g.dma_start(out=w1b[r0:r0 + P, c0:c0 + 2048],
                            in_=cb[:, sl * 2048:(sl + 1) * 2048]
                            ).then_inc(s_c1[sl], 16)
            for e in range(8):
                if e >= 1:
                    g.wait_ge(s_ymc, 32 * e)
                g.dma_start(out=b2re[:], in_=b2_d[e:e + 1, :]).then_inc(s_b2, 16)
                for dh in range(2):
                    g.dma_start(out=w2ev[:, :, dh * 512:(dh + 1) * 512],
                                in_=w2_d[e * EF:(e + 1) * EF,
                                         dh * 512:(dh + 1) * 512]
                                .rearrange("(fc p) d -> p fc d", p=P)
                                ).then_inc(s_w2, 16)

        @block.sync
        def _(sy):
            for tt in range(16):
                if tt >= 1:
                    sy.wait_ge(s_nb, tt)
                sy.dma_start(out=xb[:], in_=xmid_d[tt * P:(tt + 1) * P, :]
                             ).then_inc(s_xl, 16)
            for e in range(8):
                sy.wait_ge(s_mev, 128 * e)
                sy.dma_start(out=b1ce[:], in_=b1_d[e * P:(e + 1) * P, :]
                             ).then_inc(s_b1, 16)
                sy.wait_ge(s_c1[0], 16 * (8 * e + 8))
                sy.wait_ge(s_c1[1], 16 * (8 * e + 8))
                for gq in range(4):
                    for fb in range(32):
                        idx = (e * 4 + gq) * 32 + fb
                        sl = fb % 4
                        if idx >= 4:
                            sy.wait_ge(s_mu, idx - 3)
                        sy.dma_start(
                            out=w1cv(sl),
                            in_=w1b[e * D:(e + 1) * D, fb * P:(fb + 1) * P]
                            .rearrange("(dc p) m -> p dc m", p=P)
                            ).then_inc(s_w1[sl], 16)
            for tt in range(16):
                sy.wait_ge(s_fst, tt)
                sy.dma_start(out=xb[:], in_=xmid_d[tt * P:(tt + 1) * P, :]
                             ).then_inc(s_xl2, 16)
                sy.dma_start(out=x3b[:], in_=x3_d[tt * P:(tt + 1) * P, :]
                             ).then_inc(s_x3l, 16)
                sy.wait_ge(s_fo, tt + 1)
                sy.dma_start(out=x4_d[tt * P:(tt + 1) * P, :],
                             in_=outb[:, (tt % 2) * D:(tt % 2 + 1) * D]
                             ).then_inc(s_st[tt % 2], 16)
            sy.wait_ge(s_st[0], 16 * 8)
            sy.wait_ge(s_st[1], 16 * 8)

        @block.scalar
        def _(s):
            for tt in range(16):
                s.wait_ge(s_var2, tt + 1)
                s.activation(out=co(K_STD), in_=co(K_VAR),
                             func=AF.Sqrt).then_inc(s_std2, 1)
            for e in range(8):
                s.wait_ge(s_b1, 16 * (e + 1))
                for gq in range(4):
                    for fb in range(32):
                        idx = (e * 4 + gq) * 32 + fb
                        s.wait_ge(s_mu, idx + 1)
                        if gq > 0 or e > 0:
                            s.wait_ge(s_y, 8 * (e * 4 + gq))
                        s.activation(out=midT[:, fb * 512:(fb + 1) * 512],
                                     in_=pM[:, (fb % 2) * 512:(fb % 2 + 1) * 512],
                                     func=AF.Relu,
                                     bias=b1ce[:, fb:fb + 1]).then_inc(s_mev, 1)
                    for hb in range(8):
                        yidx = (e * 4 + gq) * 8 + hb
                        s.wait_ge(s_y, yidx + 1)
                        if yidx >= 2:
                            s.wait_ge(s_ytp, yidx - 1)
                        s.activation(out=ybf[:, (hb % 2) * 512:(hb % 2 + 1) * 512],
                                     in_=pY[:, (hb % 2) * 512:(hb % 2 + 1) * 512],
                                     func=AF.Copy).then_inc(s_ye, 1)
            for tt in range(16):
                s.wait_ge(s_var2, 16 + tt + 1)
                s.activation(out=co(K_STD), in_=co(K_VAR),
                             func=AF.Sqrt).then_inc(s_std2, 1)

        @block.tensor
        def _(t):
            t.wait_ge(s_wg, 96)
            for tt in range(16):
                t.wait_ge(s_nb, tt + 1)
                if tt >= 1:
                    t.wait_ge(s_tpc0, tt)
                for dc in range(8):
                    ins = t.transpose(out=ptp[:, dc * P:(dc + 1) * P],
                                      in_=nbf[:, dc * P:(dc + 1) * P],
                                      identity=idn[:])
                ins.then_inc(s_tp0, 1)
                t.wait_ge(s_tpc0, tt + 1)
                if tt >= 1:
                    t.wait_ge(s_gc, tt)
                for dc in range(8):
                    t.matmul(out=pg[:, 0:8],
                             lhsT=hn2T[:, dc * TOWN + tt * P: dc * TOWN + (tt + 1) * P],
                             rhs=gwb[:, dc * 8:(dc + 1) * 8],
                             start=(dc == 0), stop=False)
                t.matmul(out=pg[:, 0:8], lhsT=onesr[0:1, 0:P],
                         rhs=gbb[0:1, 0:8], start=False,
                         stop=True).then_inc(s_g, 1)
            for e in range(8):
                t.wait_ge(s_b2, 16 * (e + 1))
                for gq in range(4):
                    for fb in range(32):
                        idx = (e * 4 + gq) * 32 + fb
                        sl = fb % 4
                        psl = fb % 2
                        t.wait_ge(s_w1[sl], 16 * (idx // 4 + 1))
                        if idx >= 2:
                            t.wait_ge(s_mev, idx - 1)
                        for dc in range(8):
                            ins = t.matmul(
                                out=pM[:, psl * 512:(psl + 1) * 512],
                                lhsT=w1cv(sl)[:, dc, :],
                                rhs=hn2T[:, dc * TOWN + gq * 512:
                                         dc * TOWN + (gq + 1) * 512],
                                start=(dc == 0), stop=(dc == 7))
                        ins.then_inc(s_mu, 1)
                    t.wait_ge(s_mev, 128 * e + 32 * (gq + 1))
                    for hb in range(10):
                        if hb < 8:
                            yidx = (e * 4 + gq) * 8 + hb
                            t.wait_ge(s_w2, 16 * (2 * e + (1 if hb < 4 else 2)))
                            if yidx >= 2:
                                t.wait_ge(s_ye, yidx - 1)
                            for fc in range(32):
                                t.matmul(
                                    out=pY[:, (hb % 2) * 512:(hb % 2 + 1) * 512],
                                    lhsT=w2ev[:, fc, hb * P:(hb + 1) * P],
                                    rhs=midT[:, fc * 512:(fc + 1) * 512],
                                    start=(fc == 0), stop=False)
                            t.matmul(
                                out=pY[:, (hb % 2) * 512:(hb % 2 + 1) * 512],
                                lhsT=b2re[0:1, hb * P:(hb + 1) * P],
                                rhs=onesr[0:1, 0:512],
                                start=False, stop=True).then_inc(s_y, 1)
                        if hb >= 2:
                            hbp = hb - 2
                            yidxp = (e * 4 + gq) * 8 + hbp
                            t.wait_ge(s_ye, yidxp + 1)
                            if yidxp >= 1:
                                t.wait_ge(s_ymc, yidxp)
                            for q in range(4):
                                ins = t.transpose(
                                    out=ptp[:, q * P:(q + 1) * P],
                                    in_=ybf[:, (hbp % 2) * 512 + q * P:
                                            (hbp % 2) * 512 + (q + 1) * P],
                                    identity=idn[:])
                            ins.then_inc(s_ytp, 1)

        @block.vector
        def _(v):
            for tt in range(16):
                v.wait_ge(s_xl, 16 * (tt + 1))
                _ln2(v, xb[:], tt)
                v.tensor_scalar(out=nbf[:], in0=xb[:], scalar1=co(K_MEAN),
                                scalar2=co(K_RSTD), op0=ALU.subtract,
                                op1=ALU.mult).then_inc(s_nb, 1)
                v.wait_ge(s_tp0, tt + 1)
                for dc in range(8):
                    ins = v.tensor_scalar(
                        out=hn2T[:, dc * TOWN + tt * P: dc * TOWN + (tt + 1) * P],
                        in0=ptp[:, dc * P:(dc + 1) * P],
                        scalar1=g2[:, dc:dc + 1], scalar2=b2[:, dc:dc + 1],
                        op0=ALU.mult, op1=ALU.add)
                ins.then_inc(s_tpc0, 1)
                v.wait_ge(s_g, tt + 1)
                rsync(v, v.tensor_copy(out=lg[:], in_=pg[:, 0:8]))
                v.nop().then_inc(s_gc, 1)
                rsync(v, v.reduce_max(out=co(K_MX), in_=lg[:], axis=AX))
                v.tensor_scalar(out=maskA[:, tt * 8:(tt + 1) * 8], in0=lg[:],
                                scalar1=co(K_MX), scalar2=None,
                                op0=ALU.is_equal)
            for e in range(8):
                for gq in range(4):
                    for hb in range(8):
                        yidx = (e * 4 + gq) * 8 + hb
                        v.wait_ge(s_ytp, yidx + 1)
                        for q in range(4):
                            tt = gq * 4 + q
                            mcol = maskA[:, tt * 8 + e: tt * 8 + e + 1]
                            dst = accM[:, tt * D + hb * P: tt * D + (hb + 1) * P]
                            if e == 0:
                                ins = v.tensor_scalar(out=dst,
                                                      in0=ptp[:, q * P:(q + 1) * P],
                                                      scalar1=mcol, scalar2=None,
                                                      op0=ALU.mult)
                            else:
                                v.tensor_scalar(out=sqm[:, 0:P],
                                                in0=ptp[:, q * P:(q + 1) * P],
                                                scalar1=mcol, scalar2=None,
                                                op0=ALU.mult)
                                ins = v.tensor_add(out=dst, in0=dst,
                                                   in1=sqm[:, 0:P])
                        ins.then_inc(s_ymc, 1)
            for tt in range(16):
                v.wait_ge(s_xl2, 16 * (tt + 1))
                v.tensor_add(out=fot[:], in0=xb[:],
                             in1=accM[:, tt * D:(tt + 1) * D]).then_inc(s_fst, 1)
                _ln2(v, fot[:], 16 + tt)
                v.wait_ge(s_x3l, 16 * (tt + 1))
                if tt >= 2:
                    v.wait_ge(s_st[tt % 2], 16 * (tt // 2))
                ob = outb[:, (tt % 2) * D:(tt % 2 + 1) * D]
                v.tensor_scalar(out=ob, in0=fot[:], scalar1=co(K_MEAN),
                                scalar2=co(K_RSTD), op0=ALU.subtract,
                                op1=ALU.mult)
                v.tensor_add(out=ob, in0=ob, in1=x3b[:]).then_inc(s_fo, 1)

    esE.close()
    esP0.close()
    es.close()
    return nc


def prep_l1moe_maps(xmid, x3, inputs):
    com = {
        "g2c": inputs["L1_ffn_ln2_g"].reshape(8, P).T.copy(),
        "b2c": inputs["L1_ffn_ln2_b"].reshape(8, P).T.copy(),
        "gw_n": np.ascontiguousarray(inputs["L1_ffn_gate_w"]),
        "gb_r": inputs["L1_ffn_gate_b"].reshape(1, 8),
        "w1f": inputs["L1_ffn_w1"].reshape(8 * D, EF),
        "b1cs": np.ascontiguousarray(
            inputs["L1_ffn_b1"].reshape(8, 32, P).transpose(0, 2, 1)
        ).reshape(8 * P, 32),
        "w2f": inputs["L1_ffn_w2"].reshape(8 * EF, D),
        "b2r": np.ascontiguousarray(inputs["L1_ffn_b2"]),
        "idn": np.eye(P, dtype=BF16NP),
        "onesr": np.ones((1, 512), BF16NP),
    }
    maps = []
    for c in range(8):
        b, h = c // 2, c % 2
        mp = dict(com)
        mp["xmid_"] = np.ascontiguousarray(xmid[b, h * TOWN:(h + 1) * TOWN])
        mp["x3o_"] = np.ascontiguousarray(x3[b, h * TOWN:(h + 1) * TOWN])
        maps.append(mp)
    return maps


def run_l1moe(xmid, x3, inputs):
    nc = build_l1moe()
    maps = prep_l1moe_maps(xmid, x3, inputs)
    res = run_bass_kernel_spmd(nc, maps, core_ids=list(range(8)))
    out = np.empty((4, S, D), np.float32)
    for c in range(8):
        b, h = c // 2, c % 2
        out[b, h * TOWN:(h + 1) * TOWN] = res.results[c]["x4o"]
    return out
